# version 15
# baseline (speedup 1.0000x reference)
import numpy as np
import ml_dtypes
import concourse.bass as bass
import concourse.mybir as mybir
from concourse.bass_utils import run_bass_kernel_spmd

F32 = mybir.dt.float32
BF16 = mybir.dt.bfloat16
I32 = mybir.dt.int32
AF = mybir.ActivationFunctionType
ALU = mybir.AluOpType
AX = mybir.AxisListType

STAGE = 3
ATTN_PH = 99
DEBUG = False

D, T, NCH = 2048, 1024, 16
G, IC = 8, 11
NH, DK, DV, CH, NCK = 4, 512, 256, 64, 16
EPS = 1e-5
QSCALE = float(DK) ** -0.5
R16 = 1.0 / 16.0
RING = 3

BF = ml_dtypes.bfloat16



def _pack_ws(w, kc, occ):
    return np.ascontiguousarray(
        w.reshape(kc, 128, occ, 128).transpose(2, 1, 0, 3).reshape(occ, 128, kc * 128))


def _prep_host(inputs):
    f = np.float32
    d = {}
    gp = np.asarray(inputs["L0_ffn_gate_proj_w"], f)
    up = np.asarray(inputs["L0_ffn_up_proj_w"], f)
    dn = np.asarray(inputs["L0_ffn_down_proj_w"], f)
    d["gw_l"] = np.ascontiguousarray(
        gp.reshape(8, IC, 128, 16, 128).transpose(0, 1, 4, 3, 2).reshape(8, IC, 128, 2048))
    d["uw_l"] = np.ascontiguousarray(
        up.reshape(8, IC, 128, 16, 128).transpose(0, 1, 4, 3, 2).reshape(8, IC, 128, 2048))
    d["dw_l"] = np.ascontiguousarray(
        dn.reshape(8, 16, 128, IC, 128).transpose(0, 1, 4, 3, 2).reshape(8, 16, 128, 1408))
    mu = np.asarray(inputs["L0_ffn_mask_up_w"], f)
    mup = np.zeros((8, 64, 2048), f)
    for g in range(8):
        mup[g, 8 * g:8 * g + 8, :] = mu[g].T
    d["mu_pair"] = mup
    d["gate_w"] = np.ascontiguousarray(np.asarray(inputs["L0_ffn_gate_w"], f))
    d["inv_i"] = np.asarray(inputs["L0_ffn_inv_map"]).astype(np.int32).reshape(1, 64)
    d["wq_l"] = _pack_ws(np.asarray(inputs["L0_attn_Wq"], f), 16, 16)
    d["wk_l"] = _pack_ws(np.asarray(inputs["L0_attn_Wk"], f), 16, 16)
    d["wv_l"] = _pack_ws(np.asarray(inputs["L0_attn_Wv"], f), 16, 8)
    d["wg_l"] = _pack_ws(np.asarray(inputs["L0_attn_Wg"], f), 16, 8)
    d["wo_l"] = _pack_ws(np.asarray(inputs["L0_attn_Wo"], f), 8, 16)
    d["wgk1"] = np.ascontiguousarray(np.asarray(inputs["L0_attn_Wgk1"], f))
    d["wgk2"] = np.ascontiguousarray(np.asarray(inputs["L0_attn_Wgk2"], f))
    d["bgk2_t"] = np.ascontiguousarray(np.asarray(inputs["L0_attn_bgk2"], f).reshape(16, 128).T)
    d["wnorm_t"] = np.ascontiguousarray(
        np.tile(np.asarray(inputs["L0_attn_Wnorm"], f)[None, :], (128, 1)))
    d["ident"] = np.eye(128, dtype=BF)
    d["ones128"] = np.ones((128, 128), dtype=BF)
    d["maskT"] = np.tile(np.triu(np.ones((64, 64), np.float32)), (2, 1)).astype(BF)
    d["iota64"] = np.arange(64, dtype=f).reshape(64, 1)
    oh = np.zeros((8, 8, 128), f)
    for g in range(8):
        oh[g, g, :] = 0.5
    d["onehot_half"] = oh.astype(BF)
    return d



class Arena:
    def __init__(self, slab_ap, nbytes):
        self.slab = slab_ap
        self.nbytes = nbytes

    def at(self, off, dtype, shape):
        esz = 2 if dtype == BF16 else 4
        nfree = int(np.prod(shape[1:]))
        assert off % 4 == 0 and off + nfree * esz <= self.nbytes, (off, nfree * esz)
        e0 = off // 4
        v = self.slab[:, e0:e0 + (nfree * esz + 3) // 4]
        if dtype == BF16:
            v = v.bitcast(BF16)
        v = v[:, :nfree]
        if shape[0] != 128:
            v = v[0:shape[0], :]
        if len(shape) > 2:
            dims = shape[1:]
            names = " ".join(f"a{i}" for i in range(len(dims)))
            v = v.rearrange(f"p ({names}) -> p {names}",
                            **{f"a{i}": dims[i] for i in range(len(dims))})
        return v



def build():
    nc = bass.Bass("TRN2", target_bir_lowering=False, num_devices=8)

    dr = {}
    def di(name, shape, dt=F32):
        dr[name] = nc.dram_tensor(name, list(shape), dt, kind="ExternalInput").ap()
    di("xT", (2048, 1024))
    di("gw_l", (8, IC, 128, 2048)); di("uw_l", (8, IC, 128, 2048)); di("dw_l", (8, 16, 128, 1408))
    di("mu_pair", (8, 64, 2048))
    di("gate_w", (64, 2048)); di("inv_i", (1, 64), I32)
    di("wq_l", (16, 128, 2048)); di("wk_l", (16, 128, 2048))
    di("wv_l", (8, 128, 2048)); di("wg_l", (8, 128, 2048)); di("wo_l", (16, 128, 1024))
    di("wgk1", (2048, 16)); di("wgk2", (16, 2048))
    di("bgk2_t", (128, 16)); di("wnorm_t", (128, 256))
    di("ident", (128, 128), BF16); di("ones128", (128, 128), BF16)
    di("maskT", (128, 64), BF16); di("iota64", (64, 1))
    di("onehot_half", (8, 8, 128), BF16)
    di("mmask", (128, 8)); di("ummask", (128, 8)); di("fconst", (128, 4))
    out_d = nc.dram_tensor("out", [2048, 1024], F32, kind="ExternalOutput").ap()
    if DEBUG:
        dbg_h = nc.dram_tensor("dbg_h", [128, NCH * 1024], F32, kind="ExternalOutput").ap()
        dbg_soft = nc.dram_tensor("dbg_soft", [64, 1024], F32, kind="ExternalOutput").ap()
        dbg_scal = nc.dram_tensor("dbg_scal", [8, 1024], F32, kind="ExternalOutput").ap()
        dbg_scb = nc.dram_tensor("dbg_scb", [128, 1024], F32, kind="ExternalOutput").ap()
        dbg_a = nc.dram_tensor("dbg_a", [128, IC * 1024], F32, kind="ExternalOutput").ap()
        dbg_xg = nc.dram_tensor("dbg_xg", [128, NCH * 1024], F32, kind="ExternalOutput").ap()
    xspill_d = nc.dram_tensor("xspill", [128, 16384], F32).ap()
    comm_in = nc.dram_tensor("comm_in", [2048, 257], BF16).ap()
    comm_out = nc.dram_tensor("comm_out", [8 * 2048, 257], BF16, addr_space="Shared").ap()

    h_sb = nc.alloc_sbuf_tensor("h_sb", [128, NCH * 1024], BF16).ap()
    h3 = h_sb.rearrange("p (c t) -> p c t", c=NCH)
    bcA = nc.alloc_sbuf_tensor("bcA", [128, 1024], F32).ap()
    bcB = nc.alloc_sbuf_tensor("bcB", [128, 1024], F32).ap()
    ring = nc.alloc_sbuf_tensor("ring", [128, RING * 2048], BF16).ap()
    ident_sb = nc.alloc_sbuf_tensor("ident_sb", [128, 128], BF16).ap()
    ones_sb = nc.alloc_sbuf_tensor("ones_sb", [128, 128], BF16).ap()
    maskT_sb = nc.alloc_sbuf_tensor("maskT_sb", [128, 64], BF16).ap()
    iota_sb = nc.alloc_sbuf_tensor("iota_sb", [64, 1], F32).ap()
    oh_sb = nc.alloc_sbuf_tensor("oh_sb", [8, 8 * 128], BF16).ap()
    gwperm_sb = nc.alloc_sbuf_tensor("gwperm_sb", [128, 1024], BF16).ap()
    inv_sb = nc.alloc_sbuf_tensor("inv_sb", [1, 64], I32).ap()
    invf_sb = nc.alloc_sbuf_tensor("invf_sb", [1, 64], F32).ap()
    invb_sb = nc.alloc_sbuf_tensor("invb_sb", [1, 64], BF16).ap()
    P_sb = nc.alloc_sbuf_tensor("P_sb", [64, 64], BF16).ap()
    wgk1_sb = nc.alloc_sbuf_tensor("wgk1_sb", [128, 256], BF16).ap()
    bgk2_sb = nc.alloc_sbuf_tensor("bgk2_sb", [128, 16], F32).ap()
    wnorm_sb = nc.alloc_sbuf_tensor("wnorm_sb", [128, 256], F32).ap()
    mmask_sb = nc.alloc_sbuf_tensor("mmask_sb", [128, 8], F32).ap()
    ummask_sb = nc.alloc_sbuf_tensor("ummask_sb", [128, 8], F32).ap()
    zz64 = nc.alloc_sbuf_tensor("zz64", [128, 64], F32).ap()
    softT_sb = nc.alloc_sbuf_tensor("softT_sb", [64, 1024], BF16).ap()
    scalT_sb = nc.alloc_sbuf_tensor("scalT_sb", [8, 1024], BF16).ap()
    Bsum_sb = nc.alloc_sbuf_tensor("Bsum_sb", [128, 16], F32).ap()
    Ball_sb = nc.alloc_sbuf_tensor("Ball_sb", [128, 256], F32).ap()
    ebl_sb = nc.alloc_sbuf_tensor("ebl_sb", [128, 16], F32).ap()
    dall_sb = nc.alloc_sbuf_tensor("dall_sb", [128, 16], F32).ap()
    run_sb = nc.alloc_sbuf_tensor("run_sb", [128, 1], F32).ap()
    esc_sb = nc.alloc_sbuf_tensor("esc_sb", [128, 1], F32).ap()
    runA_sb = nc.alloc_sbuf_tensor("runA_sb", [128, 8], F32).ap()
    escA_sb = nc.alloc_sbuf_tensor("escA_sb", [128, 8], F32).ap()
    gnm = nc.alloc_sbuf_tensor("gnm", [128, 4], F32).ap()
    gt8 = nc.alloc_sbuf_tensor("gt8", [128, 8], F32).ap()
    gel = nc.alloc_sbuf_tensor("gel", [128, 64], F32).ap()
    gpr = nc.alloc_sbuf_tensor("gpr", [128, 64], F32).ap()
    gfl = nc.alloc_sbuf_tensor("gfl", [128, 64], F32).ap()
    gm0 = nc.alloc_sbuf_tensor("gm0", [128, 64], F32).ap()
    gden = nc.alloc_sbuf_tensor("gden", [128, 8], F32).ap()
    grd = nc.alloc_sbuf_tensor("grd", [128, 8], F32).ap()
    gs16 = nc.alloc_sbuf_tensor("gs16", [128, 64], BF16).ap()
    gsc = nc.alloc_sbuf_tensor("gsc", [128, 8], F32).ap()
    gsc16 = nc.alloc_sbuf_tensor("gsc16", [128, 8], BF16).ap()
    mu_sb = nc.alloc_sbuf_tensor("mu_sb", [64, 2048], BF16).ap()
    fc_sb = nc.alloc_sbuf_tensor("fc_sb", [128, 4], F32).ap()
    nc.const_aps.register(F32, EPS, fc_sb[:, 0:1])

    SLABW = 34816
    slab_t = nc.alloc_sbuf_tensor("slab", [128, SLABW], F32)
    A = Arena(slab_t.ap(), SLABW * 4)

    x_v = A.at(0, F32, [128, NCH, 1024])
    qe_v = A.at(0, BF16, [128, 16, 1024])
    S32_v = A.at(32768, F32, [128, 16, 256])
    S16_v = A.at(49152, BF16, [128, 16, 256])
    bb_v = A.at(57344, F32, [128, 1024])
    ebm_v = A.at(61440, F32, [128, 1024])
    rb_v = A.at(32768, F32, [128, 8, 257])
    xg_v = A.at(65536, BF16, [128, NCH, 1024])
    a_v = A.at(98304, BF16, [128, IC, 1024])
    tsil_v = A.at(120832, F32, [128, 1024])
    scb_v = A.at(124928, F32, [128, 1024])
    lnx_v = [A.at(129024 + i * 2048, BF16, [128, 1024]) for i in range(2)]
    lnq_v = [A.at(133120 + i * 2048, BF16, [128, 1024]) for i in range(2)]
    k_v = A.at(65536, BF16, [128, 16, 512])
    gk_v = A.at(81920, BF16, [128, 16, 512])
    v_v = A.at(98304, BF16, [128, 4, 1024])
    oraw_v = A.at(106496, BF16, [128, 8, 1024])
    kl_tm_v = A.at(122880, BF16, [128, 16, 128])
    kd_v = A.at(126976, BF16, [128, 1024])
    at16_v = A.at(129024, BF16, [128, 64])
    qs16_v = A.at(129280, BF16, [128, 256])
    gkt_v = A.at(129792, BF16, [16, 512])
    sgt_v = A.at(130816, F32, [128, 512])
    sg2_v = A.at(132864, F32, [128, 512])
    gpp_v = A.at(134912, BF16, [128, 1024])
    osq_v = A.at(136960, F32, [128, 256])
    orst_v = A.at(138240, F32, [128, 2])
    stg_v = A.at(65536, BF16, [128, 8, 257])
    S0_v = A.at(81984, BF16, [128, 16, 256])
    acc_v = A.at(90176, F32, [128, 256])
    g_tm_v = A.at(65536, BF16, [128, 8, 1024])
    og_v = A.at(81984, BF16, [128, 8, 1024])

    pA = nc.alloc_psum_tensor("pA", [128, 1024], F32).ap()
    pB = nc.alloc_psum_tensor("pB", [128, 1024], F32).ap()
    pC = nc.alloc_psum_tensor("pC", [128, 1024], F32).ap()
    pD = nc.alloc_psum_tensor("pD", [128, 512], F32).ap()
    pT = nc.alloc_psum_tensor("pT", [128, 1024], BF16).ap()

    SEM, CNT = {}, {}
    for s in ["t", "v", "s", "p", "wq", "sq", "cc", "r0", "r1",
              "w0", "w1", "w2", "mu"]:
        SEM[s] = nc.alloc_semaphore("sem_" + s)
        CNT[s] = 0

    def inc(ins, s, n=1):
        CNT[s] += n
        ins.then_inc(SEM[s], n)
        return CNT[s]

    wctr = [0]
    ring_free = [0] * RING

    with nc.Block() as block:
        def emit(eng):
            def deco(f):
                getattr(block, eng)(f)
            return deco

        def wait_marks(e, marks):
            for s, m in marks.items():
                if m > 0:
                    e.wait_ge(SEM[s], m)

        @emit('sync')
        def _(sy):
            xr = dr["xT"].rearrange("(c p) t -> p c t", p=128)
            inc(sy.dma_start(x_v[:, 0:8, :], xr[:, 0:8, :]), 'r0', 16)
            inc(sy.dma_start(x_v[:, 8:16, :], xr[:, 8:16, :]), 'r1', 16)
            srcs = [(ident_sb, dr["ident"]), (ones_sb, dr["ones128"]),
                    (maskT_sb, dr["maskT"]), (iota_sb, dr["iota64"]),
                    (oh_sb.rearrange("s (g i) -> s g i", g=8),
                     dr["onehot_half"].rearrange("g s i -> s g i")),
                    (inv_sb, dr["inv_i"]), (bgk2_sb, dr["bgk2_t"]),
                    (wnorm_sb, dr["wnorm_t"]), (mmask_sb, dr["mmask"]),
                    (ummask_sb, dr["ummask"]), (fc_sb, dr["fconst"])]
            for dst, src in srcs:
                inc(sy.dma_start(dst, src), 'sq', 16)
        m_setup_sq = CNT['sq']

        @emit('gpsimd')
        def _(g):
            inc(g.dma_start(ring[0:64, 0:2048], dr["gate_w"]), 'wq', 16)
            i = g.dma_start(wgk1_sb.rearrange("p (k c) -> p k c", k=16),
                            dr["wgk1"].rearrange("(k p) c -> p k c", p=128))
            inc(i, 'wq', 16)
            g.memset(zz64, 0.0)
            i = g.memset(run_sb, 1.0)
            inc(i, 'p', 1)
        m_setup_wq = CNT['wq']

        @emit('vector')
        def _(v):
            v.wait_ge(SEM['sq'], m_setup_sq)
            v.tensor_copy(invf_sb, inv_sb)
            i = v.tensor_copy(invb_sb, invf_sb)
            inc(i, 'v', 1)
        m1 = CNT['v']

        @emit('tensor')
        def _(t, m1=m1):
            t.wait_ge(SEM['v'], m1)
            t.wait_ge(SEM['wq'], m_setup_wq)
            i = t.matmul(pD[0:64, 0:64], ones_sb[0:1, 0:64], invb_sb, start=True, stop=True)
            inc(i, 't', 1)
        m2 = CNT['t']

        @emit('vector')
        def _(v, m2=m2):
            v.wait_ge(SEM['t'], m2)
            i = v.tensor_scalar(P_sb, pD[0:64, 0:64], iota_sb[:, 0:1], None, ALU.is_equal)
            inc(i, 'v', 1)

        for k in range(16):
            mv = CNT['v']

            @emit('tensor')
            def _(t, k=k, mv=mv):
                t.wait_ge(SEM['v'], mv)
                i = t.matmul(pD[:, 0:64], ring[0:64, k * 128:(k + 1) * 128], P_sb,
                             start=True, stop=True)
                inc(i, 't', 1)
            mt = CNT['t']

            @emit('vector')
            def _(v, k=k, mt=mt):
                v.wait_ge(SEM['t'], mt)
                i = v.tensor_copy(gwperm_sb[:, k * 64:(k + 1) * 64], pD[:, 0:64])
                inc(i, 'v', 1)
        ring_free[0] = CNT['t']

        RSEMS = ["w0", "w1", "w2"]

        def ring_tile(src_ap, width=2048, rows=128):
            idx = wctr[0]
            wctr[0] += 1
            slot = idx % RING
            fm = ring_free[slot]
            sname = RSEMS[slot]

            @emit('gpsimd')
            def _(g, slot=slot, src_ap=src_ap, fm=fm, width=width, rows=rows,
                  sname=sname):
                if fm > 0:
                    g.wait_ge(SEM['t'], fm)
                i = g.dma_start(ring[0:rows, slot * 2048: slot * 2048 + width], src_ap)
                inc(i, sname, 16)
            return (ring[0:rows, slot * 2048: slot * 2048 + width],
                    (sname, CNT[sname]), slot)

        def emit_ln(first=False):
            bar = dict(CNT)
            tmarks = []
            for c in range(NCH):
                pp = c % 2

                @emit('vector')
                def _(v, c=c, pp=pp, tmarks=list(tmarks), bar=bar, first=first):
                    if c == 0:
                        wait_marks(v, {k: bar[k] for k in ('t', 's', 'p', 'sq')})
                    if first and c == 0:
                        v.wait_ge(SEM['r0'], 16)
                    if first and c == 8:
                        v.wait_ge(SEM['r1'], 16)
                    if c >= 2:
                        v.wait_ge(SEM['t'], tmarks[c - 2])
                    v.tensor_copy(lnx_v[pp], x_v[:, c, :])
                    i = v.tensor_tensor(lnq_v[pp], lnx_v[pp], lnx_v[pp], ALU.mult)
                    inc(i, 'v', 1)
                mv = CNT['v']

                @emit('tensor')
                def _(t, c=c, pp=pp, mv=mv, bar=bar):
                    if c == 0:
                        wait_marks(t, {k: bar[k] for k in ('s',)})
                    t.wait_ge(SEM['v'], mv)
                    st, sp = (c == 0), (c == NCH - 1)
                    t.matmul(pA[:, 0:512], ones_sb, lnx_v[pp][:, 0:512], start=st, stop=sp)
                    t.matmul(pA[:, 512:1024], ones_sb, lnx_v[pp][:, 512:1024], start=st, stop=sp)
                    t.matmul(pB[:, 0:512], ones_sb, lnq_v[pp][:, 0:512], start=st, stop=sp)
                    i = t.matmul(pB[:, 512:1024], ones_sb, lnq_v[pp][:, 512:1024], start=st, stop=sp)
                    inc(i, 't', 1)
                tmarks.append(CNT['t'])
            m_sums = CNT['t']

            @emit('vector')
            def _(v, m_sums=m_sums):
                v.wait_ge(SEM['t'], m_sums)
                v.tensor_scalar(bcA, pA, 1.0 / D, None, ALU.mult)
                v.tensor_tensor(scb_v, bcA, bcA, ALU.mult)
                v.tensor_scalar(tsil_v, pB, 1.0 / D, None, ALU.mult)
                i = v.tensor_tensor(tsil_v, tsil_v, scb_v, ALU.subtract)
                inc(i, 'v', 1)
            m_var = CNT['v']

            @emit('scalar')
            def _(s, m_var=m_var):
                s.wait_ge(SEM['v'], m_var)
                i = s.activation(tsil_v, tsil_v, AF.Sqrt, bias=EPS, scale=1.0)
                inc(i, 's', 1)
            m_sd = CNT['s']

            @emit('vector')
            def _(v, m_sd=m_sd):
                v.wait_ge(SEM['s'], m_sd)
                i = v.reciprocal(bcB, tsil_v)
                mk = inc(i, 'v', 1)
                v.wait_ge(SEM['v'], mk)
                i = v.tensor_tensor(bcA, bcA, bcB, ALU.mult)
                inc(i, 'v', 1)
            m_rows = CNT['v']

            halfmarks = []
            for half in range(2):
                hs2 = slice(half * 512, (half + 1) * 512)
                for c in range(NCH):
                    if c % 4 != 3:
                        @emit('vector')
                        def _(v, c=c, hs2=hs2, m_rows=m_rows):
                            v.wait_ge(SEM['v'], m_rows)
                            v.tensor_tensor(scb_v[:, hs2], x_v[:, c, hs2],
                                            bcB[:, hs2], ALU.mult)
                            i = v.tensor_tensor(h3[:, c, hs2], scb_v[:, hs2],
                                                bcA[:, hs2], ALU.subtract)
                            inc(i, 'v', 1)
                    else:
                        @emit('gpsimd')
                        def _(g, c=c, hs2=hs2, m_rows=m_rows):
                            g.wait_ge(SEM['v'], m_rows)
                            g.tensor_tensor(tsil_v[:, hs2], x_v[:, c, hs2],
                                            bcB[:, hs2], ALU.mult)
                            i = g.tensor_tensor(h3[:, c, hs2], tsil_v[:, hs2],
                                                bcA[:, hs2], ALU.subtract)
                            inc(i, 'p', 1)
                halfmarks.append((CNT['v'], CNT['p']))
            return halfmarks

        def emit_ffn(halfmarks):
            hv, hp = halfmarks[1]
            for tl in range(8):
                m_prev_s = CNT['s']
                m_prev_v = CNT['v']
                hvt, hpt = halfmarks[0 if tl < 4 else 1]

                @emit('tensor')
                def _(t, tl=tl, hvt=hvt, hpt=hpt, m_prev_s=m_prev_s,
                      m_prev_v=m_prev_v):
                    t.wait_ge(SEM['v'], max(hvt, m_prev_v))
                    t.wait_ge(SEM['p'], hpt)
                    t.wait_ge(SEM['s'], m_prev_s)
                    for k in range(16):
                        i = t.matmul(pD[:, 0:64], h3[:, k, tl * 128:(tl + 1) * 128],
                                     gwperm_sb[:, k * 64:(k + 1) * 64],
                                     start=(k == 0), stop=(k == 15))
                    inc(i, 't', 1)
                m_sc = CNT['t']

                @emit('vector')
                def _(v, m_sc=m_sc):
                    v.wait_ge(SEM['t'], m_sc)
                    i = v.reduce_max(gnm[:, 0:1], pD[:, 0:64], AX.X, negate=True)
                    inc(i, 'v', 1)
                m_nmx = CNT['v']

                @emit('scalar')
                def _(s, m_nmx=m_nmx):
                    s.wait_ge(SEM['v'], m_nmx)
                    i = s.activation(gel, pD[:, 0:64], AF.Exp, bias=gnm[:, 0:1], scale=1.0)
                    inc(i, 's', 1)
                m_el = CNT['s']

                @emit('vector')
                def _(v, m_el=m_el):
                    v.wait_ge(SEM['s'], m_el)
                    i = v.reduce_sum(gnm[:, 1:2], gel, AX.X)
                    mk = inc(i, 'v', 1)
                    v.wait_ge(SEM['v'], mk)
                    i = v.reciprocal(gnm[:, 2:3], gnm[:, 1:2])
                    mk = inc(i, 'v', 1)
                    v.wait_ge(SEM['v'], mk)
                    v.tensor_scalar(gpr, gel, gnm[:, 2:3], None, ALU.mult)
                    i = v.max(gt8, gpr)
                    mk = inc(i, 'v', 1)
                    v.wait_ge(SEM['v'], mk)
                    v.tensor_scalar(gm0, gpr, gt8[:, 7:8], None, ALU.is_ge)
                    v.tensor_tensor(gfl, gpr, gm0, ALU.mult)
                    v.tensor_scalar(gm0, gfl, 0.0, None, ALU.is_equal)
                    i = v.scalar_tensor_tensor(gel, gm0, -1e9, gfl, ALU.mult, ALU.add)
                    inc(i, 'v', 1)
                m_flatm = CNT['v']

                @emit('scalar')
                def _(s, m_flatm=m_flatm):
                    s.wait_ge(SEM['v'], m_flatm)
                    i = s.activation(gel, gel, AF.Exp, bias=0.0, scale=1.0)
                    inc(i, 's', 1)
                m_ef = CNT['s']

                @emit('vector')
                def _(v, m_ef=m_ef):
                    v.wait_ge(SEM['s'], m_ef)
                    i = v.tensor_reduce(gden, gel.rearrange("p (g s) -> p g s", g=8),
                                        AX.X, ALU.add)
                    mk = inc(i, 'v', 1)
                    v.wait_ge(SEM['v'], mk)
                    i = v.tensor_scalar(gden, gden, 1e-12, None, ALU.add)
                    mk = inc(i, 'v', 1)
                    v.wait_ge(SEM['v'], mk)
                    i = v.reciprocal(grd, gden)
                    mk = inc(i, 'v', 1)
                    v.wait_ge(SEM['v'], mk)
                    for g in range(8):
                        v.tensor_scalar(gs16[:, 8 * g:8 * g + 8], gel[:, 8 * g:8 * g + 8],
                                        grd[:, g:g + 1], None, ALU.mult)
                    i = v.tensor_reduce(gsc, gfl.rearrange("p (g s) -> p g s", g=8),
                                        AX.X, ALU.add)
                    mk = inc(i, 'v', 1)
                    v.wait_ge(SEM['v'], mk)
                    i = v.tensor_copy(gsc16, gsc)
                    inc(i, 'v', 1)
                m_soft = CNT['v']

                @emit('tensor')
                def _(t, tl=tl, m_soft=m_soft):
                    t.wait_ge(SEM['v'], m_soft)
                    t.transpose(pT[0:64, 0:128], gs16, ident_sb)
                    i = t.transpose(pT[0:8, 128:256], gsc16, ident_sb)
                    inc(i, 't', 1)
                m_tr = CNT['t']

                @emit('vector')
                def _(v, tl=tl, m_tr=m_tr):
                    v.wait_ge(SEM['t'], m_tr)
                    v.tensor_copy(softT_sb[:, tl * 128:(tl + 1) * 128], pT[0:64, 0:128])
                    i = v.tensor_copy(scalT_sb[:, tl * 128:(tl + 1) * 128], pT[0:8, 128:256])
                    inc(i, 'v', 1)
            m_gating = CNT['v']

            m_xg_free_t = CNT['t']
            m_scb_free = CNT['v']
            m_dn = [0] * 16
            m_sil = [0] * IC
            m_a = [0] * IC
            mu_wq = 0
            for g in range(8):
                m_mu_reader = CNT['t']

                @emit('gpsimd')
                def _(gg, g=g, m_mu_reader=m_mu_reader):
                    gg.wait_ge(SEM['t'], m_mu_reader)
                    i = gg.dma_start(mu_sb, dr["mu_pair"][g])
                    inc(i, 'mu', 16)
                mu_wq = CNT['mu']

                @emit('tensor')
                def _(t, g=g, m_gating=m_gating, m_dnv=list(m_dn)):
                    t.wait_ge(SEM['v'], m_gating)
                    if m_dnv[14]:
                        t.wait_ge(SEM['v'], m_dnv[14])
                    if m_dnv[15]:
                        t.wait_ge(SEM['v'], m_dnv[15])
                    t.matmul(pA[:, 0:512], oh_sb[:, g * 128:(g + 1) * 128],
                             scalT_sb[:, 0:512], start=True, stop=True)
                    i = t.matmul(pA[:, 512:1024], oh_sb[:, g * 128:(g + 1) * 128],
                                 scalT_sb[:, 512:1024], start=True, stop=True)
                    inc(i, 't', 1)
                m_scbt = CNT['t']

                @emit('vector')
                def _(v, m_scbt=m_scbt, m_a10=m_a[IC - 1]):
                    v.wait_ge(SEM['t'], m_scbt)
                    if m_a10:
                        v.wait_ge(SEM['v'], m_a10)
                    i = v.tensor_copy(scb_v, pA)
                    inc(i, 'v', 1)
                m_scb = CNT['v']

                m_xg = [0] * NCH
                for c in range(NCH):
                    pX = pA if c % 2 == 0 else pB

                    @emit('tensor')
                    def _(t, g=g, c=c, pX=pX, mu_wq=mu_wq, m_scb=m_scb,
                          m_xgf=m_xg_free_t, m_xg=list(m_xg)):
                        t.wait_ge(SEM['mu'], mu_wq)
                        t.wait_ge(SEM['t'], m_xgf)
                        if c == 0:
                            t.wait_ge(SEM['v'], m_scb)
                        if c >= 2:
                            t.wait_ge(SEM['v'], m_xg[c - 2])
                        lhsT = mu_sb[:, c * 128:(c + 1) * 128]
                        t.matmul(pX[:, 0:512], lhsT, softT_sb[:, 0:512], start=True, stop=True)
                        i = t.matmul(pX[:, 512:1024], lhsT, softT_sb[:, 512:1024],
                                     start=True, stop=True)
                        inc(i, 't', 1)
                    m_dc = CNT['t']

                    @emit('vector')
                    def _(v, c=c, pX=pX, m_dc=m_dc):
                        v.wait_ge(SEM['t'], m_dc)
                        i = v.tensor_tensor(xg_v[:, c, :], h3[:, c, :], pX, ALU.add)
                        inc(i, 'v', 1)
                    m_xg[c] = CNT['v']
                m_xg_done = CNT['v']

                banks3 = [pA, pB, pC]
                bfree = {0: ('v', m_xg[14]), 1: ('v', m_xg[15]),
                         2: ('v', m_a[IC - 1])}
                for oc in range(IC):
                    gi = (2 * oc) % 3
                    ui = (2 * oc + 1) % 3
                    pG = banks3[gi]
                    pU = banks3[ui]
                    gw_tile, gw_wq, gw_slot = ring_tile(dr["gw_l"][g][oc])

                    @emit('tensor')
                    def _(t, g=g, oc=oc, pG=pG, gw_wq=gw_wq, m_xg_done=m_xg_done,
                          bf=bfree[gi]):
                        t.wait_ge(SEM[gw_wq[0]], gw_wq[1])
                        t.wait_ge(SEM['v'], m_xg_done)
                        if bf[1]:
                            t.wait_ge(SEM[bf[0]], bf[1])
                        for k in range(16):
                            t.matmul(pG[:, 0:512], gw_tile[:, k * 128:(k + 1) * 128],
                                     xg_v[:, k, 0:512], start=(k == 0), stop=(k == 15))
                            i = t.matmul(pG[:, 512:1024], gw_tile[:, k * 128:(k + 1) * 128],
                                         xg_v[:, k, 512:1024], start=(k == 0), stop=(k == 15))
                        inc(i, 't', 1)
                    ring_free[gw_slot] = CNT['t']
                    m_gt = CNT['t']

                    @emit('scalar')
                    def _(s, oc=oc, pG=pG, m_gt=m_gt):
                        s.wait_ge(SEM['t'], m_gt)
                        i = s.activation(tsil_v, pG, AF.Silu, bias=0.0, scale=1.0)
                        inc(i, 's', 1)
                    m_sil[oc] = CNT['s']
                    bfree[gi] = ('s', m_sil[oc])

                    uw_tile, uw_wq, uw_slot = ring_tile(dr["uw_l"][g][oc])

                    @emit('tensor')
                    def _(t, g=g, oc=oc, pU=pU, uw_wq=uw_wq, bf=bfree[ui],
                          m_xg_done=m_xg_done):
                        t.wait_ge(SEM[uw_wq[0]], uw_wq[1])
                        t.wait_ge(SEM['v'], m_xg_done)
                        if bf[1]:
                            t.wait_ge(SEM[bf[0]], bf[1])
                        for k in range(16):
                            t.matmul(pU[:, 0:512], uw_tile[:, k * 128:(k + 1) * 128],
                                     xg_v[:, k, 0:512], start=(k == 0), stop=(k == 15))
                            i = t.matmul(pU[:, 512:1024], uw_tile[:, k * 128:(k + 1) * 128],
                                         xg_v[:, k, 512:1024], start=(k == 0), stop=(k == 15))
                        inc(i, 't', 1)
                    ring_free[uw_slot] = CNT['t']
                    m_up = CNT['t']

                    @emit('vector')
                    def _(v, oc=oc, pU=pU, m_up=m_up, m_silo=m_sil[oc], m_scb=m_scb):
                        v.wait_ge(SEM['t'], m_up)
                        v.wait_ge(SEM['s'], m_silo)
                        v.wait_ge(SEM['v'], m_scb)
                        i = v.tensor_tensor(tsil_v, tsil_v, pU, ALU.mult)
                        mk = inc(i, 'v', 1)
                        v.wait_ge(SEM['v'], mk)
                        i = v.tensor_tensor(a_v[:, oc, :], tsil_v, scb_v, ALU.mult)
                        inc(i, 'v', 1)
                    m_a[oc] = CNT['v']
                    bfree[ui] = ('v', m_a[oc])
                m_a_done = CNT['v']

                m_dn = [0] * 16
                for hc in range(16):
                    dw_tile, dw_wq, dw_slot = ring_tile(dr["dw_l"][g][hc], width=1408)
                    pX = pA if hc % 2 == 0 else pB

                    @emit('tensor')
                    def _(t, g=g, hc=hc, pX=pX, dw_wq=dw_wq, m_a_done=m_a_done,
                          m_sil_last=m_sil[IC - 1], m_dn2=(m_dn[hc - 2] if hc >= 2 else 0)):
                        t.wait_ge(SEM[dw_wq[0]], dw_wq[1])
                        t.wait_ge(SEM['v'], m_a_done)
                        t.wait_ge(SEM['s'], m_sil_last)
                        if m_dn2:
                            t.wait_ge(SEM['v'], m_dn2)
                        for ik in range(IC):
                            t.matmul(pX[:, 0:512], dw_tile[:, ik * 128:(ik + 1) * 128],
                                     a_v[:, ik, 0:512], start=(ik == 0), stop=(ik == IC - 1))
                            i = t.matmul(pX[:, 512:1024], dw_tile[:, ik * 128:(ik + 1) * 128],
                                         a_v[:, ik, 512:1024], start=(ik == 0), stop=(ik == IC - 1))
                        inc(i, 't', 1)
                    ring_free[dw_slot] = CNT['t']
                    m_dt = CNT['t']

                    @emit('vector')
                    def _(v, hc=hc, pX=pX, m_dt=m_dt):
                        v.wait_ge(SEM['t'], m_dt)
                        i = v.tensor_tensor(x_v[:, hc, :], x_v[:, hc, :], pX, ALU.add)
                        inc(i, 'v', 1)
                    m_dn[hc] = CNT['v']
                m_xg_free_t = CNT['t']
            return list(m_dn)


        def emit_attention():
            kdv = kd_v.rearrange("p (q t) -> p q t", q=16)
            klv = gpp_v.rearrange("p (q t) -> p q t", q=16)
            bbq = bb_v.rearrange("p (q t) -> p q t", q=16)
            ebq = ebm_v.rearrange("p (q t) -> p q t", q=16)
            ballv = Ball_sb.rearrange("p (h c d) -> p h c d", h=4, c=16)

            bar = dict(CNT)

            @emit('sync')
            def _(sy, bar=bar):
                wait_marks(sy, {k: bar[k] for k in ('t', 'v', 's', 'p')})
                i = sy.dma_start(xspill_d, A.at(0, F32, [128, 16384]))
                inc(i, 'sq', 16)
            m_spill = CNT['sq']

            @emit('gpsimd')
            def _(g, m_spill=m_spill):
                g.wait_ge(SEM['sq'], m_spill)
                g.memset(S32_v, 0.0)
                g.memset(S16_v, 0.0)
                i = g.memset(Bsum_sb, 0.0)
                inc(i, 'p', 1)
            m_zero = CNT['p']

            if ATTN_PH < 2:
                return
            psl = [pA[:, 0:512], pA[:, 512:1024], pB[:, 0:512], pB[:, 512:1024]]
            ps_free = [0, 0, 0, 0]
            ps_free_sem = ['s', 's', 's', 's']
            m_s16 = [('p', m_zero)] * NH
            m_exps_prev = 0
            m_oev = [0] * NH
            m_oev_ph = [0, 0]
            m_supd = [0] * NH
            m_supd_last = 0
            m_atm_prev = 0
            m_ptfree = [0, 0]

            for hf in range(2):
                hs = slice(hf * 512, hf * 512 + 512)
                wgk2_t, wgk2_wq, wgk2_slot = ring_tile(dr["wgk2"], width=2048, rows=16)
                m_pD_free = m_atm_prev

                @emit('tensor')
                def _(t, hf=hf, hs=hs, wgk2_wq=wgk2_wq, bar=bar, m_pD_free=m_pD_free):
                    if hf == 0:
                        wait_marks(t, {k: bar[k] for k in ('v', 's', 'p')})
                    t.wait_ge(SEM[wgk2_wq[0]], wgk2_wq[1])
                    if m_pD_free:
                        t.wait_ge(SEM['v'], m_pD_free)
                    for k in range(16):
                        i = t.matmul(pD[0:16, 0:512], wgk1_sb[:, k * 16:(k + 1) * 16],
                                     h3[:, k, hs], start=(k == 0), stop=(k == 15))
                    inc(i, 't', 1)
                m_gk1 = CNT['t']

                @emit('vector')
                def _(v, m_gk1=m_gk1):
                    v.wait_ge(SEM['t'], m_gk1)
                    i = v.tensor_copy(gkt_v, pD[0:16, 0:512])
                    inc(i, 'v', 1)
                m_gkt = CNT['v']

                m_sig_prev = 0
                for op_ in range(8):
                    for half in range(2):
                        oc = op_ * 2 + half
                        pX = pC[:, half * 512:half * 512 + 512]

                        @emit('tensor')
                        def _(t, oc=oc, pX=pX, m_gkt=m_gkt, wgk2_slot=wgk2_slot,
                              m_prev=m_sig_prev):
                            t.wait_ge(SEM['v'], m_gkt)
                            if m_prev > 0:
                                t.wait_ge(SEM['s'], m_prev)
                            i = t.matmul(pX, ring[0:16, wgk2_slot * 2048 + oc * 128:
                                                  wgk2_slot * 2048 + (oc + 1) * 128],
                                         gkt_v, start=True, stop=True)
                            inc(i, 't', 1)
                    m_g2 = CNT['t']

                    @emit('scalar')
                    def _(sc, op_=op_, m_g2=m_g2):
                        sc.wait_ge(SEM['t'], m_g2)
                        sc.activation(sgt_v, pC[:, 0:512], AF.Sigmoid,
                                      bias=bgk2_sb[:, 2 * op_:2 * op_ + 1], scale=1.0)
                        i = sc.activation(sg2_v, pC[:, 512:1024], AF.Sigmoid,
                                          bias=bgk2_sb[:, 2 * op_ + 1:2 * op_ + 2], scale=1.0)
                        mk = inc(i, 's', 1)
                        sc.activation(gk_v[:, 2 * op_, :], sgt_v, AF.Ln, bias=0.0, scale=1.0)
                        i = sc.activation(gk_v[:, 2 * op_ + 1, :], sg2_v, AF.Ln,
                                          bias=0.0, scale=1.0)
                        inc(i, 's', 1)
                    m_sig_prev = CNT['s'] - 1
                ring_free[wgk2_slot] = CNT['t']
                m_gk_done = CNT['s']

                for which, wsrc, scale_ in (("q", "wq_l", QSCALE), ("k", "wk_l", 1.0)):
                    for oc in range(16):
                        tile, twq, slot = ring_tile(dr[wsrc][oc])
                        sl = oc % 4

                        @emit('tensor')
                        def _(t, oc=oc, sl=sl, tile=tile, twq=twq,
                              fm=ps_free[sl], fsem=ps_free_sem[sl], hs=hs):
                            t.wait_ge(SEM[twq[0]], twq[1])
                            if fm:
                                t.wait_ge(SEM[fsem], fm)
                            for k in range(16):
                                i = t.matmul(psl[sl], tile[:, k * 128:(k + 1) * 128],
                                             h3[:, k, hs], start=(k == 0), stop=(k == 15))
                            inc(i, 't', 1)
                        ring_free[slot] = CNT['t']
                        m_mm = CNT['t']

                        @emit('vector')
                        def _(v, which=which, oc=oc, sl=sl, m_mm=m_mm, hf=hf,
                              scale_=scale_, m_spill=m_spill):
                            v.wait_ge(SEM['t'], m_mm)
                            if which == "q":
                                v.wait_ge(SEM['sq'], m_spill)
                            dstv = (qe_v[:, oc, hf * 512:hf * 512 + 512] if which == "q"
                                    else k_v[:, oc, :])
                            if which == "q":
                                i = v.tensor_scalar(dstv, psl[sl], scale_, None, ALU.mult)
                            else:
                                i = v.tensor_copy(dstv, psl[sl])
                            inc(i, 'v', 1)
                        ps_free[sl] = CNT['v']
                        ps_free_sem[sl] = 'v'

                for oc in range(8):
                    tile, twq, slot = ring_tile(dr["wv_l"][oc])
                    sl = oc % 4

                    @emit('tensor')
                    def _(t, oc=oc, sl=sl, tile=tile, twq=twq,
                          fm=ps_free[sl], fsem=ps_free_sem[sl], hs=hs):
                        t.wait_ge(SEM[twq[0]], twq[1])
                        if fm:
                            t.wait_ge(SEM[fsem], fm)
                        for k in range(16):
                            i = t.matmul(psl[sl], tile[:, k * 128:(k + 1) * 128],
                                         h3[:, k, hs], start=(k == 0), stop=(k == 15))
                        inc(i, 't', 1)
                    ring_free[slot] = CNT['t']
                    m_mm = CNT['t']

                    @emit('vector')
                    def _(v, sl=sl, m_mm=m_mm, m_gk_done=m_gk_done):
                        v.wait_ge(SEM['t'], m_mm)
                        v.wait_ge(SEM['s'], m_gk_done)
                        i = v.tensor_copy(gpp_v[:, 0:512], psl[sl])
                        inc(i, 'v', 1)
                    ps_free[sl] = CNT['v']
                    ps_free_sem[sl] = 'v'
                    m_ev = CNT['v']

                    @emit('tensor')
                    def _(t, m_ev=m_ev, m_ptf=m_ptfree[0]):
                        t.wait_ge(SEM['v'], m_ev)
                        if m_ptf:
                            t.wait_ge(SEM['v'], m_ptf)
                        for j in range(4):
                            i = t.transpose(pT[:, j * 128:(j + 1) * 128],
                                            gpp_v[:, j * 128:(j + 1) * 128], ident_sb)
                        inc(i, 't', 1)
                    m_tr = CNT['t']

                    @emit('vector')
                    def _(v, oc=oc, m_tr=m_tr):
                        v.wait_ge(SEM['t'], m_tr)
                        i = v.tensor_copy(v_v[:, 0:4, oc * 128:(oc + 1) * 128],
                                          pT.rearrange("p (j t) -> p j t", j=8)[:, 0:4, :])
                        inc(i, 'v', 1)
                    m_ptfree[0] = CNT['v']

                for c in range(8):
                    ci = hf * 8 + c

                    @emit('vector')
                    def _(v, c=c, ci=ci, m_exps_prev=m_exps_prev, m_gk_done=m_gk_done,
                          m_zero=m_zero):
                        v.wait_ge(SEM['s'], max(m_exps_prev, m_gk_done))
                        v.wait_ge(SEM['p'], m_zero)
                        v.tensor_copy(ballv[:, :, ci, :],
                                      Bsum_sb.rearrange("p (h d) -> p h d", h=4))
                        for q in range(16):
                            i = v.tensor_tensor_scan(bbq[:, q, :],
                                                     gk_v[:, q, c * 64:(c + 1) * 64],
                                                     zz64, 0.0, ALU.add, ALU.add)
                        mk = inc(i, 'v', 1)
                        v.wait_ge(SEM['v'], mk)
                        i = v.tensor_tensor(Bsum_sb, Bsum_sb, bbq[:, :, 63], ALU.add)
                        inc(i, 'v', 1)
                    m_b = CNT['v']

                    @emit('scalar')
                    def _(sc, m_b=m_b):
                        sc.wait_ge(SEM['v'], m_b)
                        sc.activation(ebl_sb, bbq[:, :, 63], AF.Exp, bias=0.0, scale=R16)
                        i = sc.activation(ebm_v, bb_v, AF.Exp, bias=0.0, scale=R16)
                        inc(i, 's', 1)
                    m_eb = CNT['s']

                    m_exps_prev = m_eb

                    @emit('vector')
                    def _(v, ci=ci, c=c, m_eb=m_eb):
                        v.wait_ge(SEM['s'], m_eb)
                        i = v.tensor_tensor(
                            qe_v[:, :, ci * 64:ci * 64 + 64],
                            qe_v[:, :, ci * 64:ci * 64 + 64],
                            ebq, ALU.mult)
                        mk = inc(i, 'v', 1)
                        v.wait_ge(SEM['v'], mk)
                        i = v.reciprocal(bb_v, ebm_v)
                        mk = inc(i, 'v', 1)
                        v.wait_ge(SEM['v'], mk)
                        i = v.tensor_tensor(kdv, k_v[:, :, c * 64:c * 64 + 64],
                                            bbq, ALU.mult)
                        mk = inc(i, 'v', 1)
                        v.wait_ge(SEM['v'], mk)
                        for q in range(16):
                            i = v.tensor_scalar(klv[:, q, :], kdv[:, q, :],
                                                ebl_sb[:, q:q + 1], None, ALU.mult)
                        inc(i, 'v', 1)
                    m_kl = CNT['v']

                    for h in range(NH):
                        ph = h % 2

                        b0 = 64 * (c % 2)

                        @emit('tensor')
                        def _(t, h=h, ph=ph, b0=b0, m_kl=m_kl, m_ptf=m_ptfree[ph]):
                            t.wait_ge(SEM['v'], m_kl)
                            if m_ptf:
                                t.wait_ge(SEM['v'], m_ptf)
                            for dc in range(4):
                                i = t.transpose(
                                    pT[b0:b0 + 64,
                                       ph * 512 + dc * 128:ph * 512 + (dc + 1) * 128],
                                    gpp_v[:, (4 * h + dc) * 64:(4 * h + dc + 1) * 64],
                                    ident_sb, tile_position=(0, b0))
                            inc(i, 't', 1)
                        m_trh = CNT['t']

                        @emit('vector')
                        def _(v, h=h, ph=ph, b0=b0, m_trh=m_trh):
                            v.wait_ge(SEM['t'], m_trh)
                            i = v.tensor_copy(kl_tm_v[b0:b0 + 64, 4 * h:4 * h + 4, :],
                                              pT.rearrange("p (j t) -> p j t", j=8)
                                              [b0:b0 + 64, ph * 4:ph * 4 + 4, :])
                            inc(i, 'v', 1)
                        m_klev = CNT['v']
                        m_ptfree[ph] = m_klev

                        @emit('tensor')
                        def _(t, h=h, ci=ci, b0=b0, m_kl=m_kl, m_atm_prev=m_atm_prev):
                            if m_atm_prev:
                                t.wait_ge(SEM['v'], m_atm_prev)
                            t.wait_ge(SEM['v'], m_kl)
                            for dc in range(4):
                                i = t.matmul(pD[b0:b0 + 64, 0:64], kdv[:, 4 * h + dc, :],
                                             qe_v[:, 4 * h + dc, ci * 64:ci * 64 + 64],
                                             start=(dc == 0), stop=(dc == 3),
                                             tile_position=(0, b0))
                            inc(i, 't', 1)
                        m_at = CNT['t']

                        @emit('vector')
                        def _(v, b0=b0, m_at=m_at):
                            v.wait_ge(SEM['t'], m_at)
                            i = v.tensor_tensor(at16_v[b0:b0 + 64, :],
                                                pD[b0:b0 + 64, 0:64],
                                                maskT_sb[b0:b0 + 64, :], ALU.mult)
                            inc(i, 'v', 1)
                        m_atm = CNT['v']
                        m_atm_prev = m_atm

                        @emit('tensor')
                        def _(t, h=h, c=c, ci=ci, ph=ph, m_atm=m_atm,
                              m_s16h=m_s16[h], m_oev2=m_oev_ph[ph]):
                            t.wait_ge(SEM['v'], m_atm)
                            t.wait_ge(SEM[m_s16h[0]], m_s16h[1])
                            if m_oev2:
                                t.wait_ge(SEM['v'], m_oev2)
                            b0_ = 64 * (c % 2)
                            vtm = v_v[b0_:b0_ + 64, c // 2, h * 256:(h + 1) * 256]
                            t.matmul(pC[b0_:b0_ + 64, ph * 512:ph * 512 + 256],
                                     at16_v[b0_:b0_ + 64, :], vtm,
                                     start=True, stop=False, tile_position=(b0_, b0_))
                            for dc in range(4):
                                i = t.matmul(pC[b0_:b0_ + 64, ph * 512:ph * 512 + 256],
                                             qe_v[:, 4 * h + dc, ci * 64:ci * 64 + 64],
                                             S16_v[:, 4 * h + dc, :],
                                             start=False, stop=(dc == 3),
                                             tile_position=(0, b0_))
                            inc(i, 't', 1)
                        m_o = CNT['t']

                        @emit('vector')
                        def _(v, h=h, ci=ci, ph=ph, m_o=m_o):
                            v.wait_ge(SEM['t'], m_o)
                            b0_ = 64 * (ci % 2)
                            i = v.tensor_copy(
                                oraw_v[b0_:b0_ + 64, ci // 2, h * 256:(h + 1) * 256],
                                pC[b0_:b0_ + 64, ph * 512:ph * 512 + 256])
                            inc(i, 'v', 1)
                        m_oev[h] = CNT['v']
                        m_oev_ph[ph] = CNT['v']

                        @emit('tensor')
                        def _(t, h=h, c=c, b0=b0, m_klev=m_klev, m_supd_prev=m_supd_last):
                            t.wait_ge(SEM['v'], m_klev)
                            if m_supd_prev:
                                t.wait_ge(SEM['v'], m_supd_prev)
                            vtm = v_v[b0:b0 + 64, c // 2, h * 256:(h + 1) * 256]
                            for dc in range(4):
                                i = t.matmul(pA[:, dc * 256:(dc + 1) * 256],
                                             kl_tm_v[b0:b0 + 64, 4 * h + dc, :], vtm,
                                             start=True, stop=True)
                            inc(i, 't', 1)
                        m_M = CNT['t']

                        @emit('vector')
                        def _(v, h=h, m_M=m_M):
                            v.wait_ge(SEM['t'], m_M)
                            for dc in range(4):
                                i = v.scalar_tensor_tensor(
                                    S32_v[:, 4 * h + dc, :], S32_v[:, 4 * h + dc, :],
                                    ebl_sb[:, 4 * h + dc:4 * h + dc + 1],
                                    pA[:, dc * 256:(dc + 1) * 256], ALU.mult, ALU.add)
                            inc(i, 'v', 1)
                        m_supd[h] = CNT['v']
                        m_supd_last = CNT['v']

                        if h == NH - 1:
                            @emit('vector')
                            def _(v, m_o=m_o):
                                v.wait_ge(SEM['t'], m_o)
                                i = v.tensor_copy(S16_v, S32_v)
                                inc(i, 'v', 1)
                            for hh in range(NH):
                                m_s16[hh] = ('v', CNT['v'])

            if ATTN_PH < 3:
                return
            m_scan_t = CNT['t']
            m_scan_v = CNT['v']
            ps_free = [m_scan_v] * 4
            ps_free_sem = ['v'] * 4

            @emit('scalar')
            def _(sc, m_scan_v=m_scan_v):
                sc.wait_ge(SEM['v'], m_scan_v)
                i = sc.activation(dall_sb, Bsum_sb, AF.Exp, bias=0.0, scale=R16)
                inc(i, 's', 1)
            m_dall = CNT['s']

            m_exp_dma = 0
            for hd in range(2):
                @emit('vector')
                def _(v, hd=hd, m_dall=m_dall, m_scan_t=m_scan_t, m_prev=m_exp_dma):
                    v.wait_ge(SEM['s'], m_dall)
                    v.wait_ge(SEM['t'], m_scan_t)
                    if m_prev:
                        v.wait_ge(SEM['sq'], m_prev)
                    v.tensor_copy(stg_v[:, :, 0:256], S32_v[:, hd * 8:(hd + 1) * 8, :])
                    i = v.tensor_copy(stg_v[:, :, 256], dall_sb[:, hd * 8:(hd + 1) * 8])
                    inc(i, 'v', 1)
                m_st = CNT['v']

                @emit('sync')
                def _(sy, hd=hd, m_st=m_st):
                    sy.wait_ge(SEM['v'], m_st)
                    i = sy.dma_start(
                        comm_in.rearrange("(q p) w -> p q w", p=128)
                        [:, hd * 8:(hd + 1) * 8, :],
                        stg_v)
                    inc(i, 'sq', 16)
                m_exp_dma = CNT['sq']
            m_export = CNT['sq']

            @emit('gpsimd')
            def _(g, m_export=m_export):
                g.wait_ge(SEM['sq'], m_export)
                i = g.collective_compute(
                    "AllGather", ALU.bypass, replica_groups=[list(range(8))],
                    ins=[comm_in], outs=[comm_out])
                inc(i, 'cc', 1)
            m_cc = CNT['cc']

            if ATTN_PH < 4:
                return
            for oc in range(8):
                tile, twq, slot = ring_tile(dr["wg_l"][oc])
                for hf in range(2):
                    sl = (oc * 2 + hf) % 4

                    @emit('tensor')
                    def _(t, oc=oc, hf=hf, sl=sl, tile=tile, twq=twq,
                          fm=ps_free[sl], fsem=ps_free_sem[sl]):
                        t.wait_ge(SEM[twq[0]], twq[1])
                        if fm:
                            t.wait_ge(SEM[fsem], fm)
                        for k in range(16):
                            i = t.matmul(psl[sl], tile[:, k * 128:(k + 1) * 128],
                                         h3[:, k, hf * 512:hf * 512 + 512],
                                         start=(k == 0), stop=(k == 15))
                        inc(i, 't', 1)
                    ring_free[slot] = CNT['t']
                    m_mm = CNT['t']

                    @emit('vector')
                    def _(v, sl=sl, m_mm=m_mm, m_export=m_export):
                        v.wait_ge(SEM['t'], m_mm)
                        v.wait_ge(SEM['sq'], m_export)
                        i = v.tensor_copy(gpp_v[:, 0:512], psl[sl])
                        inc(i, 'v', 1)
                    ps_free[sl] = CNT['v']
                    ps_free_sem[sl] = 'v'
                    m_ev = CNT['v']

                    @emit('tensor')
                    def _(t, m_ev=m_ev, m_ptf=m_ptfree[0]):
                        t.wait_ge(SEM['v'], m_ev)
                        if m_ptf:
                            t.wait_ge(SEM['v'], m_ptf)
                        for j in range(4):
                            i = t.transpose(pT[:, j * 128:(j + 1) * 128],
                                            gpp_v[:, j * 128:(j + 1) * 128], ident_sb)
                        inc(i, 't', 1)
                    m_tr = CNT['t']

                    @emit('vector')
                    def _(v, oc=oc, hf=hf, m_tr=m_tr):
                        v.wait_ge(SEM['t'], m_tr)
                        i = v.tensor_copy(
                            g_tm_v[:, hf * 4:hf * 4 + 4, oc * 128:(oc + 1) * 128],
                            pT.rearrange("p (j t) -> p j t", j=8)[:, 0:4, :])
                        inc(i, 'v', 1)
                    m_ptfree[0] = CNT['v']
            m_gproj = CNT['v']

            if ATTN_PH < 5:
                return
            rbh_v = [A.at(32768, BF16, [128, 8, 257]), A.at(41088, BF16, [128, 8, 257])]
            acch_v = A.at(49408, F32, [128, 8, 256])
            comm_rv = comm_out.rearrange("(r q p) w -> r p q w", r=8, q=16)
            m_rb_war = [0, 0]
            m_s0h = []

            def emit_comb_half(hcs):
                @emit('vector')
                def _(v, hcs=hcs, m_dall=m_dall, m_scan_t=m_scan_t):
                    if hcs == 0:
                        v.wait_ge(SEM['s'], m_dall)
                        v.wait_ge(SEM['t'], m_scan_t)
                    v.memset(acch_v, 0.0)
                    i = v.memset(runA_sb, 1.0)
                    inc(i, 'v', 1)

                for pi, p in enumerate(range(7, -1, -1)):
                    par = pi % 2
                    rsem = 'r0' if par == 0 else 'r1'

                    @emit('sync')
                    def _(sy, hcs=hcs, p=p, par=par, rsem=rsem, m_cc=m_cc,
                          m_rbw=m_rb_war[par], m_scan_v=m_scan_v):
                        sy.wait_ge(SEM['cc'], m_cc)
                        if m_rbw:
                            sy.wait_ge(SEM['v'], m_rbw)
                        else:
                            sy.wait_ge(SEM['v'], m_scan_v)
                        for q in range(8):
                            i = sy.dma_start(rbh_v[par][:, q, :],
                                             comm_rv[p][:, hcs * 8 + q, :])
                            inc(i, rsem, 16)
                    m_dma = CNT[rsem]

                    @emit('vector')
                    def _(v, p=p, par=par, rsem=rsem, m_dma=m_dma, m_prev=CNT['v']):
                        v.wait_ge(SEM[rsem], m_dma)
                        v.wait_ge(SEM['v'], m_prev)
                        i = v.tensor_scalar(escA_sb, runA_sb, mmask_sb[:, p:p + 1],
                                            None, ALU.mult)
                        mk = inc(i, 'v', 1)
                        v.wait_ge(SEM['v'], mk)
                        for q in range(8):
                            i = v.scalar_tensor_tensor(acch_v[:, q, :],
                                                       rbh_v[par][:, q, 0:256],
                                                       escA_sb[:, q:q + 1],
                                                       acch_v[:, q, :],
                                                       ALU.mult, ALU.add)
                        mk = inc(i, 'v', 1)
                        v.wait_ge(SEM['v'], mk)
                        i = v.tensor_scalar(escA_sb, rbh_v[par][:, :, 256],
                                            mmask_sb[:, p:p + 1],
                                            ummask_sb[:, p:p + 1], ALU.mult, ALU.add)
                        mk = inc(i, 'v', 1)
                        v.wait_ge(SEM['v'], mk)
                        i = v.tensor_tensor(runA_sb, runA_sb, escA_sb, ALU.mult)
                        inc(i, 'v', 1)
                    m_rb_war[par] = CNT['v']

                @emit('vector')
                def _(v, hcs=hcs):
                    i = v.tensor_copy(S0_v[:, hcs * 8:hcs * 8 + 8, :], acch_v)
                    inc(i, 'v', 1)
                m_s0h.append(CNT['v'])

            @emit('scalar')
            def _(sc, m_scan_v=m_scan_v, m_dall=m_dall):
                sc.wait_ge(SEM['v'], m_scan_v)
                sc.wait_ge(SEM['s'], m_dall)
                i = sc.activation(ebm_v[:, 0:256], Ball_sb, AF.Exp, bias=0.0, scale=R16)
                inc(i, 's', 1)
            m_ebB = CNT['s']
            eb2 = ebm_v[:, 0:256].rearrange("p (h c d) -> p h c d", h=4, c=16)

            qs2 = [qs16_v, A.at(129792, BF16, [128, 256])]
            m_qsf = [m_scan_t, m_scan_t]
            m_o3ev = [m_scan_v, m_scan_v]

            def emit_pb(h):
                for ci in range(16):
                    par = ci % 2
                    b0 = 64 * par

                    @emit('vector')
                    def _(v, ci=ci, h=h, par=par, m_ebB=m_ebB, m_qsfp=m_qsf[par]):
                        v.wait_ge(SEM['s'], m_ebB)
                        if m_qsfp:
                            v.wait_ge(SEM['t'], m_qsfp)
                        for dc in range(4):
                            i = v.tensor_scalar(
                                qs2[par][:, dc * 64:(dc + 1) * 64],
                                qe_v[:, 4 * h + dc, ci * 64:ci * 64 + 64],
                                eb2[:, h, ci, dc:dc + 1], None, ALU.mult)
                        inc(i, 'v', 1)
                    m_qs = CNT['v']

                    @emit('tensor')
                    def _(t, ci=ci, h=h, par=par, b0=b0, m_qs=m_qs,
                          m_oev2=m_o3ev[par], m_s0=m_s0h[h // 2]):
                        t.wait_ge(SEM['v'], m_qs)
                        t.wait_ge(SEM['v'], m_s0)
                        t.wait_ge(SEM['v'], m_oev2)
                        for dc in range(4):
                            i = t.matmul(pC[b0:b0 + 64, par * 512:par * 512 + 256],
                                         qs2[par][:, dc * 64:(dc + 1) * 64],
                                         S0_v[:, 4 * h + dc, :],
                                         start=(dc == 0), stop=(dc == 3),
                                         tile_position=(0, b0))
                        inc(i, 't', 1)
                    m_qsf[par] = CNT['t']
                    m_o3 = CNT['t']

                    @emit('vector')
                    def _(v, ci=ci, h=h, par=par, b0=b0, m_o3=m_o3):
                        v.wait_ge(SEM['t'], m_o3)
                        sl_ = oraw_v[b0:b0 + 64, ci // 2, h * 256:(h + 1) * 256]
                        i = v.tensor_tensor(sl_, sl_,
                                            pC[b0:b0 + 64, par * 512:par * 512 + 256],
                                            ALU.add)
                        inc(i, 'v', 1)
                    m_o3ev[par] = CNT['v']

            for hcs in range(2):
                emit_comb_half(hcs)
                emit_pb(2 * hcs)
                emit_pb(2 * hcs + 1)
            m_comb = CNT['v']
            m_qs_free_t = max(m_qsf)
            m_phaseB = CNT['v']

            if ATTN_PH < 7:
                return
            sumsq_v = A.at(138240, F32, [128, 32])
            rstd_v = A.at(138496, F32, [128, 32])

            @emit('scalar')
            def _(sc, m_phaseB=m_phaseB, m_gproj=m_gproj):
                sc.wait_ge(SEM['v'], max(m_phaseB, m_gproj))
                for j in range(8):
                    for h in range(NH):
                        idx = j * 4 + h
                        sc.activation(osq_v, oraw_v[:, j, h * 256:(h + 1) * 256],
                                      AF.Square, bias=0.0, scale=1.0,
                                      accum_out=sumsq_v[:, idx:idx + 1])
                sc.activation(rstd_v, sumsq_v, AF.Ln, bias=EPS, scale=1.0 / DV)
                i = sc.activation(rstd_v, rstd_v, AF.Exp, bias=0.0, scale=-0.5)
                inc(i, 's', 1)
            m_rstd = CNT['s']

            sil_scr = [A.at(122880, F32, [128, 1024]), A.at(130816, F32, [128, 1024])]
            m_vj = [0] * 8
            for j in range(8):
                @emit('scalar')
                def _(sc, j=j, m_rstd=m_rstd, m_prevv=(m_vj[j - 2] if j >= 2 else 0)):
                    if m_prevv:
                        sc.wait_ge(SEM['v'], m_prevv)
                    i = sc.activation(sil_scr[j % 2], g_tm_v[:, j, :], AF.Silu,
                                      bias=0.0, scale=1.0)
                    inc(i, 's', 1)
                m_silj = CNT['s']

                for h in range(NH):
                    idx = j * 4 + h
                    osl = oraw_v[:, j, h * 256:(h + 1) * 256]

                    @emit('vector')
                    def _(v, osl=osl, j=j, h=h, idx=idx, m_silj=m_silj,
                          m_rstd=m_rstd):
                        if h == 0:
                            v.wait_ge(SEM['s'], m_silj)
                        if idx == 0:
                            v.wait_ge(SEM['s'], m_rstd)
                        v.tensor_scalar(osq_v, osl, rstd_v[:, idx:idx + 1],
                                        None, ALU.mult)
                        v.tensor_tensor(osq_v, osq_v, wnorm_sb, ALU.mult)
                        i = v.tensor_tensor(osl, osq_v,
                                            sil_scr[j % 2][:, h * 256:(h + 1) * 256],
                                            ALU.mult)
                        inc(i, 'v', 1)
                m_vj[j] = CNT['v']
            m_gated = CNT['v']

            if ATTN_PH < 75:
                return
            m_ogtr = [0, 0]
            m_og_ser = [0]
            for tg in range(8):
                for vg in range(2):
                    @emit('tensor')
                    def _(t, tg=tg, vg=vg, m_gated=m_gated, m_ptf=m_og_ser[0]):
                        t.wait_ge(SEM['v'], m_gated)
                        if m_ptf:
                            t.wait_ge(SEM['v'], m_ptf)
                        for j in range(4):
                            vc = vg * 4 + j
                            i = t.transpose(
                                pT[:, vg * 512 + j * 128:vg * 512 + (j + 1) * 128],
                                oraw_v[:, tg, vc * 128:(vc + 1) * 128], ident_sb)
                        inc(i, 't', 1)
                    m_tr = CNT['t']

                    @emit('vector')
                    def _(v, tg=tg, vg=vg, m_tr=m_tr):
                        v.wait_ge(SEM['t'], m_tr)
                        i = v.tensor_copy(
                            og_v[:, vg * 4:vg * 4 + 4, tg * 128:(tg + 1) * 128],
                            pT.rearrange("p (j t) -> p j t", j=8)[:, vg * 4:vg * 4 + 4, :])
                        inc(i, 'v', 1)
                    m_ogtr[vg] = CNT['v']
                    m_og_ser[0] = CNT['v']
            m_og = CNT['v']

            if ATTN_PH < 8:
                return
            @emit('sync')
            def _(sy, m_phaseB=m_phaseB, m_qsf=m_qs_free_t):
                sy.wait_ge(SEM['v'], m_phaseB)
                sy.wait_ge(SEM['t'], m_qsf)
                i = sy.dma_start(A.at(0, F32, [128, 16384]), xspill_d)
                inc(i, 'sq', 16)
            m_reload = CNT['sq']

            for oc in range(16):
                tile, twq, slot = ring_tile(dr["wo_l"][oc], width=1024)
                for hf in range(2):
                    sl = (oc * 2 + hf) % 4

                    @emit('tensor')
                    def _(t, oc=oc, hf=hf, sl=sl, tile=tile, twq=twq, m_og=m_og,
                          fm=ps_free[sl], fsem=ps_free_sem[sl]):
                        t.wait_ge(SEM[twq[0]], twq[1])
                        t.wait_ge(SEM['v'], m_og)
                        if fm:
                            t.wait_ge(SEM[fsem], fm)
                        for kv in range(8):
                            i = t.matmul(psl[sl], tile[:, kv * 128:(kv + 1) * 128],
                                         og_v[:, kv, hf * 512:hf * 512 + 512],
                                         start=(kv == 0), stop=(kv == 7))
                        inc(i, 't', 1)
                    ring_free[slot] = CNT['t']
                    m_mm = CNT['t']

                    @emit('vector')
                    def _(v, oc=oc, hf=hf, sl=sl, m_mm=m_mm, m_reload=m_reload):
                        v.wait_ge(SEM['t'], m_mm)
                        v.wait_ge(SEM['sq'], m_reload)
                        xs = x_v[:, oc, hf * 512:hf * 512 + 512]
                        i = v.tensor_tensor(xs, xs, psl[sl], ALU.add)
                        inc(i, 'v', 1)
                    ps_free[sl] = CNT['v']
                    ps_free_sem[sl] = 'v'

        hm1 = emit_ln(first=True)
        emit_ffn(hm1)

        if STAGE >= 2:
            emit_ln()
            emit_attention()
        if STAGE >= 3:
            hm3 = emit_ln()
            m_dn_final = emit_ffn(hm3)

        if DEBUG:
            bar = dict(CNT)

            @emit('gpsimd')
            def _(gg):
                wait_marks(gg, {k: bar[k] for k in ('t', 'v', 's')})
                inc(gg.dma_start(dbg_h, h_sb), 'wq', 16)
                inc(gg.dma_start(dbg_soft, softT_sb), 'wq', 16)
                inc(gg.dma_start(dbg_scal, scalT_sb), 'wq', 16)
                inc(gg.dma_start(dbg_scb, scb_v), 'wq', 16)
                inc(gg.dma_start(dbg_a, a_v.rearrange("p a b -> p (a b)")), 'wq', 16)
                i = gg.dma_start(dbg_xg, xg_v.rearrange("p a b -> p (a b)"))
                inc(i, 'wq', 16)
                gg.wait_ge(SEM['wq'], CNT['wq'])

        bar = dict(CNT)

        @emit('sync')
        def _(sy):
            ov = out_d.rearrange("(c p) t -> p c t", p=128)
            for hc in range(NCH):
                sy.wait_ge(SEM['v'], m_dn_final[hc])
                i = sy.dma_start(ov[:, hc, :], x_v[:, hc, :])
                inc(i, 'sq', 16)
            wait_marks(sy, {k: bar[k] for k in ('t', 'v', 's', 'p')})
            sy.wait_ge(SEM['sq'], CNT['sq'])

    return nc


_CACHE = {}


def kernel(**inputs):
    d = _prep_host(inputs)
    f = np.float32
    x = np.asarray(inputs["L0_attn_x"], f).reshape(8192, 2048)
    shards = [np.ascontiguousarray(x[c * T:(c + 1) * T].T) for c in range(8)]

    nc = build()

    shared = {k: v for k, v in d.items()}
    in_maps = []
    for c in range(8):
        g0 = 0 if c < 4 else 4
        mm = np.zeros((128, 8), f)
        for p in range(8):
            if g0 <= p < c:
                mm[:, p] = 1.0
        m = dict(shared)
        m["xT"] = shards[c]
        m["fconst"] = np.tile(np.array([[EPS, 0.0, 0.0, 0.0]], f), (128, 1))
        m["mmask"] = mm
        m["ummask"] = 1.0 - mm
        in_maps.append(m)

    res = run_bass_kernel_spmd(nc, in_maps, core_ids=list(range(8)))
    outs = [np.asarray(res.results[c]["out"], f).T for c in range(8)]
    return np.concatenate(outs, axis=0).reshape(2, 4096, 2048)




# version 16 (final)
# speedup vs baseline: 1.0091x; 1.0091x over previous
import numpy as np
import ml_dtypes
import concourse.bass as bass
import concourse.mybir as mybir
from concourse.bass_utils import run_bass_kernel_spmd

F32 = mybir.dt.float32
BF16 = mybir.dt.bfloat16
I32 = mybir.dt.int32
AF = mybir.ActivationFunctionType
ALU = mybir.AluOpType
AX = mybir.AxisListType

STAGE = 3
ATTN_PH = 99
DEBUG = False

D, T, NCH = 2048, 1024, 16
G, IC = 8, 11
NH, DK, DV, CH, NCK = 4, 512, 256, 64, 16
EPS = 1e-5
QSCALE = float(DK) ** -0.5
R16 = 1.0 / 16.0
RING = 3

BF = ml_dtypes.bfloat16



def _pack_ws(w, kc, occ):
    return np.ascontiguousarray(
        w.reshape(kc, 128, occ, 128).transpose(2, 1, 0, 3).reshape(occ, 128, kc * 128))


def _prep_host(inputs):
    f = np.float32
    d = {}
    gp = np.asarray(inputs["L0_ffn_gate_proj_w"], f)
    up = np.asarray(inputs["L0_ffn_up_proj_w"], f)
    dn = np.asarray(inputs["L0_ffn_down_proj_w"], f)
    d["gw_l"] = np.ascontiguousarray(
        gp.reshape(8, IC, 128, 16, 128).transpose(0, 1, 4, 3, 2).reshape(8, IC, 128, 2048))
    d["uw_l"] = np.ascontiguousarray(
        up.reshape(8, IC, 128, 16, 128).transpose(0, 1, 4, 3, 2).reshape(8, IC, 128, 2048))
    d["dw_l"] = np.ascontiguousarray(
        dn.reshape(8, 16, 128, IC, 128).transpose(0, 1, 4, 3, 2).reshape(8, 16, 128, 1408))
    mu = np.asarray(inputs["L0_ffn_mask_up_w"], f)
    mup = np.zeros((8, 64, 2048), f)
    for g in range(8):
        mup[g, 8 * g:8 * g + 8, :] = mu[g].T
    d["mu_pair"] = mup
    d["gate_w"] = np.ascontiguousarray(np.asarray(inputs["L0_ffn_gate_w"], f))
    d["inv_i"] = np.asarray(inputs["L0_ffn_inv_map"]).astype(np.int32).reshape(1, 64)
    d["wq_l"] = _pack_ws(np.asarray(inputs["L0_attn_Wq"], f), 16, 16)
    d["wk_l"] = _pack_ws(np.asarray(inputs["L0_attn_Wk"], f), 16, 16)
    d["wv_l"] = _pack_ws(np.asarray(inputs["L0_attn_Wv"], f), 16, 8)
    d["wg_l"] = _pack_ws(np.asarray(inputs["L0_attn_Wg"], f), 16, 8)
    d["wo_l"] = _pack_ws(np.asarray(inputs["L0_attn_Wo"], f), 8, 16)
    d["wgk1"] = np.ascontiguousarray(np.asarray(inputs["L0_attn_Wgk1"], f))
    d["wgk2"] = np.ascontiguousarray(np.asarray(inputs["L0_attn_Wgk2"], f))
    d["bgk2_t"] = np.ascontiguousarray(np.asarray(inputs["L0_attn_bgk2"], f).reshape(16, 128).T)
    d["wnorm_t"] = np.ascontiguousarray(
        np.tile(np.asarray(inputs["L0_attn_Wnorm"], f)[None, :], (128, 1)))
    d["ident"] = np.eye(128, dtype=BF)
    d["ones128"] = np.ones((128, 128), dtype=BF)
    d["maskT"] = np.tile(np.triu(np.ones((64, 64), np.float32)), (2, 1)).astype(BF)
    d["iota64"] = np.arange(64, dtype=f).reshape(64, 1)
    oh = np.zeros((8, 8, 128), f)
    for g in range(8):
        oh[g, g, :] = 0.5
    d["onehot_half"] = oh.astype(BF)
    return d



class Arena:
    def __init__(self, slab_ap, nbytes):
        self.slab = slab_ap
        self.nbytes = nbytes

    def at(self, off, dtype, shape):
        esz = 2 if dtype == BF16 else 4
        nfree = int(np.prod(shape[1:]))
        assert off % 4 == 0 and off + nfree * esz <= self.nbytes, (off, nfree * esz)
        e0 = off // 4
        v = self.slab[:, e0:e0 + (nfree * esz + 3) // 4]
        if dtype == BF16:
            v = v.bitcast(BF16)
        v = v[:, :nfree]
        if shape[0] != 128:
            v = v[0:shape[0], :]
        if len(shape) > 2:
            dims = shape[1:]
            names = " ".join(f"a{i}" for i in range(len(dims)))
            v = v.rearrange(f"p ({names}) -> p {names}",
                            **{f"a{i}": dims[i] for i in range(len(dims))})
        return v



def build():
    nc = bass.Bass("TRN2", target_bir_lowering=False, num_devices=8)

    dr = {}
    def di(name, shape, dt=F32):
        dr[name] = nc.dram_tensor(name, list(shape), dt, kind="ExternalInput").ap()
    di("xT", (2048, 1024))
    di("gw_l", (8, IC, 128, 2048)); di("uw_l", (8, IC, 128, 2048)); di("dw_l", (8, 16, 128, 1408))
    di("mu_pair", (8, 64, 2048))
    di("gate_w", (64, 2048)); di("inv_i", (1, 64), I32)
    di("wq_l", (16, 128, 2048)); di("wk_l", (16, 128, 2048))
    di("wv_l", (8, 128, 2048)); di("wg_l", (8, 128, 2048)); di("wo_l", (16, 128, 1024))
    di("wgk1", (2048, 16)); di("wgk2", (16, 2048))
    di("bgk2_t", (128, 16)); di("wnorm_t", (128, 256))
    di("ident", (128, 128), BF16); di("ones128", (128, 128), BF16)
    di("maskT", (128, 64), BF16); di("iota64", (64, 1))
    di("onehot_half", (8, 8, 128), BF16)
    di("mmask", (128, 8)); di("ummask", (128, 8)); di("fconst", (128, 4))
    out_d = nc.dram_tensor("out", [2048, 1024], F32, kind="ExternalOutput").ap()
    if DEBUG:
        dbg_h = nc.dram_tensor("dbg_h", [128, NCH * 1024], F32, kind="ExternalOutput").ap()
        dbg_soft = nc.dram_tensor("dbg_soft", [64, 1024], F32, kind="ExternalOutput").ap()
        dbg_scal = nc.dram_tensor("dbg_scal", [8, 1024], F32, kind="ExternalOutput").ap()
        dbg_scb = nc.dram_tensor("dbg_scb", [128, 1024], F32, kind="ExternalOutput").ap()
        dbg_a = nc.dram_tensor("dbg_a", [128, IC * 1024], F32, kind="ExternalOutput").ap()
        dbg_xg = nc.dram_tensor("dbg_xg", [128, NCH * 1024], F32, kind="ExternalOutput").ap()
    xspill_d = nc.dram_tensor("xspill", [128, 16384], F32).ap()
    comm_in = nc.dram_tensor("comm_in", [2048, 257], BF16).ap()
    comm_out = nc.dram_tensor("comm_out", [8 * 2048, 257], BF16, addr_space="Shared").ap()

    h_sb = nc.alloc_sbuf_tensor("h_sb", [128, NCH * 1024], BF16).ap()
    h3 = h_sb.rearrange("p (c t) -> p c t", c=NCH)
    bcA = nc.alloc_sbuf_tensor("bcA", [128, 1024], F32).ap()
    bcB = nc.alloc_sbuf_tensor("bcB", [128, 1024], F32).ap()
    ring = nc.alloc_sbuf_tensor("ring", [128, RING * 2048], BF16).ap()
    ident_sb = nc.alloc_sbuf_tensor("ident_sb", [128, 128], BF16).ap()
    ones_sb = nc.alloc_sbuf_tensor("ones_sb", [128, 128], BF16).ap()
    maskT_sb = nc.alloc_sbuf_tensor("maskT_sb", [128, 64], BF16).ap()
    iota_sb = nc.alloc_sbuf_tensor("iota_sb", [64, 1], F32).ap()
    oh_sb = nc.alloc_sbuf_tensor("oh_sb", [8, 8 * 128], BF16).ap()
    gwperm_sb = nc.alloc_sbuf_tensor("gwperm_sb", [128, 1024], BF16).ap()
    inv_sb = nc.alloc_sbuf_tensor("inv_sb", [1, 64], I32).ap()
    invf_sb = nc.alloc_sbuf_tensor("invf_sb", [1, 64], F32).ap()
    invb_sb = nc.alloc_sbuf_tensor("invb_sb", [1, 64], BF16).ap()
    P_sb = nc.alloc_sbuf_tensor("P_sb", [64, 64], BF16).ap()
    wgk1_sb = nc.alloc_sbuf_tensor("wgk1_sb", [128, 256], BF16).ap()
    bgk2_sb = nc.alloc_sbuf_tensor("bgk2_sb", [128, 16], F32).ap()
    wnorm_sb = nc.alloc_sbuf_tensor("wnorm_sb", [128, 256], F32).ap()
    mmask_sb = nc.alloc_sbuf_tensor("mmask_sb", [128, 8], F32).ap()
    ummask_sb = nc.alloc_sbuf_tensor("ummask_sb", [128, 8], F32).ap()
    zz64 = nc.alloc_sbuf_tensor("zz64", [128, 64], F32).ap()
    softT_sb = nc.alloc_sbuf_tensor("softT_sb", [64, 1024], BF16).ap()
    scalT_sb = nc.alloc_sbuf_tensor("scalT_sb", [8, 1024], BF16).ap()
    Bsum_sb = nc.alloc_sbuf_tensor("Bsum_sb", [128, 16], F32).ap()
    Ball_sb = nc.alloc_sbuf_tensor("Ball_sb", [128, 256], F32).ap()
    ebl_sb = nc.alloc_sbuf_tensor("ebl_sb", [128, 16], F32).ap()
    dall_sb = nc.alloc_sbuf_tensor("dall_sb", [128, 16], F32).ap()
    run_sb = nc.alloc_sbuf_tensor("run_sb", [128, 1], F32).ap()
    esc_sb = nc.alloc_sbuf_tensor("esc_sb", [128, 1], F32).ap()
    runA_sb = nc.alloc_sbuf_tensor("runA_sb", [128, 8], F32).ap()
    escA_sb = nc.alloc_sbuf_tensor("escA_sb", [128, 8], F32).ap()
    gnm = nc.alloc_sbuf_tensor("gnm", [128, 4], F32).ap()
    gt8 = nc.alloc_sbuf_tensor("gt8", [128, 8], F32).ap()
    gel = nc.alloc_sbuf_tensor("gel", [128, 64], F32).ap()
    gpr = nc.alloc_sbuf_tensor("gpr", [128, 64], F32).ap()
    gfl = nc.alloc_sbuf_tensor("gfl", [128, 64], F32).ap()
    gm0 = nc.alloc_sbuf_tensor("gm0", [128, 64], F32).ap()
    gden = nc.alloc_sbuf_tensor("gden", [128, 8], F32).ap()
    grd = nc.alloc_sbuf_tensor("grd", [128, 8], F32).ap()
    gs16 = nc.alloc_sbuf_tensor("gs16", [128, 64], BF16).ap()
    gsc = nc.alloc_sbuf_tensor("gsc", [128, 8], F32).ap()
    gsc16 = nc.alloc_sbuf_tensor("gsc16", [128, 8], BF16).ap()
    mu_sb = nc.alloc_sbuf_tensor("mu_sb", [64, 2048], BF16).ap()
    fc_sb = nc.alloc_sbuf_tensor("fc_sb", [128, 4], F32).ap()
    nc.const_aps.register(F32, EPS, fc_sb[:, 0:1])

    SLABW = 34816
    slab_t = nc.alloc_sbuf_tensor("slab", [128, SLABW], F32)
    A = Arena(slab_t.ap(), SLABW * 4)

    x_v = A.at(0, F32, [128, NCH, 1024])
    qe_v = A.at(0, BF16, [128, 16, 1024])
    S32_v = A.at(32768, F32, [128, 16, 256])
    S16_v = A.at(49152, BF16, [128, 16, 256])
    bb_v = A.at(57344, F32, [128, 1024])
    ebm_v = A.at(61440, F32, [128, 1024])
    rb_v = A.at(32768, F32, [128, 8, 257])
    xg_v = A.at(65536, BF16, [128, NCH, 1024])
    a_v = A.at(98304, BF16, [128, IC, 1024])
    tsil_v = A.at(120832, F32, [128, 1024])
    scb_v = A.at(124928, F32, [128, 1024])
    lnx_v = [A.at(129024 + i * 2048, BF16, [128, 1024]) for i in range(2)]
    lnq_v = [A.at(133120 + i * 2048, BF16, [128, 1024]) for i in range(2)]
    k_v = A.at(65536, BF16, [128, 16, 512])
    gk_v = A.at(81920, BF16, [128, 16, 512])
    v_v = A.at(98304, BF16, [128, 4, 1024])
    oraw_v = A.at(106496, BF16, [128, 8, 1024])
    kl_tm_v = A.at(122880, BF16, [128, 16, 128])
    kd_v = A.at(126976, BF16, [128, 1024])
    at16_v = A.at(129024, BF16, [128, 64])
    qs16_v = A.at(129280, BF16, [128, 256])
    gkt_v = A.at(129792, BF16, [16, 512])
    sgt_v = A.at(130816, F32, [128, 512])
    sg2_v = A.at(132864, F32, [128, 512])
    gpp_v = A.at(134912, BF16, [128, 1024])
    osq_v = A.at(136960, F32, [128, 256])
    orst_v = A.at(138240, F32, [128, 2])
    stg_v = A.at(65536, BF16, [128, 8, 257])
    S0_v = A.at(81984, BF16, [128, 16, 256])
    acc_v = A.at(90176, F32, [128, 256])
    g_tm_v = A.at(65536, BF16, [128, 8, 1024])
    og_v = A.at(81984, BF16, [128, 8, 1024])

    pA = nc.alloc_psum_tensor("pA", [128, 1024], F32).ap()
    pB = nc.alloc_psum_tensor("pB", [128, 1024], F32).ap()
    pC = nc.alloc_psum_tensor("pC", [128, 1024], F32).ap()
    pD = nc.alloc_psum_tensor("pD", [128, 512], F32).ap()
    pT = nc.alloc_psum_tensor("pT", [128, 1024], BF16).ap()

    SEM, CNT = {}, {}
    for s in ["t", "v", "s", "p", "wq", "sq", "cc", "r0", "r1",
              "w0", "w1", "w2", "mu"]:
        SEM[s] = nc.alloc_semaphore("sem_" + s)
        CNT[s] = 0

    def inc(ins, s, n=1):
        CNT[s] += n
        ins.then_inc(SEM[s], n)
        return CNT[s]

    wctr = [0]
    ring_free = [0] * RING

    with nc.Block() as block:
        def emit(eng):
            def deco(f):
                getattr(block, eng)(f)
            return deco

        def wait_marks(e, marks):
            for s, m in marks.items():
                if m > 0:
                    e.wait_ge(SEM[s], m)

        @emit('sync')
        def _(sy):
            xr = dr["xT"].rearrange("(c p) t -> p c t", p=128)
            inc(sy.dma_start(x_v[:, 0:8, :], xr[:, 0:8, :]), 'r0', 16)
            inc(sy.dma_start(x_v[:, 8:16, :], xr[:, 8:16, :]), 'r1', 16)
            srcs = [(ident_sb, dr["ident"]), (ones_sb, dr["ones128"]),
                    (maskT_sb, dr["maskT"]), (iota_sb, dr["iota64"]),
                    (oh_sb.rearrange("s (g i) -> s g i", g=8),
                     dr["onehot_half"].rearrange("g s i -> s g i")),
                    (inv_sb, dr["inv_i"]), (bgk2_sb, dr["bgk2_t"]),
                    (wnorm_sb, dr["wnorm_t"]), (mmask_sb, dr["mmask"]),
                    (ummask_sb, dr["ummask"]), (fc_sb, dr["fconst"])]
            for dst, src in srcs:
                inc(sy.dma_start(dst, src), 'sq', 16)
        m_setup_sq = CNT['sq']

        @emit('gpsimd')
        def _(g):
            inc(g.dma_start(ring[0:64, 0:2048], dr["gate_w"]), 'wq', 16)
            i = g.dma_start(wgk1_sb.rearrange("p (k c) -> p k c", k=16),
                            dr["wgk1"].rearrange("(k p) c -> p k c", p=128))
            inc(i, 'wq', 16)
            g.memset(zz64, 0.0)
            i = g.memset(run_sb, 1.0)
            inc(i, 'p', 1)
        m_setup_wq = CNT['wq']

        @emit('vector')
        def _(v):
            v.wait_ge(SEM['sq'], m_setup_sq)
            v.tensor_copy(invf_sb, inv_sb)
            i = v.tensor_copy(invb_sb, invf_sb)
            inc(i, 'v', 1)
        m1 = CNT['v']

        @emit('tensor')
        def _(t, m1=m1):
            t.wait_ge(SEM['v'], m1)
            t.wait_ge(SEM['wq'], m_setup_wq)
            i = t.matmul(pD[0:64, 0:64], ones_sb[0:1, 0:64], invb_sb, start=True, stop=True)
            inc(i, 't', 1)
        m2 = CNT['t']

        @emit('vector')
        def _(v, m2=m2):
            v.wait_ge(SEM['t'], m2)
            i = v.tensor_scalar(P_sb, pD[0:64, 0:64], iota_sb[:, 0:1], None, ALU.is_equal)
            inc(i, 'v', 1)

        for k in range(16):
            mv = CNT['v']

            @emit('tensor')
            def _(t, k=k, mv=mv):
                t.wait_ge(SEM['v'], mv)
                i = t.matmul(pD[:, 0:64], ring[0:64, k * 128:(k + 1) * 128], P_sb,
                             start=True, stop=True)
                inc(i, 't', 1)
            mt = CNT['t']

            @emit('vector')
            def _(v, k=k, mt=mt):
                v.wait_ge(SEM['t'], mt)
                i = v.tensor_copy(gwperm_sb[:, k * 64:(k + 1) * 64], pD[:, 0:64])
                inc(i, 'v', 1)
        ring_free[0] = CNT['t']

        RSEMS = ["w0", "w1", "w2"]

        def ring_tile(src_ap, width=2048, rows=128):
            idx = wctr[0]
            wctr[0] += 1
            slot = idx % RING
            fm = ring_free[slot]
            sname = RSEMS[slot]

            @emit('gpsimd')
            def _(g, slot=slot, src_ap=src_ap, fm=fm, width=width, rows=rows,
                  sname=sname):
                if fm > 0:
                    g.wait_ge(SEM['t'], fm)
                i = g.dma_start(ring[0:rows, slot * 2048: slot * 2048 + width], src_ap)
                inc(i, sname, 16)
            return (ring[0:rows, slot * 2048: slot * 2048 + width],
                    (sname, CNT[sname]), slot)

        def emit_ln(first=False):
            bar = dict(CNT)
            tmarks = []
            for c in range(NCH):
                pp = c % 2

                @emit('vector')
                def _(v, c=c, pp=pp, tmarks=list(tmarks), bar=bar, first=first):
                    if c == 0:
                        wait_marks(v, {k: bar[k] for k in ('t', 's', 'p', 'sq')})
                    if first and c == 0:
                        v.wait_ge(SEM['r0'], 16)
                    if first and c == 8:
                        v.wait_ge(SEM['r1'], 16)
                    if c >= 2:
                        v.wait_ge(SEM['t'], tmarks[c - 2])
                    v.tensor_copy(lnx_v[pp], x_v[:, c, :])
                    i = v.tensor_tensor(lnq_v[pp], lnx_v[pp], lnx_v[pp], ALU.mult)
                    inc(i, 'v', 1)
                mv = CNT['v']

                @emit('tensor')
                def _(t, c=c, pp=pp, mv=mv, bar=bar):
                    if c == 0:
                        wait_marks(t, {k: bar[k] for k in ('s',)})
                    t.wait_ge(SEM['v'], mv)
                    st, sp = (c == 0), (c == NCH - 1)
                    t.matmul(pA[:, 0:512], ones_sb, lnx_v[pp][:, 0:512], start=st, stop=sp)
                    t.matmul(pA[:, 512:1024], ones_sb, lnx_v[pp][:, 512:1024], start=st, stop=sp)
                    t.matmul(pB[:, 0:512], ones_sb, lnq_v[pp][:, 0:512], start=st, stop=sp)
                    i = t.matmul(pB[:, 512:1024], ones_sb, lnq_v[pp][:, 512:1024], start=st, stop=sp)
                    inc(i, 't', 1)
                tmarks.append(CNT['t'])
            m_sums = CNT['t']

            @emit('vector')
            def _(v, m_sums=m_sums):
                v.wait_ge(SEM['t'], m_sums)
                v.tensor_scalar(bcA, pA, 1.0 / D, None, ALU.mult)
                v.tensor_tensor(scb_v, bcA, bcA, ALU.mult)
                v.tensor_scalar(tsil_v, pB, 1.0 / D, None, ALU.mult)
                i = v.tensor_tensor(tsil_v, tsil_v, scb_v, ALU.subtract)
                inc(i, 'v', 1)
            m_var = CNT['v']

            @emit('scalar')
            def _(s, m_var=m_var):
                s.wait_ge(SEM['v'], m_var)
                i = s.activation(tsil_v, tsil_v, AF.Sqrt, bias=EPS, scale=1.0)
                inc(i, 's', 1)
            m_sd = CNT['s']

            @emit('vector')
            def _(v, m_sd=m_sd):
                v.wait_ge(SEM['s'], m_sd)
                i = v.reciprocal(bcB, tsil_v)
                mk = inc(i, 'v', 1)
                v.wait_ge(SEM['v'], mk)
                i = v.tensor_tensor(bcA, bcA, bcB, ALU.mult)
                inc(i, 'v', 1)
            m_rows = CNT['v']

            halfmarks = []
            for half in range(2):
                hs2 = slice(half * 512, (half + 1) * 512)
                for c in range(NCH):
                    if c % 4 != 3:
                        @emit('vector')
                        def _(v, c=c, hs2=hs2, m_rows=m_rows):
                            v.wait_ge(SEM['v'], m_rows)
                            v.tensor_tensor(scb_v[:, hs2], x_v[:, c, hs2],
                                            bcB[:, hs2], ALU.mult)
                            i = v.tensor_tensor(h3[:, c, hs2], scb_v[:, hs2],
                                                bcA[:, hs2], ALU.subtract)
                            inc(i, 'v', 1)
                    else:
                        @emit('gpsimd')
                        def _(g, c=c, hs2=hs2, m_rows=m_rows):
                            g.wait_ge(SEM['v'], m_rows)
                            g.tensor_tensor(tsil_v[:, hs2], x_v[:, c, hs2],
                                            bcB[:, hs2], ALU.mult)
                            i = g.tensor_tensor(h3[:, c, hs2], tsil_v[:, hs2],
                                                bcA[:, hs2], ALU.subtract)
                            inc(i, 'p', 1)
                halfmarks.append((CNT['v'], CNT['p']))
            return halfmarks

        def emit_ffn(halfmarks):
            hv, hp = halfmarks[1]
            for tl in range(8):
                m_prev_s = CNT['s']
                m_prev_v = CNT['v']
                hvt, hpt = halfmarks[0 if tl < 4 else 1]

                @emit('tensor')
                def _(t, tl=tl, hvt=hvt, hpt=hpt, m_prev_s=m_prev_s,
                      m_prev_v=m_prev_v):
                    t.wait_ge(SEM['v'], max(hvt, m_prev_v))
                    t.wait_ge(SEM['p'], hpt)
                    t.wait_ge(SEM['s'], m_prev_s)
                    for k in range(16):
                        i = t.matmul(pD[:, 0:64], h3[:, k, tl * 128:(tl + 1) * 128],
                                     gwperm_sb[:, k * 64:(k + 1) * 64],
                                     start=(k == 0), stop=(k == 15))
                    inc(i, 't', 1)
                m_sc = CNT['t']

                @emit('vector')
                def _(v, m_sc=m_sc):
                    v.wait_ge(SEM['t'], m_sc)
                    i = v.reduce_max(gnm[:, 0:1], pD[:, 0:64], AX.X, negate=True)
                    inc(i, 'v', 1)
                m_nmx = CNT['v']

                @emit('scalar')
                def _(s, m_nmx=m_nmx):
                    s.wait_ge(SEM['v'], m_nmx)
                    i = s.activation(gel, pD[:, 0:64], AF.Exp, bias=gnm[:, 0:1], scale=1.0)
                    inc(i, 's', 1)
                m_el = CNT['s']

                @emit('vector')
                def _(v, m_el=m_el):
                    v.wait_ge(SEM['s'], m_el)
                    i = v.reduce_sum(gnm[:, 1:2], gel, AX.X)
                    mk = inc(i, 'v', 1)
                    v.wait_ge(SEM['v'], mk)
                    i = v.reciprocal(gnm[:, 2:3], gnm[:, 1:2])
                    mk = inc(i, 'v', 1)
                    v.wait_ge(SEM['v'], mk)
                    v.tensor_scalar(gpr, gel, gnm[:, 2:3], None, ALU.mult)
                    i = v.max(gt8, gpr)
                    mk = inc(i, 'v', 1)
                    v.wait_ge(SEM['v'], mk)
                    v.tensor_scalar(gm0, gpr, gt8[:, 7:8], None, ALU.is_ge)
                    v.tensor_tensor(gfl, gpr, gm0, ALU.mult)
                    v.tensor_scalar(gm0, gfl, 0.0, None, ALU.is_equal)
                    i = v.scalar_tensor_tensor(gel, gm0, -1e9, gfl, ALU.mult, ALU.add)
                    inc(i, 'v', 1)
                m_flatm = CNT['v']

                @emit('scalar')
                def _(s, m_flatm=m_flatm):
                    s.wait_ge(SEM['v'], m_flatm)
                    i = s.activation(gel, gel, AF.Exp, bias=0.0, scale=1.0)
                    inc(i, 's', 1)
                m_ef = CNT['s']

                @emit('vector')
                def _(v, m_ef=m_ef):
                    v.wait_ge(SEM['s'], m_ef)
                    i = v.tensor_reduce(gden, gel.rearrange("p (g s) -> p g s", g=8),
                                        AX.X, ALU.add)
                    mk = inc(i, 'v', 1)
                    v.wait_ge(SEM['v'], mk)
                    i = v.tensor_scalar(gden, gden, 1e-12, None, ALU.add)
                    mk = inc(i, 'v', 1)
                    v.wait_ge(SEM['v'], mk)
                    i = v.reciprocal(grd, gden)
                    mk = inc(i, 'v', 1)
                    v.wait_ge(SEM['v'], mk)
                    for g in range(8):
                        v.tensor_scalar(gs16[:, 8 * g:8 * g + 8], gel[:, 8 * g:8 * g + 8],
                                        grd[:, g:g + 1], None, ALU.mult)
                    i = v.tensor_reduce(gsc, gfl.rearrange("p (g s) -> p g s", g=8),
                                        AX.X, ALU.add)
                    mk = inc(i, 'v', 1)
                    v.wait_ge(SEM['v'], mk)
                    i = v.tensor_copy(gsc16, gsc)
                    inc(i, 'v', 1)
                m_soft = CNT['v']

                @emit('tensor')
                def _(t, tl=tl, m_soft=m_soft):
                    t.wait_ge(SEM['v'], m_soft)
                    t.transpose(pT[0:64, 0:128], gs16, ident_sb)
                    i = t.transpose(pT[0:8, 128:256], gsc16, ident_sb)
                    inc(i, 't', 1)
                m_tr = CNT['t']

                @emit('vector')
                def _(v, tl=tl, m_tr=m_tr):
                    v.wait_ge(SEM['t'], m_tr)
                    v.tensor_copy(softT_sb[:, tl * 128:(tl + 1) * 128], pT[0:64, 0:128])
                    i = v.tensor_copy(scalT_sb[:, tl * 128:(tl + 1) * 128], pT[0:8, 128:256])
                    inc(i, 'v', 1)
            m_gating = CNT['v']

            m_xg_free_t = CNT['t']
            m_scb_free = CNT['v']
            m_dn = [0] * 16
            m_sil = [0] * IC
            m_a = [0] * IC
            mu_wq = 0
            for g in range(8):
                m_mu_reader = CNT['t']

                @emit('gpsimd')
                def _(gg, g=g, m_mu_reader=m_mu_reader):
                    gg.wait_ge(SEM['t'], m_mu_reader)
                    i = gg.dma_start(mu_sb, dr["mu_pair"][g])
                    inc(i, 'mu', 16)
                mu_wq = CNT['mu']

                @emit('tensor')
                def _(t, g=g, m_gating=m_gating, m_dnv=list(m_dn)):
                    t.wait_ge(SEM['v'], m_gating)
                    if m_dnv[14]:
                        t.wait_ge(SEM['v'], m_dnv[14])
                    if m_dnv[15]:
                        t.wait_ge(SEM['v'], m_dnv[15])
                    t.matmul(pA[:, 0:512], oh_sb[:, g * 128:(g + 1) * 128],
                             scalT_sb[:, 0:512], start=True, stop=True)
                    i = t.matmul(pA[:, 512:1024], oh_sb[:, g * 128:(g + 1) * 128],
                                 scalT_sb[:, 512:1024], start=True, stop=True)
                    inc(i, 't', 1)
                m_scbt = CNT['t']

                @emit('vector')
                def _(v, m_scbt=m_scbt, m_a10=m_a[IC - 1]):
                    v.wait_ge(SEM['t'], m_scbt)
                    if m_a10:
                        v.wait_ge(SEM['v'], m_a10)
                    i = v.tensor_copy(scb_v, pA)
                    inc(i, 'v', 1)
                m_scb = CNT['v']

                m_xg = [0] * NCH
                for c in range(NCH):
                    pX = pA if c % 2 == 0 else pB

                    @emit('tensor')
                    def _(t, g=g, c=c, pX=pX, mu_wq=mu_wq, m_scb=m_scb,
                          m_xgf=m_xg_free_t, m_xg=list(m_xg)):
                        t.wait_ge(SEM['mu'], mu_wq)
                        t.wait_ge(SEM['t'], m_xgf)
                        if c == 0:
                            t.wait_ge(SEM['v'], m_scb)
                        if c >= 2:
                            t.wait_ge(SEM['v'], m_xg[c - 2])
                        lhsT = mu_sb[:, c * 128:(c + 1) * 128]
                        t.matmul(pX[:, 0:512], lhsT, softT_sb[:, 0:512], start=True, stop=True)
                        i = t.matmul(pX[:, 512:1024], lhsT, softT_sb[:, 512:1024],
                                     start=True, stop=True)
                        inc(i, 't', 1)
                    m_dc = CNT['t']

                    @emit('vector')
                    def _(v, c=c, pX=pX, m_dc=m_dc):
                        v.wait_ge(SEM['t'], m_dc)
                        i = v.tensor_tensor(xg_v[:, c, :], h3[:, c, :], pX, ALU.add)
                        inc(i, 'v', 1)
                    m_xg[c] = CNT['v']
                m_xg_done = CNT['v']

                banks3 = [pA, pB, pC]
                bfree = {0: ('v', m_xg[14]), 1: ('v', m_xg[15]),
                         2: ('v', m_a[IC - 1])}
                for oc in range(IC):
                    gi = (2 * oc) % 3
                    ui = (2 * oc + 1) % 3
                    pG = banks3[gi]
                    pU = banks3[ui]
                    gw_tile, gw_wq, gw_slot = ring_tile(dr["gw_l"][g][oc])

                    @emit('tensor')
                    def _(t, g=g, oc=oc, pG=pG, gw_wq=gw_wq, m_xg_done=m_xg_done,
                          bf=bfree[gi]):
                        t.wait_ge(SEM[gw_wq[0]], gw_wq[1])
                        t.wait_ge(SEM['v'], m_xg_done)
                        if bf[1]:
                            t.wait_ge(SEM[bf[0]], bf[1])
                        for k in range(16):
                            t.matmul(pG[:, 0:512], gw_tile[:, k * 128:(k + 1) * 128],
                                     xg_v[:, k, 0:512], start=(k == 0), stop=(k == 15))
                            i = t.matmul(pG[:, 512:1024], gw_tile[:, k * 128:(k + 1) * 128],
                                         xg_v[:, k, 512:1024], start=(k == 0), stop=(k == 15))
                        inc(i, 't', 1)
                    ring_free[gw_slot] = CNT['t']
                    m_gt = CNT['t']

                    @emit('scalar')
                    def _(s, oc=oc, pG=pG, m_gt=m_gt):
                        s.wait_ge(SEM['t'], m_gt)
                        i = s.activation(tsil_v, pG, AF.Silu, bias=0.0, scale=1.0)
                        inc(i, 's', 1)
                    m_sil[oc] = CNT['s']
                    bfree[gi] = ('s', m_sil[oc])

                    uw_tile, uw_wq, uw_slot = ring_tile(dr["uw_l"][g][oc])

                    @emit('tensor')
                    def _(t, g=g, oc=oc, pU=pU, uw_wq=uw_wq, bf=bfree[ui],
                          m_xg_done=m_xg_done):
                        t.wait_ge(SEM[uw_wq[0]], uw_wq[1])
                        t.wait_ge(SEM['v'], m_xg_done)
                        if bf[1]:
                            t.wait_ge(SEM[bf[0]], bf[1])
                        for k in range(16):
                            t.matmul(pU[:, 0:512], uw_tile[:, k * 128:(k + 1) * 128],
                                     xg_v[:, k, 0:512], start=(k == 0), stop=(k == 15))
                            i = t.matmul(pU[:, 512:1024], uw_tile[:, k * 128:(k + 1) * 128],
                                         xg_v[:, k, 512:1024], start=(k == 0), stop=(k == 15))
                        inc(i, 't', 1)
                    ring_free[uw_slot] = CNT['t']
                    m_up = CNT['t']

                    @emit('vector')
                    def _(v, oc=oc, pU=pU, m_up=m_up, m_silo=m_sil[oc], m_scb=m_scb):
                        v.wait_ge(SEM['t'], m_up)
                        v.wait_ge(SEM['s'], m_silo)
                        v.wait_ge(SEM['v'], m_scb)
                        i = v.tensor_tensor(tsil_v, tsil_v, pU, ALU.mult)
                        mk = inc(i, 'v', 1)
                        v.wait_ge(SEM['v'], mk)
                        i = v.tensor_tensor(a_v[:, oc, :], tsil_v, scb_v, ALU.mult)
                        inc(i, 'v', 1)
                    m_a[oc] = CNT['v']
                    bfree[ui] = ('v', m_a[oc])
                m_a_done = CNT['v']

                m_dn = [0] * 16
                for hc in range(16):
                    dw_tile, dw_wq, dw_slot = ring_tile(dr["dw_l"][g][hc], width=1408)
                    pX = pA if hc % 2 == 0 else pB

                    @emit('tensor')
                    def _(t, g=g, hc=hc, pX=pX, dw_wq=dw_wq, m_a_done=m_a_done,
                          m_sil_last=m_sil[IC - 1], m_dn2=(m_dn[hc - 2] if hc >= 2 else 0)):
                        t.wait_ge(SEM[dw_wq[0]], dw_wq[1])
                        t.wait_ge(SEM['v'], m_a_done)
                        t.wait_ge(SEM['s'], m_sil_last)
                        if m_dn2:
                            t.wait_ge(SEM['v'], m_dn2)
                        for ik in range(IC):
                            t.matmul(pX[:, 0:512], dw_tile[:, ik * 128:(ik + 1) * 128],
                                     a_v[:, ik, 0:512], start=(ik == 0), stop=(ik == IC - 1))
                            i = t.matmul(pX[:, 512:1024], dw_tile[:, ik * 128:(ik + 1) * 128],
                                         a_v[:, ik, 512:1024], start=(ik == 0), stop=(ik == IC - 1))
                        inc(i, 't', 1)
                    ring_free[dw_slot] = CNT['t']
                    m_dt = CNT['t']

                    @emit('vector')
                    def _(v, hc=hc, pX=pX, m_dt=m_dt):
                        v.wait_ge(SEM['t'], m_dt)
                        i = v.tensor_tensor(x_v[:, hc, :], x_v[:, hc, :], pX, ALU.add)
                        inc(i, 'v', 1)
                    m_dn[hc] = CNT['v']
                m_xg_free_t = CNT['t']
            return list(m_dn)


        def emit_attention():
            kdv = kd_v.rearrange("p (q t) -> p q t", q=16)
            klv = gpp_v.rearrange("p (q t) -> p q t", q=16)
            bbq = bb_v.rearrange("p (q t) -> p q t", q=16)
            ebq = ebm_v.rearrange("p (q t) -> p q t", q=16)
            ballv = Ball_sb.rearrange("p (h c d) -> p h c d", h=4, c=16)

            bar = dict(CNT)

            @emit('sync')
            def _(sy, bar=bar):
                wait_marks(sy, {k: bar[k] for k in ('t', 'v', 's', 'p')})
                i = sy.dma_start(xspill_d, A.at(0, F32, [128, 16384]))
                inc(i, 'sq', 16)
            m_spill = CNT['sq']

            @emit('gpsimd')
            def _(g, m_spill=m_spill):
                g.wait_ge(SEM['sq'], m_spill)
                g.memset(S32_v, 0.0)
                g.memset(S16_v, 0.0)
                i = g.memset(Bsum_sb, 0.0)
                inc(i, 'p', 1)
            m_zero = CNT['p']

            if ATTN_PH < 2:
                return
            psl = [pA[:, 0:512], pA[:, 512:1024], pB[:, 0:512], pB[:, 512:1024]]
            ps_free = [0, 0, 0, 0]
            ps_free_sem = ['s', 's', 's', 's']
            m_s16 = [('p', m_zero)] * NH
            m_exps_prev = 0
            m_oev = [0] * NH
            m_oev_ph = [0, 0]
            m_supd = [0] * NH
            m_supd_last = 0
            m_atm_prev = 0
            m_ptfree = [0, 0]
            m_qe_last = [0]

            for hf in range(2):
                hs = slice(hf * 512, hf * 512 + 512)
                wgk2_t, wgk2_wq, wgk2_slot = ring_tile(dr["wgk2"], width=2048, rows=16)
                m_pD_free = m_atm_prev

                @emit('tensor')
                def _(t, hf=hf, hs=hs, wgk2_wq=wgk2_wq, bar=bar, m_pD_free=m_pD_free):
                    if hf == 0:
                        wait_marks(t, {k: bar[k] for k in ('v', 's', 'p')})
                    t.wait_ge(SEM[wgk2_wq[0]], wgk2_wq[1])
                    if m_pD_free:
                        t.wait_ge(SEM['v'], m_pD_free)
                    for k in range(16):
                        i = t.matmul(pD[0:16, 0:512], wgk1_sb[:, k * 16:(k + 1) * 16],
                                     h3[:, k, hs], start=(k == 0), stop=(k == 15))
                    inc(i, 't', 1)
                m_gk1 = CNT['t']

                @emit('vector')
                def _(v, m_gk1=m_gk1):
                    v.wait_ge(SEM['t'], m_gk1)
                    i = v.tensor_copy(gkt_v, pD[0:16, 0:512])
                    inc(i, 'v', 1)
                m_gkt = CNT['v']

                m_sig_prev = 0
                for op_ in range(8):
                    for half in range(2):
                        oc = op_ * 2 + half
                        pX = pC[:, half * 512:half * 512 + 512]

                        @emit('tensor')
                        def _(t, oc=oc, pX=pX, m_gkt=m_gkt, wgk2_slot=wgk2_slot,
                              m_prev=m_sig_prev):
                            t.wait_ge(SEM['v'], m_gkt)
                            if m_prev > 0:
                                t.wait_ge(SEM['s'], m_prev)
                            i = t.matmul(pX, ring[0:16, wgk2_slot * 2048 + oc * 128:
                                                  wgk2_slot * 2048 + (oc + 1) * 128],
                                         gkt_v, start=True, stop=True)
                            inc(i, 't', 1)
                    m_g2 = CNT['t']

                    @emit('scalar')
                    def _(sc, op_=op_, m_g2=m_g2):
                        sc.wait_ge(SEM['t'], m_g2)
                        sc.activation(sgt_v, pC[:, 0:512], AF.Sigmoid,
                                      bias=bgk2_sb[:, 2 * op_:2 * op_ + 1], scale=1.0)
                        i = sc.activation(sg2_v, pC[:, 512:1024], AF.Sigmoid,
                                          bias=bgk2_sb[:, 2 * op_ + 1:2 * op_ + 2], scale=1.0)
                        mk = inc(i, 's', 1)
                        sc.activation(gk_v[:, 2 * op_, :], sgt_v, AF.Ln, bias=0.0, scale=1.0)
                        i = sc.activation(gk_v[:, 2 * op_ + 1, :], sg2_v, AF.Ln,
                                          bias=0.0, scale=1.0)
                        inc(i, 's', 1)
                    m_sig_prev = CNT['s'] - 1
                ring_free[wgk2_slot] = CNT['t']
                m_gk_done = CNT['s']

                for which, wsrc, scale_ in (("q", "wq_l", QSCALE), ("k", "wk_l", 1.0)):
                    for oc in range(16):
                        tile, twq, slot = ring_tile(dr[wsrc][oc])
                        sl = oc % 4

                        @emit('tensor')
                        def _(t, oc=oc, sl=sl, tile=tile, twq=twq,
                              fm=ps_free[sl], fsem=ps_free_sem[sl], hs=hs):
                            t.wait_ge(SEM[twq[0]], twq[1])
                            if fm:
                                t.wait_ge(SEM[fsem], fm)
                            for k in range(16):
                                i = t.matmul(psl[sl], tile[:, k * 128:(k + 1) * 128],
                                             h3[:, k, hs], start=(k == 0), stop=(k == 15))
                            inc(i, 't', 1)
                        ring_free[slot] = CNT['t']
                        m_mm = CNT['t']

                        @emit('vector')
                        def _(v, which=which, oc=oc, sl=sl, m_mm=m_mm, hf=hf,
                              scale_=scale_, m_spill=m_spill):
                            v.wait_ge(SEM['t'], m_mm)
                            if which == "q":
                                v.wait_ge(SEM['sq'], m_spill)
                            dstv = (qe_v[:, oc, hf * 512:hf * 512 + 512] if which == "q"
                                    else k_v[:, oc, :])
                            if which == "q":
                                i = v.tensor_scalar(dstv, psl[sl], scale_, None, ALU.mult)
                            else:
                                i = v.tensor_copy(dstv, psl[sl])
                            inc(i, 'v', 1)
                        ps_free[sl] = CNT['v']
                        ps_free_sem[sl] = 'v'

                for oc in range(8):
                    tile, twq, slot = ring_tile(dr["wv_l"][oc])
                    sl = oc % 4

                    @emit('tensor')
                    def _(t, oc=oc, sl=sl, tile=tile, twq=twq,
                          fm=ps_free[sl], fsem=ps_free_sem[sl], hs=hs):
                        t.wait_ge(SEM[twq[0]], twq[1])
                        if fm:
                            t.wait_ge(SEM[fsem], fm)
                        for k in range(16):
                            i = t.matmul(psl[sl], tile[:, k * 128:(k + 1) * 128],
                                         h3[:, k, hs], start=(k == 0), stop=(k == 15))
                        inc(i, 't', 1)
                    ring_free[slot] = CNT['t']
                    m_mm = CNT['t']

                    @emit('vector')
                    def _(v, sl=sl, m_mm=m_mm, m_gk_done=m_gk_done):
                        v.wait_ge(SEM['t'], m_mm)
                        v.wait_ge(SEM['s'], m_gk_done)
                        i = v.tensor_copy(gpp_v[:, 0:512], psl[sl])
                        inc(i, 'v', 1)
                    ps_free[sl] = CNT['v']
                    ps_free_sem[sl] = 'v'
                    m_ev = CNT['v']

                    @emit('tensor')
                    def _(t, m_ev=m_ev, m_ptf=m_ptfree[0]):
                        t.wait_ge(SEM['v'], m_ev)
                        if m_ptf:
                            t.wait_ge(SEM['v'], m_ptf)
                        for j in range(4):
                            i = t.transpose(pT[:, j * 128:(j + 1) * 128],
                                            gpp_v[:, j * 128:(j + 1) * 128], ident_sb)
                        inc(i, 't', 1)
                    m_tr = CNT['t']

                    @emit('vector')
                    def _(v, oc=oc, m_tr=m_tr):
                        v.wait_ge(SEM['t'], m_tr)
                        i = v.tensor_copy(v_v[:, 0:4, oc * 128:(oc + 1) * 128],
                                          pT.rearrange("p (j t) -> p j t", j=8)[:, 0:4, :])
                        inc(i, 'v', 1)
                    m_ptfree[0] = CNT['v']

                for c in range(8):
                    ci = hf * 8 + c

                    @emit('vector')
                    def _(v, c=c, ci=ci, m_exps_prev=m_exps_prev, m_gk_done=m_gk_done,
                          m_zero=m_zero):
                        v.wait_ge(SEM['s'], max(m_exps_prev, m_gk_done))
                        v.wait_ge(SEM['p'], m_zero)
                        v.tensor_copy(ballv[:, :, ci, :],
                                      Bsum_sb.rearrange("p (h d) -> p h d", h=4))
                        for q in range(16):
                            i = v.tensor_tensor_scan(bbq[:, q, :],
                                                     gk_v[:, q, c * 64:(c + 1) * 64],
                                                     zz64, 0.0, ALU.add, ALU.add)
                        mk = inc(i, 'v', 1)
                        v.wait_ge(SEM['v'], mk)
                        i = v.tensor_tensor(Bsum_sb, Bsum_sb, bbq[:, :, 63], ALU.add)
                        inc(i, 'v', 1)
                    m_b = CNT['v']

                    @emit('scalar')
                    def _(sc, m_b=m_b):
                        sc.wait_ge(SEM['v'], m_b)
                        sc.activation(ebl_sb, bbq[:, :, 63], AF.Exp, bias=0.0, scale=R16)
                        i = sc.activation(ebm_v, bb_v, AF.Exp, bias=0.0, scale=R16)
                        inc(i, 's', 1)
                    m_eb = CNT['s']

                    @emit('gpsimd')
                    def _(g, ci=ci, m_eb=m_eb):
                        g.wait_ge(SEM['s'], m_eb)
                        i = g.tensor_tensor(
                            qe_v[:, :, ci * 64:ci * 64 + 64],
                            qe_v[:, :, ci * 64:ci * 64 + 64],
                            ebq, ALU.mult)
                        inc(i, 'p', 1)
                    m_qe = CNT['p']
                    m_qe_last[0] = m_qe

                    @emit('scalar')
                    def _(sc, m_qe=m_qe):
                        sc.wait_ge(SEM['p'], m_qe)
                        i = sc.activation(ebm_v, bb_v, AF.Exp, bias=0.0, scale=-R16)
                        inc(i, 's', 1)
                    m_em = CNT['s']
                    m_exps_prev = m_em

                    @emit('vector')
                    def _(v, c=c, m_em=m_em):
                        v.wait_ge(SEM['s'], m_em)
                        v.tensor_tensor(kdv, k_v[:, :, c * 64:c * 64 + 64],
                                        ebq, ALU.mult)
                        for q in range(16):
                            i = v.tensor_scalar(klv[:, q, :], kdv[:, q, :],
                                                ebl_sb[:, q:q + 1], None, ALU.mult)
                        inc(i, 'v', 1)
                    m_kl = CNT['v']

                    for h in range(NH):
                        ph = h % 2

                        b0 = 64 * (c % 2)

                        @emit('tensor')
                        def _(t, h=h, ph=ph, b0=b0, m_kl=m_kl, m_ptf=m_ptfree[ph]):
                            t.wait_ge(SEM['v'], m_kl)
                            if m_ptf:
                                t.wait_ge(SEM['v'], m_ptf)
                            for dc in range(4):
                                i = t.transpose(
                                    pT[b0:b0 + 64,
                                       ph * 512 + dc * 128:ph * 512 + (dc + 1) * 128],
                                    gpp_v[:, (4 * h + dc) * 64:(4 * h + dc + 1) * 64],
                                    ident_sb, tile_position=(0, b0))
                            inc(i, 't', 1)
                        m_trh = CNT['t']

                        @emit('vector')
                        def _(v, h=h, ph=ph, b0=b0, m_trh=m_trh):
                            v.wait_ge(SEM['t'], m_trh)
                            i = v.tensor_copy(kl_tm_v[b0:b0 + 64, 4 * h:4 * h + 4, :],
                                              pT.rearrange("p (j t) -> p j t", j=8)
                                              [b0:b0 + 64, ph * 4:ph * 4 + 4, :])
                            inc(i, 'v', 1)
                        m_klev = CNT['v']
                        m_ptfree[ph] = m_klev

                        @emit('tensor')
                        def _(t, h=h, ci=ci, b0=b0, m_kl=m_kl, m_atm_prev=m_atm_prev,
                              m_qe=m_qe):
                            if m_atm_prev:
                                t.wait_ge(SEM['v'], m_atm_prev)
                            t.wait_ge(SEM['v'], m_kl)
                            t.wait_ge(SEM['p'], m_qe)
                            for dc in range(4):
                                i = t.matmul(pD[b0:b0 + 64, 0:64], kdv[:, 4 * h + dc, :],
                                             qe_v[:, 4 * h + dc, ci * 64:ci * 64 + 64],
                                             start=(dc == 0), stop=(dc == 3),
                                             tile_position=(0, b0))
                            inc(i, 't', 1)
                        m_at = CNT['t']

                        @emit('vector')
                        def _(v, b0=b0, m_at=m_at):
                            v.wait_ge(SEM['t'], m_at)
                            i = v.tensor_tensor(at16_v[b0:b0 + 64, :],
                                                pD[b0:b0 + 64, 0:64],
                                                maskT_sb[b0:b0 + 64, :], ALU.mult)
                            inc(i, 'v', 1)
                        m_atm = CNT['v']
                        m_atm_prev = m_atm

                        @emit('tensor')
                        def _(t, h=h, c=c, ci=ci, ph=ph, m_atm=m_atm,
                              m_s16h=m_s16[h], m_oev2=m_oev_ph[ph]):
                            t.wait_ge(SEM['v'], m_atm)
                            t.wait_ge(SEM[m_s16h[0]], m_s16h[1])
                            if m_oev2:
                                t.wait_ge(SEM['v'], m_oev2)
                            b0_ = 64 * (c % 2)
                            vtm = v_v[b0_:b0_ + 64, c // 2, h * 256:(h + 1) * 256]
                            t.matmul(pC[b0_:b0_ + 64, ph * 512:ph * 512 + 256],
                                     at16_v[b0_:b0_ + 64, :], vtm,
                                     start=True, stop=False, tile_position=(b0_, b0_))
                            for dc in range(4):
                                i = t.matmul(pC[b0_:b0_ + 64, ph * 512:ph * 512 + 256],
                                             qe_v[:, 4 * h + dc, ci * 64:ci * 64 + 64],
                                             S16_v[:, 4 * h + dc, :],
                                             start=False, stop=(dc == 3),
                                             tile_position=(0, b0_))
                            inc(i, 't', 1)
                        m_o = CNT['t']

                        @emit('vector')
                        def _(v, h=h, ci=ci, ph=ph, m_o=m_o):
                            v.wait_ge(SEM['t'], m_o)
                            b0_ = 64 * (ci % 2)
                            i = v.tensor_copy(
                                oraw_v[b0_:b0_ + 64, ci // 2, h * 256:(h + 1) * 256],
                                pC[b0_:b0_ + 64, ph * 512:ph * 512 + 256])
                            inc(i, 'v', 1)
                        m_oev[h] = CNT['v']
                        m_oev_ph[ph] = CNT['v']

                        @emit('tensor')
                        def _(t, h=h, c=c, b0=b0, m_klev=m_klev, m_supd_prev=m_supd_last):
                            t.wait_ge(SEM['v'], m_klev)
                            if m_supd_prev:
                                t.wait_ge(SEM['v'], m_supd_prev)
                            vtm = v_v[b0:b0 + 64, c // 2, h * 256:(h + 1) * 256]
                            for dc in range(4):
                                i = t.matmul(pA[:, dc * 256:(dc + 1) * 256],
                                             kl_tm_v[b0:b0 + 64, 4 * h + dc, :], vtm,
                                             start=True, stop=True)
                            inc(i, 't', 1)
                        m_M = CNT['t']

                        @emit('vector')
                        def _(v, h=h, m_M=m_M):
                            v.wait_ge(SEM['t'], m_M)
                            for dc in range(4):
                                i = v.scalar_tensor_tensor(
                                    S32_v[:, 4 * h + dc, :], S32_v[:, 4 * h + dc, :],
                                    ebl_sb[:, 4 * h + dc:4 * h + dc + 1],
                                    pA[:, dc * 256:(dc + 1) * 256], ALU.mult, ALU.add)
                            inc(i, 'v', 1)
                        m_supd[h] = CNT['v']
                        m_supd_last = CNT['v']

                        if h == NH - 1:
                            @emit('vector')
                            def _(v, m_o=m_o):
                                v.wait_ge(SEM['t'], m_o)
                                i = v.tensor_copy(S16_v, S32_v)
                                inc(i, 'v', 1)
                            for hh in range(NH):
                                m_s16[hh] = ('v', CNT['v'])

            if ATTN_PH < 3:
                return
            m_scan_t = CNT['t']
            m_scan_v = CNT['v']
            ps_free = [m_scan_v] * 4
            ps_free_sem = ['v'] * 4

            @emit('scalar')
            def _(sc, m_scan_v=m_scan_v):
                sc.wait_ge(SEM['v'], m_scan_v)
                i = sc.activation(dall_sb, Bsum_sb, AF.Exp, bias=0.0, scale=R16)
                inc(i, 's', 1)
            m_dall = CNT['s']

            m_exp_dma = 0
            for hd in range(2):
                @emit('vector')
                def _(v, hd=hd, m_dall=m_dall, m_scan_t=m_scan_t, m_prev=m_exp_dma):
                    v.wait_ge(SEM['s'], m_dall)
                    v.wait_ge(SEM['t'], m_scan_t)
                    if m_prev:
                        v.wait_ge(SEM['sq'], m_prev)
                    v.tensor_copy(stg_v[:, :, 0:256], S32_v[:, hd * 8:(hd + 1) * 8, :])
                    i = v.tensor_copy(stg_v[:, :, 256], dall_sb[:, hd * 8:(hd + 1) * 8])
                    inc(i, 'v', 1)
                m_st = CNT['v']

                @emit('sync')
                def _(sy, hd=hd, m_st=m_st):
                    sy.wait_ge(SEM['v'], m_st)
                    i = sy.dma_start(
                        comm_in.rearrange("(q p) w -> p q w", p=128)
                        [:, hd * 8:(hd + 1) * 8, :],
                        stg_v)
                    inc(i, 'sq', 16)
                m_exp_dma = CNT['sq']
            m_export = CNT['sq']

            @emit('gpsimd')
            def _(g, m_export=m_export):
                g.wait_ge(SEM['sq'], m_export)
                i = g.collective_compute(
                    "AllGather", ALU.bypass, replica_groups=[list(range(8))],
                    ins=[comm_in], outs=[comm_out])
                inc(i, 'cc', 1)
            m_cc = CNT['cc']

            if ATTN_PH < 4:
                return
            for oc in range(8):
                tile, twq, slot = ring_tile(dr["wg_l"][oc])
                for hf in range(2):
                    sl = (oc * 2 + hf) % 4

                    @emit('tensor')
                    def _(t, oc=oc, hf=hf, sl=sl, tile=tile, twq=twq,
                          fm=ps_free[sl], fsem=ps_free_sem[sl]):
                        t.wait_ge(SEM[twq[0]], twq[1])
                        if fm:
                            t.wait_ge(SEM[fsem], fm)
                        for k in range(16):
                            i = t.matmul(psl[sl], tile[:, k * 128:(k + 1) * 128],
                                         h3[:, k, hf * 512:hf * 512 + 512],
                                         start=(k == 0), stop=(k == 15))
                        inc(i, 't', 1)
                    ring_free[slot] = CNT['t']
                    m_mm = CNT['t']

                    @emit('vector')
                    def _(v, sl=sl, m_mm=m_mm, m_export=m_export):
                        v.wait_ge(SEM['t'], m_mm)
                        v.wait_ge(SEM['sq'], m_export)
                        i = v.tensor_copy(gpp_v[:, 0:512], psl[sl])
                        inc(i, 'v', 1)
                    ps_free[sl] = CNT['v']
                    ps_free_sem[sl] = 'v'
                    m_ev = CNT['v']

                    @emit('tensor')
                    def _(t, m_ev=m_ev, m_ptf=m_ptfree[0]):
                        t.wait_ge(SEM['v'], m_ev)
                        if m_ptf:
                            t.wait_ge(SEM['v'], m_ptf)
                        for j in range(4):
                            i = t.transpose(pT[:, j * 128:(j + 1) * 128],
                                            gpp_v[:, j * 128:(j + 1) * 128], ident_sb)
                        inc(i, 't', 1)
                    m_tr = CNT['t']

                    @emit('vector')
                    def _(v, oc=oc, hf=hf, m_tr=m_tr):
                        v.wait_ge(SEM['t'], m_tr)
                        i = v.tensor_copy(
                            g_tm_v[:, hf * 4:hf * 4 + 4, oc * 128:(oc + 1) * 128],
                            pT.rearrange("p (j t) -> p j t", j=8)[:, 0:4, :])
                        inc(i, 'v', 1)
                    m_ptfree[0] = CNT['v']
            m_gproj = CNT['v']

            if ATTN_PH < 5:
                return
            rbh_v = [A.at(32768, BF16, [128, 8, 257]), A.at(41088, BF16, [128, 8, 257])]
            acch_v = A.at(49408, F32, [128, 8, 256])
            comm_rv = comm_out.rearrange("(r q p) w -> r p q w", r=8, q=16)
            m_rb_war = [0, 0]
            m_s0h = []

            def emit_comb_half(hcs):
                @emit('vector')
                def _(v, hcs=hcs, m_dall=m_dall, m_scan_t=m_scan_t):
                    if hcs == 0:
                        v.wait_ge(SEM['s'], m_dall)
                        v.wait_ge(SEM['t'], m_scan_t)
                    v.memset(acch_v, 0.0)
                    i = v.memset(runA_sb, 1.0)
                    inc(i, 'v', 1)

                for pi, p in enumerate(range(7, -1, -1)):
                    par = pi % 2
                    rsem = 'r0' if par == 0 else 'r1'

                    @emit('sync')
                    def _(sy, hcs=hcs, p=p, par=par, rsem=rsem, m_cc=m_cc,
                          m_rbw=m_rb_war[par], m_scan_v=m_scan_v):
                        sy.wait_ge(SEM['cc'], m_cc)
                        if m_rbw:
                            sy.wait_ge(SEM['v'], m_rbw)
                        else:
                            sy.wait_ge(SEM['v'], m_scan_v)
                        for q in range(8):
                            i = sy.dma_start(rbh_v[par][:, q, :],
                                             comm_rv[p][:, hcs * 8 + q, :])
                            inc(i, rsem, 16)
                    m_dma = CNT[rsem]

                    @emit('vector')
                    def _(v, p=p, par=par, rsem=rsem, m_dma=m_dma, m_prev=CNT['v']):
                        v.wait_ge(SEM[rsem], m_dma)
                        v.wait_ge(SEM['v'], m_prev)
                        i = v.tensor_scalar(escA_sb, runA_sb, mmask_sb[:, p:p + 1],
                                            None, ALU.mult)
                        mk = inc(i, 'v', 1)
                        v.wait_ge(SEM['v'], mk)
                        for q in range(8):
                            i = v.scalar_tensor_tensor(acch_v[:, q, :],
                                                       rbh_v[par][:, q, 0:256],
                                                       escA_sb[:, q:q + 1],
                                                       acch_v[:, q, :],
                                                       ALU.mult, ALU.add)
                        mk = inc(i, 'v', 1)
                        v.wait_ge(SEM['v'], mk)
                        i = v.tensor_scalar(escA_sb, rbh_v[par][:, :, 256],
                                            mmask_sb[:, p:p + 1],
                                            ummask_sb[:, p:p + 1], ALU.mult, ALU.add)
                        mk = inc(i, 'v', 1)
                        v.wait_ge(SEM['v'], mk)
                        i = v.tensor_tensor(runA_sb, runA_sb, escA_sb, ALU.mult)
                        inc(i, 'v', 1)
                    m_rb_war[par] = CNT['v']

                @emit('vector')
                def _(v, hcs=hcs):
                    i = v.tensor_copy(S0_v[:, hcs * 8:hcs * 8 + 8, :], acch_v)
                    inc(i, 'v', 1)
                m_s0h.append(CNT['v'])

            @emit('scalar')
            def _(sc, m_scan_v=m_scan_v, m_dall=m_dall, m_qel=m_qe_last[0]):
                sc.wait_ge(SEM['v'], m_scan_v)
                sc.wait_ge(SEM['s'], m_dall)
                sc.wait_ge(SEM['p'], m_qel)
                i = sc.activation(ebm_v[:, 0:256], Ball_sb, AF.Exp, bias=0.0, scale=R16)
                inc(i, 's', 1)
            m_ebB = CNT['s']
            eb2 = ebm_v[:, 0:256].rearrange("p (h c d) -> p h c d", h=4, c=16)

            qs2 = [qs16_v, A.at(129792, BF16, [128, 256])]
            m_qsf = [m_scan_t, m_scan_t]
            m_o3ev = [m_scan_v, m_scan_v]

            def emit_pb(h):
                for ci in range(16):
                    par = ci % 2
                    b0 = 64 * par

                    @emit('vector')
                    def _(v, ci=ci, h=h, par=par, m_ebB=m_ebB, m_qsfp=m_qsf[par]):
                        v.wait_ge(SEM['s'], m_ebB)
                        if m_qsfp:
                            v.wait_ge(SEM['t'], m_qsfp)
                        for dc in range(4):
                            i = v.tensor_scalar(
                                qs2[par][:, dc * 64:(dc + 1) * 64],
                                qe_v[:, 4 * h + dc, ci * 64:ci * 64 + 64],
                                eb2[:, h, ci, dc:dc + 1], None, ALU.mult)
                        inc(i, 'v', 1)
                    m_qs = CNT['v']

                    @emit('tensor')
                    def _(t, ci=ci, h=h, par=par, b0=b0, m_qs=m_qs,
                          m_oev2=m_o3ev[par], m_s0=m_s0h[h // 2]):
                        t.wait_ge(SEM['v'], m_qs)
                        t.wait_ge(SEM['v'], m_s0)
                        t.wait_ge(SEM['v'], m_oev2)
                        for dc in range(4):
                            i = t.matmul(pC[b0:b0 + 64, par * 512:par * 512 + 256],
                                         qs2[par][:, dc * 64:(dc + 1) * 64],
                                         S0_v[:, 4 * h + dc, :],
                                         start=(dc == 0), stop=(dc == 3),
                                         tile_position=(0, b0))
                        inc(i, 't', 1)
                    m_qsf[par] = CNT['t']
                    m_o3 = CNT['t']

                    @emit('vector')
                    def _(v, ci=ci, h=h, par=par, b0=b0, m_o3=m_o3):
                        v.wait_ge(SEM['t'], m_o3)
                        sl_ = oraw_v[b0:b0 + 64, ci // 2, h * 256:(h + 1) * 256]
                        i = v.tensor_tensor(sl_, sl_,
                                            pC[b0:b0 + 64, par * 512:par * 512 + 256],
                                            ALU.add)
                        inc(i, 'v', 1)
                    m_o3ev[par] = CNT['v']

            for hcs in range(2):
                emit_comb_half(hcs)
                emit_pb(2 * hcs)
                emit_pb(2 * hcs + 1)
            m_comb = CNT['v']
            m_qs_free_t = max(m_qsf)
            m_phaseB = CNT['v']

            if ATTN_PH < 7:
                return
            sumsq_v = A.at(138240, F32, [128, 32])
            rstd_v = A.at(138496, F32, [128, 32])

            @emit('scalar')
            def _(sc, m_phaseB=m_phaseB, m_gproj=m_gproj):
                sc.wait_ge(SEM['v'], max(m_phaseB, m_gproj))
                for j in range(8):
                    for h in range(NH):
                        idx = j * 4 + h
                        sc.activation(osq_v, oraw_v[:, j, h * 256:(h + 1) * 256],
                                      AF.Square, bias=0.0, scale=1.0,
                                      accum_out=sumsq_v[:, idx:idx + 1])
                sc.activation(rstd_v, sumsq_v, AF.Ln, bias=EPS, scale=1.0 / DV)
                i = sc.activation(rstd_v, rstd_v, AF.Exp, bias=0.0, scale=-0.5)
                inc(i, 's', 1)
            m_rstd = CNT['s']

            sil_scr = [A.at(122880, F32, [128, 1024]), A.at(130816, F32, [128, 1024])]
            m_vj = [0] * 8
            for j in range(8):
                @emit('scalar')
                def _(sc, j=j, m_rstd=m_rstd, m_prevv=(m_vj[j - 2] if j >= 2 else 0)):
                    if m_prevv:
                        sc.wait_ge(SEM['v'], m_prevv)
                    i = sc.activation(sil_scr[j % 2], g_tm_v[:, j, :], AF.Silu,
                                      bias=0.0, scale=1.0)
                    inc(i, 's', 1)
                m_silj = CNT['s']

                for h in range(NH):
                    idx = j * 4 + h
                    osl = oraw_v[:, j, h * 256:(h + 1) * 256]

                    @emit('vector')
                    def _(v, osl=osl, j=j, h=h, idx=idx, m_silj=m_silj,
                          m_rstd=m_rstd):
                        if h == 0:
                            v.wait_ge(SEM['s'], m_silj)
                        if idx == 0:
                            v.wait_ge(SEM['s'], m_rstd)
                        v.tensor_scalar(osq_v, osl, rstd_v[:, idx:idx + 1],
                                        None, ALU.mult)
                        v.tensor_tensor(osq_v, osq_v, wnorm_sb, ALU.mult)
                        i = v.tensor_tensor(osl, osq_v,
                                            sil_scr[j % 2][:, h * 256:(h + 1) * 256],
                                            ALU.mult)
                        inc(i, 'v', 1)
                m_vj[j] = CNT['v']
            m_gated = CNT['v']

            if ATTN_PH < 75:
                return
            m_ogtr = [0, 0]
            m_og_ser = [0]
            for tg in range(8):
                for vg in range(2):
                    @emit('tensor')
                    def _(t, tg=tg, vg=vg, m_gated=m_gated, m_ptf=m_og_ser[0]):
                        t.wait_ge(SEM['v'], m_gated)
                        if m_ptf:
                            t.wait_ge(SEM['v'], m_ptf)
                        for j in range(4):
                            vc = vg * 4 + j
                            i = t.transpose(
                                pT[:, vg * 512 + j * 128:vg * 512 + (j + 1) * 128],
                                oraw_v[:, tg, vc * 128:(vc + 1) * 128], ident_sb)
                        inc(i, 't', 1)
                    m_tr = CNT['t']

                    @emit('vector')
                    def _(v, tg=tg, vg=vg, m_tr=m_tr):
                        v.wait_ge(SEM['t'], m_tr)
                        i = v.tensor_copy(
                            og_v[:, vg * 4:vg * 4 + 4, tg * 128:(tg + 1) * 128],
                            pT.rearrange("p (j t) -> p j t", j=8)[:, vg * 4:vg * 4 + 4, :])
                        inc(i, 'v', 1)
                    m_ogtr[vg] = CNT['v']
                    m_og_ser[0] = CNT['v']
            m_og = CNT['v']

            if ATTN_PH < 8:
                return
            @emit('sync')
            def _(sy, m_phaseB=m_phaseB, m_qsf=m_qs_free_t, m_qel=m_qe_last[0]):
                sy.wait_ge(SEM['v'], m_phaseB)
                sy.wait_ge(SEM['t'], m_qsf)
                sy.wait_ge(SEM['p'], m_qel)
                i = sy.dma_start(A.at(0, F32, [128, 16384]), xspill_d)
                inc(i, 'sq', 16)
            m_reload = CNT['sq']

            for oc in range(16):
                tile, twq, slot = ring_tile(dr["wo_l"][oc], width=1024)
                for hf in range(2):
                    sl = (oc * 2 + hf) % 4

                    @emit('tensor')
                    def _(t, oc=oc, hf=hf, sl=sl, tile=tile, twq=twq, m_og=m_og,
                          fm=ps_free[sl], fsem=ps_free_sem[sl]):
                        t.wait_ge(SEM[twq[0]], twq[1])
                        t.wait_ge(SEM['v'], m_og)
                        if fm:
                            t.wait_ge(SEM[fsem], fm)
                        for kv in range(8):
                            i = t.matmul(psl[sl], tile[:, kv * 128:(kv + 1) * 128],
                                         og_v[:, kv, hf * 512:hf * 512 + 512],
                                         start=(kv == 0), stop=(kv == 7))
                        inc(i, 't', 1)
                    ring_free[slot] = CNT['t']
                    m_mm = CNT['t']

                    @emit('vector')
                    def _(v, oc=oc, hf=hf, sl=sl, m_mm=m_mm, m_reload=m_reload):
                        v.wait_ge(SEM['t'], m_mm)
                        v.wait_ge(SEM['sq'], m_reload)
                        xs = x_v[:, oc, hf * 512:hf * 512 + 512]
                        i = v.tensor_tensor(xs, xs, psl[sl], ALU.add)
                        inc(i, 'v', 1)
                    ps_free[sl] = CNT['v']
                    ps_free_sem[sl] = 'v'

        hm1 = emit_ln(first=True)
        emit_ffn(hm1)

        if STAGE >= 2:
            emit_ln()
            emit_attention()
        if STAGE >= 3:
            hm3 = emit_ln()
            m_dn_final = emit_ffn(hm3)

        if DEBUG:
            bar = dict(CNT)

            @emit('gpsimd')
            def _(gg):
                wait_marks(gg, {k: bar[k] for k in ('t', 'v', 's')})
                inc(gg.dma_start(dbg_h, h_sb), 'wq', 16)
                inc(gg.dma_start(dbg_soft, softT_sb), 'wq', 16)
                inc(gg.dma_start(dbg_scal, scalT_sb), 'wq', 16)
                inc(gg.dma_start(dbg_scb, scb_v), 'wq', 16)
                inc(gg.dma_start(dbg_a, a_v.rearrange("p a b -> p (a b)")), 'wq', 16)
                i = gg.dma_start(dbg_xg, xg_v.rearrange("p a b -> p (a b)"))
                inc(i, 'wq', 16)
                gg.wait_ge(SEM['wq'], CNT['wq'])

        bar = dict(CNT)

        @emit('sync')
        def _(sy):
            ov = out_d.rearrange("(c p) t -> p c t", p=128)
            for hc in range(NCH):
                sy.wait_ge(SEM['v'], m_dn_final[hc])
                i = sy.dma_start(ov[:, hc, :], x_v[:, hc, :])
                inc(i, 'sq', 16)
            wait_marks(sy, {k: bar[k] for k in ('t', 'v', 's', 'p')})
            sy.wait_ge(SEM['sq'], CNT['sq'])

    return nc


_CACHE = {}


def kernel(**inputs):
    d = _prep_host(inputs)
    f = np.float32
    x = np.asarray(inputs["L0_attn_x"], f).reshape(8192, 2048)
    shards = [np.ascontiguousarray(x[c * T:(c + 1) * T].T) for c in range(8)]

    nc = build()

    shared = {k: v for k, v in d.items()}
    in_maps = []
    for c in range(8):
        g0 = 0 if c < 4 else 4
        mm = np.zeros((128, 8), f)
        for p in range(8):
            if g0 <= p < c:
                mm[:, p] = 1.0
        m = dict(shared)
        m["xT"] = shards[c]
        m["fconst"] = np.tile(np.array([[EPS, 0.0, 0.0, 0.0]], f), (128, 1))
        m["mmask"] = mm
        m["ummask"] = 1.0 - mm
        in_maps.append(m)

    res = run_bass_kernel_spmd(nc, in_maps, core_ids=list(range(8)))
    outs = [np.asarray(res.results[c]["out"], f).T for c in range(8)]
    return np.concatenate(outs, axis=0).reshape(2, 4096, 2048)




# version 17
# speedup vs baseline: 1.0147x; 1.0055x over previous
import numpy as np
import ml_dtypes
import concourse.bass as bass
import concourse.mybir as mybir
from concourse.bass_utils import run_bass_kernel_spmd

F32 = mybir.dt.float32
BF16 = mybir.dt.bfloat16
I32 = mybir.dt.int32
AF = mybir.ActivationFunctionType
ALU = mybir.AluOpType
AX = mybir.AxisListType

STAGE = 3
ATTN_PH = 99
DEBUG = False

D, T, NCH = 2048, 1024, 16
G, IC = 8, 11
NH, DK, DV, CH, NCK = 4, 512, 256, 64, 16
EPS = 1e-5
QSCALE = float(DK) ** -0.5
R16 = 1.0 / 16.0
RING = 3

BF = ml_dtypes.bfloat16



def _pack_ws(w, kc, occ):
    return np.ascontiguousarray(
        w.reshape(kc, 128, occ, 128).transpose(2, 1, 0, 3).reshape(occ, 128, kc * 128))


def _prep_host(inputs):
    f = np.float32
    d = {}
    gp = np.asarray(inputs["L0_ffn_gate_proj_w"], f)
    up = np.asarray(inputs["L0_ffn_up_proj_w"], f)
    dn = np.asarray(inputs["L0_ffn_down_proj_w"], f)
    d["gw_l"] = np.ascontiguousarray(
        gp.reshape(8, IC, 128, 16, 128).transpose(0, 1, 4, 3, 2).reshape(8, IC, 128, 2048))
    d["uw_l"] = np.ascontiguousarray(
        up.reshape(8, IC, 128, 16, 128).transpose(0, 1, 4, 3, 2).reshape(8, IC, 128, 2048))
    d["dw_l"] = np.ascontiguousarray(
        dn.reshape(8, 16, 128, IC, 128).transpose(0, 1, 4, 3, 2).reshape(8, 16, 128, 1408))
    mu = np.asarray(inputs["L0_ffn_mask_up_w"], f)
    mup = np.zeros((8, 64, 2048), f)
    for g in range(8):
        mup[g, 8 * g:8 * g + 8, :] = mu[g].T
    d["mu_pair"] = mup
    d["gate_w"] = np.ascontiguousarray(np.asarray(inputs["L0_ffn_gate_w"], f))
    d["inv_i"] = np.asarray(inputs["L0_ffn_inv_map"]).astype(np.int32).reshape(1, 64)
    d["wq_l"] = _pack_ws(np.asarray(inputs["L0_attn_Wq"], f), 16, 16)
    d["wk_l"] = _pack_ws(np.asarray(inputs["L0_attn_Wk"], f), 16, 16)
    d["wv_l"] = _pack_ws(np.asarray(inputs["L0_attn_Wv"], f), 16, 8)
    d["wg_l"] = _pack_ws(np.asarray(inputs["L0_attn_Wg"], f), 16, 8)
    d["wo_l"] = _pack_ws(np.asarray(inputs["L0_attn_Wo"], f), 8, 16)
    d["wgk1"] = np.ascontiguousarray(np.asarray(inputs["L0_attn_Wgk1"], f))
    d["wgk2"] = np.ascontiguousarray(np.asarray(inputs["L0_attn_Wgk2"], f))
    d["bgk2_t"] = np.ascontiguousarray(np.asarray(inputs["L0_attn_bgk2"], f).reshape(16, 128).T)
    d["wnorm_t"] = np.ascontiguousarray(
        np.tile(np.asarray(inputs["L0_attn_Wnorm"], f)[None, :], (128, 1)))
    d["ident"] = np.eye(128, dtype=BF)
    d["ones128"] = np.ones((128, 128), dtype=BF)
    d["maskT"] = np.tile(np.triu(np.ones((64, 64), np.float32)), (2, 1)).astype(BF)
    d["iota64"] = np.arange(64, dtype=f).reshape(64, 1)
    oh = np.zeros((8, 8, 128), f)
    for g in range(8):
        oh[g, g, :] = 0.5
    d["onehot_half"] = oh.astype(BF)
    return d



class Arena:
    def __init__(self, slab_ap, nbytes):
        self.slab = slab_ap
        self.nbytes = nbytes

    def at(self, off, dtype, shape):
        esz = 2 if dtype == BF16 else 4
        nfree = int(np.prod(shape[1:]))
        assert off % 4 == 0 and off + nfree * esz <= self.nbytes, (off, nfree * esz)
        e0 = off // 4
        v = self.slab[:, e0:e0 + (nfree * esz + 3) // 4]
        if dtype == BF16:
            v = v.bitcast(BF16)
        v = v[:, :nfree]
        if shape[0] != 128:
            v = v[0:shape[0], :]
        if len(shape) > 2:
            dims = shape[1:]
            names = " ".join(f"a{i}" for i in range(len(dims)))
            v = v.rearrange(f"p ({names}) -> p {names}",
                            **{f"a{i}": dims[i] for i in range(len(dims))})
        return v



def build():
    nc = bass.Bass("TRN2", target_bir_lowering=False, num_devices=8)

    dr = {}
    def di(name, shape, dt=F32):
        dr[name] = nc.dram_tensor(name, list(shape), dt, kind="ExternalInput").ap()
    di("xT", (2048, 1024))
    di("gw_l", (8, IC, 128, 2048)); di("uw_l", (8, IC, 128, 2048)); di("dw_l", (8, 16, 128, 1408))
    di("mu_pair", (8, 64, 2048))
    di("gate_w", (64, 2048)); di("inv_i", (1, 64), I32)
    di("wq_l", (16, 128, 2048)); di("wk_l", (16, 128, 2048))
    di("wv_l", (8, 128, 2048)); di("wg_l", (8, 128, 2048)); di("wo_l", (16, 128, 1024))
    di("wgk1", (2048, 16)); di("wgk2", (16, 2048))
    di("bgk2_t", (128, 16)); di("wnorm_t", (128, 256))
    di("ident", (128, 128), BF16); di("ones128", (128, 128), BF16)
    di("maskT", (128, 64), BF16); di("iota64", (64, 1))
    di("onehot_half", (8, 8, 128), BF16)
    di("mmask", (128, 8)); di("ummask", (128, 8)); di("fconst", (128, 4))
    out_d = nc.dram_tensor("out", [2048, 1024], F32, kind="ExternalOutput").ap()
    if DEBUG:
        dbg_h = nc.dram_tensor("dbg_h", [128, NCH * 1024], F32, kind="ExternalOutput").ap()
        dbg_soft = nc.dram_tensor("dbg_soft", [64, 1024], F32, kind="ExternalOutput").ap()
        dbg_scal = nc.dram_tensor("dbg_scal", [8, 1024], F32, kind="ExternalOutput").ap()
        dbg_scb = nc.dram_tensor("dbg_scb", [128, 1024], F32, kind="ExternalOutput").ap()
        dbg_a = nc.dram_tensor("dbg_a", [128, IC * 1024], F32, kind="ExternalOutput").ap()
        dbg_xg = nc.dram_tensor("dbg_xg", [128, NCH * 1024], F32, kind="ExternalOutput").ap()
    xspill_d = nc.dram_tensor("xspill", [128, 16384], F32).ap()
    comm_in = nc.dram_tensor("comm_in", [2048, 257], BF16).ap()
    comm_out = nc.dram_tensor("comm_out", [8 * 2048, 257], BF16, addr_space="Shared").ap()

    h_sb = nc.alloc_sbuf_tensor("h_sb", [128, NCH * 1024], BF16).ap()
    h3 = h_sb.rearrange("p (c t) -> p c t", c=NCH)
    bcA = nc.alloc_sbuf_tensor("bcA", [128, 1024], F32).ap()
    bcB = nc.alloc_sbuf_tensor("bcB", [128, 1024], F32).ap()
    ring = nc.alloc_sbuf_tensor("ring", [128, RING * 2048], BF16).ap()
    ident_sb = nc.alloc_sbuf_tensor("ident_sb", [128, 128], BF16).ap()
    ones_sb = nc.alloc_sbuf_tensor("ones_sb", [128, 128], BF16).ap()
    maskT_sb = nc.alloc_sbuf_tensor("maskT_sb", [128, 64], BF16).ap()
    iota_sb = nc.alloc_sbuf_tensor("iota_sb", [64, 1], F32).ap()
    oh_sb = nc.alloc_sbuf_tensor("oh_sb", [8, 8 * 128], BF16).ap()
    gwperm_sb = nc.alloc_sbuf_tensor("gwperm_sb", [128, 1024], BF16).ap()
    inv_sb = nc.alloc_sbuf_tensor("inv_sb", [1, 64], I32).ap()
    invf_sb = nc.alloc_sbuf_tensor("invf_sb", [1, 64], F32).ap()
    invb_sb = nc.alloc_sbuf_tensor("invb_sb", [1, 64], BF16).ap()
    P_sb = nc.alloc_sbuf_tensor("P_sb", [64, 64], BF16).ap()
    wgk1_sb = nc.alloc_sbuf_tensor("wgk1_sb", [128, 256], BF16).ap()
    bgk2_sb = nc.alloc_sbuf_tensor("bgk2_sb", [128, 16], F32).ap()
    wnorm_sb = nc.alloc_sbuf_tensor("wnorm_sb", [128, 256], F32).ap()
    mmask_sb = nc.alloc_sbuf_tensor("mmask_sb", [128, 8], F32).ap()
    ummask_sb = nc.alloc_sbuf_tensor("ummask_sb", [128, 8], F32).ap()
    zz64 = nc.alloc_sbuf_tensor("zz64", [128, 64], F32).ap()
    softT_sb = nc.alloc_sbuf_tensor("softT_sb", [64, 1024], BF16).ap()
    scalT_sb = nc.alloc_sbuf_tensor("scalT_sb", [8, 1024], BF16).ap()
    Bsum_sb = nc.alloc_sbuf_tensor("Bsum_sb", [128, 16], F32).ap()
    Ball_sb = nc.alloc_sbuf_tensor("Ball_sb", [128, 256], F32).ap()
    ebl_sb = nc.alloc_sbuf_tensor("ebl_sb", [128, 16], F32).ap()
    dall_sb = nc.alloc_sbuf_tensor("dall_sb", [128, 16], F32).ap()
    run_sb = nc.alloc_sbuf_tensor("run_sb", [128, 1], F32).ap()
    esc_sb = nc.alloc_sbuf_tensor("esc_sb", [128, 1], F32).ap()
    runA_sb = nc.alloc_sbuf_tensor("runA_sb", [128, 8], F32).ap()
    escA_sb = nc.alloc_sbuf_tensor("escA_sb", [128, 8], F32).ap()
    gnm = nc.alloc_sbuf_tensor("gnm", [128, 4], F32).ap()
    gt8 = nc.alloc_sbuf_tensor("gt8", [128, 8], F32).ap()
    gel = nc.alloc_sbuf_tensor("gel", [128, 64], F32).ap()
    gpr = nc.alloc_sbuf_tensor("gpr", [128, 64], F32).ap()
    gfl = nc.alloc_sbuf_tensor("gfl", [128, 64], F32).ap()
    gm0 = nc.alloc_sbuf_tensor("gm0", [128, 64], F32).ap()
    gden = nc.alloc_sbuf_tensor("gden", [128, 8], F32).ap()
    grd = nc.alloc_sbuf_tensor("grd", [128, 8], F32).ap()
    gs16 = nc.alloc_sbuf_tensor("gs16", [128, 64], BF16).ap()
    gsc = nc.alloc_sbuf_tensor("gsc", [128, 8], F32).ap()
    gsc16 = nc.alloc_sbuf_tensor("gsc16", [128, 8], BF16).ap()
    mu_sb = nc.alloc_sbuf_tensor("mu_sb", [64, 2048], BF16).ap()
    fc_sb = nc.alloc_sbuf_tensor("fc_sb", [128, 4], F32).ap()
    nc.const_aps.register(F32, EPS, fc_sb[:, 0:1])

    SLABW = 34816
    slab_t = nc.alloc_sbuf_tensor("slab", [128, SLABW], F32)
    A = Arena(slab_t.ap(), SLABW * 4)

    x_v = A.at(0, F32, [128, NCH, 1024])
    qe_v = A.at(0, BF16, [128, 16, 1024])
    S32_v = A.at(32768, F32, [128, 16, 256])
    S16_v = A.at(49152, BF16, [128, 16, 256])
    bb_v = A.at(57344, F32, [128, 1024])
    ebm_v = A.at(61440, F32, [128, 1024])
    rb_v = A.at(32768, F32, [128, 8, 257])
    xg_v = A.at(65536, BF16, [128, NCH, 1024])
    a_v = A.at(98304, BF16, [128, IC, 1024])
    tsil_v = A.at(120832, F32, [128, 1024])
    scb_v = A.at(124928, F32, [128, 1024])
    lnx_v = [A.at(129024 + i * 2048, BF16, [128, 1024]) for i in range(2)]
    lnq_v = [A.at(133120 + i * 2048, BF16, [128, 1024]) for i in range(2)]
    k_v = A.at(65536, BF16, [128, 16, 512])
    gk_v = A.at(81920, BF16, [128, 16, 512])
    v_v = A.at(98304, BF16, [128, 4, 1024])
    oraw_v = A.at(106496, BF16, [128, 8, 1024])
    kl_tm_v = A.at(122880, BF16, [128, 16, 128])
    kd_v = A.at(126976, BF16, [128, 1024])
    at16_v = A.at(129024, BF16, [128, 64])
    qs16_v = A.at(129280, BF16, [128, 256])
    gkt_v = A.at(129792, BF16, [16, 512])
    sgt_v = A.at(130816, F32, [128, 512])
    sg2_v = A.at(132864, F32, [128, 512])
    gpp_v = A.at(134912, BF16, [128, 1024])
    osq_v = A.at(136960, F32, [128, 256])
    orst_v = A.at(138240, F32, [128, 2])
    stg_v = A.at(65536, BF16, [128, 8, 257])
    S0_v = A.at(81984, BF16, [128, 16, 256])
    acc_v = A.at(90176, F32, [128, 256])
    g_tm_v = A.at(65536, BF16, [128, 8, 1024])
    og_v = A.at(81984, BF16, [128, 8, 1024])

    pA = nc.alloc_psum_tensor("pA", [128, 1024], F32).ap()
    pB = nc.alloc_psum_tensor("pB", [128, 1024], F32).ap()
    pC = nc.alloc_psum_tensor("pC", [128, 1024], F32).ap()
    pD = nc.alloc_psum_tensor("pD", [128, 512], F32).ap()
    pT = nc.alloc_psum_tensor("pT", [128, 1024], BF16).ap()

    SEM, CNT = {}, {}
    for s in ["t", "v", "s", "p", "wq", "sq", "cc", "r0", "r1",
              "w0", "w1", "w2", "mu"]:
        SEM[s] = nc.alloc_semaphore("sem_" + s)
        CNT[s] = 0

    def inc(ins, s, n=1):
        CNT[s] += n
        ins.then_inc(SEM[s], n)
        return CNT[s]

    wctr = [0]
    ring_free = [0] * RING

    with nc.Block() as block:
        def emit(eng):
            def deco(f):
                getattr(block, eng)(f)
            return deco

        def wait_marks(e, marks):
            for s, m in marks.items():
                if m > 0:
                    e.wait_ge(SEM[s], m)

        @emit('sync')
        def _(sy):
            xr = dr["xT"].rearrange("(c p) t -> p c t", p=128)
            inc(sy.dma_start(x_v[:, 0:8, :], xr[:, 0:8, :]), 'r0', 16)
            inc(sy.dma_start(x_v[:, 8:16, :], xr[:, 8:16, :]), 'r1', 16)
            srcs = [(ident_sb, dr["ident"]), (ones_sb, dr["ones128"]),
                    (maskT_sb, dr["maskT"]), (iota_sb, dr["iota64"]),
                    (oh_sb.rearrange("s (g i) -> s g i", g=8),
                     dr["onehot_half"].rearrange("g s i -> s g i")),
                    (inv_sb, dr["inv_i"]), (bgk2_sb, dr["bgk2_t"]),
                    (wnorm_sb, dr["wnorm_t"]), (mmask_sb, dr["mmask"]),
                    (ummask_sb, dr["ummask"]), (fc_sb, dr["fconst"])]
            for dst, src in srcs:
                inc(sy.dma_start(dst, src), 'sq', 16)
        m_setup_sq = CNT['sq']

        @emit('gpsimd')
        def _(g):
            inc(g.dma_start(ring[0:64, 0:2048], dr["gate_w"]), 'wq', 16)
            i = g.dma_start(wgk1_sb.rearrange("p (k c) -> p k c", k=16),
                            dr["wgk1"].rearrange("(k p) c -> p k c", p=128))
            inc(i, 'wq', 16)
            g.memset(zz64, 0.0)
            i = g.memset(run_sb, 1.0)
            inc(i, 'p', 1)
        m_setup_wq = CNT['wq']

        @emit('vector')
        def _(v):
            v.wait_ge(SEM['sq'], m_setup_sq)
            v.tensor_copy(invf_sb, inv_sb)
            i = v.tensor_copy(invb_sb, invf_sb)
            inc(i, 'v', 1)
        m1 = CNT['v']

        @emit('tensor')
        def _(t, m1=m1):
            t.wait_ge(SEM['v'], m1)
            t.wait_ge(SEM['wq'], m_setup_wq)
            i = t.matmul(pD[0:64, 0:64], ones_sb[0:1, 0:64], invb_sb, start=True, stop=True)
            inc(i, 't', 1)
        m2 = CNT['t']

        @emit('vector')
        def _(v, m2=m2):
            v.wait_ge(SEM['t'], m2)
            i = v.tensor_scalar(P_sb, pD[0:64, 0:64], iota_sb[:, 0:1], None, ALU.is_equal)
            inc(i, 'v', 1)

        for k in range(16):
            mv = CNT['v']

            @emit('tensor')
            def _(t, k=k, mv=mv):
                t.wait_ge(SEM['v'], mv)
                i = t.matmul(pD[:, 0:64], ring[0:64, k * 128:(k + 1) * 128], P_sb,
                             start=True, stop=True)
                inc(i, 't', 1)
            mt = CNT['t']

            @emit('vector')
            def _(v, k=k, mt=mt):
                v.wait_ge(SEM['t'], mt)
                i = v.tensor_copy(gwperm_sb[:, k * 64:(k + 1) * 64], pD[:, 0:64])
                inc(i, 'v', 1)
        ring_free[0] = CNT['t']

        RSEMS = ["w0", "w1", "w2"]

        def ring_tile(src_ap, width=2048, rows=128):
            idx = wctr[0]
            wctr[0] += 1
            slot = idx % RING
            fm = ring_free[slot]
            sname = RSEMS[slot]

            @emit('gpsimd')
            def _(g, slot=slot, src_ap=src_ap, fm=fm, width=width, rows=rows,
                  sname=sname):
                if fm > 0:
                    g.wait_ge(SEM['t'], fm)
                i = g.dma_start(ring[0:rows, slot * 2048: slot * 2048 + width], src_ap)
                inc(i, sname, 16)
            return (ring[0:rows, slot * 2048: slot * 2048 + width],
                    (sname, CNT[sname]), slot)

        def emit_ln(first=False):
            bar = dict(CNT)
            tmarks = []
            for c in range(NCH):
                pp = c % 2

                @emit('vector')
                def _(v, c=c, pp=pp, tmarks=list(tmarks), bar=bar, first=first):
                    if c == 0:
                        wait_marks(v, {k: bar[k] for k in ('t', 's', 'p', 'sq')})
                    if first and c == 0:
                        v.wait_ge(SEM['r0'], 16)
                    if first and c == 8:
                        v.wait_ge(SEM['r1'], 16)
                    if c >= 2:
                        v.wait_ge(SEM['t'], tmarks[c - 2])
                    v.tensor_copy(lnx_v[pp], x_v[:, c, :])
                    i = v.tensor_tensor(lnq_v[pp], lnx_v[pp], lnx_v[pp], ALU.mult)
                    inc(i, 'v', 1)
                mv = CNT['v']

                @emit('tensor')
                def _(t, c=c, pp=pp, mv=mv, bar=bar):
                    if c == 0:
                        wait_marks(t, {k: bar[k] for k in ('s',)})
                    t.wait_ge(SEM['v'], mv)
                    st, sp = (c == 0), (c == NCH - 1)
                    t.matmul(pA[:, 0:512], ones_sb, lnx_v[pp][:, 0:512], start=st, stop=sp)
                    t.matmul(pA[:, 512:1024], ones_sb, lnx_v[pp][:, 512:1024], start=st, stop=sp)
                    t.matmul(pB[:, 0:512], ones_sb, lnq_v[pp][:, 0:512], start=st, stop=sp)
                    i = t.matmul(pB[:, 512:1024], ones_sb, lnq_v[pp][:, 512:1024], start=st, stop=sp)
                    inc(i, 't', 1)
                tmarks.append(CNT['t'])
            m_sums = CNT['t']

            @emit('vector')
            def _(v, m_sums=m_sums):
                v.wait_ge(SEM['t'], m_sums)
                v.tensor_scalar(bcA, pA, 1.0 / D, None, ALU.mult)
                v.tensor_tensor(scb_v, bcA, bcA, ALU.mult)
                v.tensor_scalar(tsil_v, pB, 1.0 / D, None, ALU.mult)
                i = v.tensor_tensor(tsil_v, tsil_v, scb_v, ALU.subtract)
                inc(i, 'v', 1)
            m_var = CNT['v']

            @emit('scalar')
            def _(s, m_var=m_var):
                s.wait_ge(SEM['v'], m_var)
                i = s.activation(tsil_v, tsil_v, AF.Sqrt, bias=EPS, scale=1.0)
                inc(i, 's', 1)
            m_sd = CNT['s']

            @emit('vector')
            def _(v, m_sd=m_sd):
                v.wait_ge(SEM['s'], m_sd)
                i = v.reciprocal(bcB, tsil_v)
                mk = inc(i, 'v', 1)
                v.wait_ge(SEM['v'], mk)
                i = v.tensor_tensor(bcA, bcA, bcB, ALU.mult)
                inc(i, 'v', 1)
            m_rows = CNT['v']

            halfmarks = []
            for half in range(2):
                hs2 = slice(half * 512, (half + 1) * 512)
                for c in range(NCH):
                    if c % 4 != 3:
                        @emit('vector')
                        def _(v, c=c, hs2=hs2, m_rows=m_rows):
                            v.wait_ge(SEM['v'], m_rows)
                            v.tensor_tensor(scb_v[:, hs2], x_v[:, c, hs2],
                                            bcB[:, hs2], ALU.mult)
                            i = v.tensor_tensor(h3[:, c, hs2], scb_v[:, hs2],
                                                bcA[:, hs2], ALU.subtract)
                            inc(i, 'v', 1)
                    else:
                        @emit('gpsimd')
                        def _(g, c=c, hs2=hs2, m_rows=m_rows):
                            g.wait_ge(SEM['v'], m_rows)
                            g.tensor_tensor(tsil_v[:, hs2], x_v[:, c, hs2],
                                            bcB[:, hs2], ALU.mult)
                            i = g.tensor_tensor(h3[:, c, hs2], tsil_v[:, hs2],
                                                bcA[:, hs2], ALU.subtract)
                            inc(i, 'p', 1)
                halfmarks.append((CNT['v'], CNT['p']))
            return halfmarks

        def emit_ffn(halfmarks):
            hv, hp = halfmarks[1]
            for tl in range(8):
                m_prev_s = CNT['s']
                m_prev_v = CNT['v']
                hvt, hpt = halfmarks[0 if tl < 4 else 1]

                @emit('tensor')
                def _(t, tl=tl, hvt=hvt, hpt=hpt, m_prev_s=m_prev_s,
                      m_prev_v=m_prev_v):
                    t.wait_ge(SEM['v'], max(hvt, m_prev_v))
                    t.wait_ge(SEM['p'], hpt)
                    t.wait_ge(SEM['s'], m_prev_s)
                    for k in range(16):
                        i = t.matmul(pD[:, 0:64], h3[:, k, tl * 128:(tl + 1) * 128],
                                     gwperm_sb[:, k * 64:(k + 1) * 64],
                                     start=(k == 0), stop=(k == 15))
                    inc(i, 't', 1)
                m_sc = CNT['t']

                @emit('vector')
                def _(v, m_sc=m_sc):
                    v.wait_ge(SEM['t'], m_sc)
                    i = v.reduce_max(gnm[:, 0:1], pD[:, 0:64], AX.X, negate=True)
                    inc(i, 'v', 1)
                m_nmx = CNT['v']

                @emit('scalar')
                def _(s, m_nmx=m_nmx):
                    s.wait_ge(SEM['v'], m_nmx)
                    i = s.activation(gel, pD[:, 0:64], AF.Exp, bias=gnm[:, 0:1], scale=1.0)
                    inc(i, 's', 1)
                m_el = CNT['s']

                @emit('vector')
                def _(v, m_el=m_el):
                    v.wait_ge(SEM['s'], m_el)
                    i = v.reduce_sum(gnm[:, 1:2], gel, AX.X)
                    mk = inc(i, 'v', 1)
                    v.wait_ge(SEM['v'], mk)
                    i = v.reciprocal(gnm[:, 2:3], gnm[:, 1:2])
                    mk = inc(i, 'v', 1)
                    v.wait_ge(SEM['v'], mk)
                    v.tensor_scalar(gpr, gel, gnm[:, 2:3], None, ALU.mult)
                    i = v.max(gt8, gpr)
                    mk = inc(i, 'v', 1)
                    v.wait_ge(SEM['v'], mk)
                    v.tensor_scalar(gm0, gpr, gt8[:, 7:8], None, ALU.is_ge)
                    v.tensor_tensor(gfl, gpr, gm0, ALU.mult)
                    v.tensor_scalar(gm0, gfl, 0.0, None, ALU.is_equal)
                    i = v.scalar_tensor_tensor(gel, gm0, -1e9, gfl, ALU.mult, ALU.add)
                    inc(i, 'v', 1)
                m_flatm = CNT['v']

                @emit('scalar')
                def _(s, m_flatm=m_flatm):
                    s.wait_ge(SEM['v'], m_flatm)
                    i = s.activation(gel, gel, AF.Exp, bias=0.0, scale=1.0)
                    inc(i, 's', 1)
                m_ef = CNT['s']

                @emit('vector')
                def _(v, m_ef=m_ef):
                    v.wait_ge(SEM['s'], m_ef)
                    i = v.tensor_reduce(gden, gel.rearrange("p (g s) -> p g s", g=8),
                                        AX.X, ALU.add)
                    mk = inc(i, 'v', 1)
                    v.wait_ge(SEM['v'], mk)
                    i = v.tensor_scalar(gden, gden, 1e-12, None, ALU.add)
                    mk = inc(i, 'v', 1)
                    v.wait_ge(SEM['v'], mk)
                    i = v.reciprocal(grd, gden)
                    mk = inc(i, 'v', 1)
                    v.wait_ge(SEM['v'], mk)
                    for g in range(8):
                        v.tensor_scalar(gs16[:, 8 * g:8 * g + 8], gel[:, 8 * g:8 * g + 8],
                                        grd[:, g:g + 1], None, ALU.mult)
                    i = v.tensor_reduce(gsc, gfl.rearrange("p (g s) -> p g s", g=8),
                                        AX.X, ALU.add)
                    mk = inc(i, 'v', 1)
                    v.wait_ge(SEM['v'], mk)
                    i = v.tensor_copy(gsc16, gsc)
                    inc(i, 'v', 1)
                m_soft = CNT['v']

                @emit('tensor')
                def _(t, tl=tl, m_soft=m_soft):
                    t.wait_ge(SEM['v'], m_soft)
                    t.transpose(pT[0:64, 0:128], gs16, ident_sb)
                    i = t.transpose(pT[0:8, 128:256], gsc16, ident_sb)
                    inc(i, 't', 1)
                m_tr = CNT['t']

                @emit('vector')
                def _(v, tl=tl, m_tr=m_tr):
                    v.wait_ge(SEM['t'], m_tr)
                    v.tensor_copy(softT_sb[:, tl * 128:(tl + 1) * 128], pT[0:64, 0:128])
                    i = v.tensor_copy(scalT_sb[:, tl * 128:(tl + 1) * 128], pT[0:8, 128:256])
                    inc(i, 'v', 1)
            m_gating = CNT['v']

            m_xg_free_t = CNT['t']
            m_scb_free = CNT['v']
            m_dn = [0] * 16
            m_sil = [0] * IC
            m_a = [0] * IC
            mu_wq = 0
            for g in range(8):
                m_mu_reader = CNT['t']

                @emit('gpsimd')
                def _(gg, g=g, m_mu_reader=m_mu_reader):
                    gg.wait_ge(SEM['t'], m_mu_reader)
                    i = gg.dma_start(mu_sb, dr["mu_pair"][g])
                    inc(i, 'mu', 16)
                mu_wq = CNT['mu']

                @emit('tensor')
                def _(t, g=g, m_gating=m_gating, m_dnv=list(m_dn)):
                    t.wait_ge(SEM['v'], m_gating)
                    if m_dnv[14]:
                        t.wait_ge(SEM['v'], m_dnv[14])
                    if m_dnv[15]:
                        t.wait_ge(SEM['v'], m_dnv[15])
                    t.matmul(pA[:, 0:512], oh_sb[:, g * 128:(g + 1) * 128],
                             scalT_sb[:, 0:512], start=True, stop=True)
                    i = t.matmul(pA[:, 512:1024], oh_sb[:, g * 128:(g + 1) * 128],
                                 scalT_sb[:, 512:1024], start=True, stop=True)
                    inc(i, 't', 1)
                m_scbt = CNT['t']

                @emit('vector')
                def _(v, m_scbt=m_scbt, m_a10=m_a[IC - 1]):
                    v.wait_ge(SEM['t'], m_scbt)
                    if m_a10:
                        v.wait_ge(SEM['v'], m_a10)
                    i = v.tensor_copy(scb_v, pA)
                    inc(i, 'v', 1)
                m_scb = CNT['v']

                m_xg = [0] * NCH
                for c in range(NCH):
                    pX = pA if c % 2 == 0 else pB

                    @emit('tensor')
                    def _(t, g=g, c=c, pX=pX, mu_wq=mu_wq, m_scb=m_scb,
                          m_xgf=m_xg_free_t, m_xg=list(m_xg)):
                        t.wait_ge(SEM['mu'], mu_wq)
                        t.wait_ge(SEM['t'], m_xgf)
                        if c == 0:
                            t.wait_ge(SEM['v'], m_scb)
                        if c >= 2:
                            t.wait_ge(SEM['v'], m_xg[c - 2])
                        lhsT = mu_sb[:, c * 128:(c + 1) * 128]
                        t.matmul(pX[:, 0:512], lhsT, softT_sb[:, 0:512], start=True, stop=True)
                        i = t.matmul(pX[:, 512:1024], lhsT, softT_sb[:, 512:1024],
                                     start=True, stop=True)
                        inc(i, 't', 1)
                    m_dc = CNT['t']

                    @emit('vector')
                    def _(v, c=c, pX=pX, m_dc=m_dc):
                        v.wait_ge(SEM['t'], m_dc)
                        i = v.tensor_tensor(xg_v[:, c, :], h3[:, c, :], pX, ALU.add)
                        inc(i, 'v', 1)
                    m_xg[c] = CNT['v']
                m_xg_done = CNT['v']

                banks3 = [pA, pB, pC]
                bfree = {0: ('v', m_xg[14]), 1: ('v', m_xg[15]),
                         2: ('v', m_a[IC - 1])}
                for oc in range(IC):
                    gi = (2 * oc) % 3
                    ui = (2 * oc + 1) % 3
                    pG = banks3[gi]
                    pU = banks3[ui]
                    gw_tile, gw_wq, gw_slot = ring_tile(dr["gw_l"][g][oc])

                    @emit('tensor')
                    def _(t, g=g, oc=oc, pG=pG, gw_wq=gw_wq, m_xg_done=m_xg_done,
                          bf=bfree[gi]):
                        t.wait_ge(SEM[gw_wq[0]], gw_wq[1])
                        t.wait_ge(SEM['v'], m_xg_done)
                        if bf[1]:
                            t.wait_ge(SEM[bf[0]], bf[1])
                        for k in range(16):
                            t.matmul(pG[:, 0:512], gw_tile[:, k * 128:(k + 1) * 128],
                                     xg_v[:, k, 0:512], start=(k == 0), stop=(k == 15))
                            i = t.matmul(pG[:, 512:1024], gw_tile[:, k * 128:(k + 1) * 128],
                                         xg_v[:, k, 512:1024], start=(k == 0), stop=(k == 15))
                        inc(i, 't', 1)
                    ring_free[gw_slot] = CNT['t']
                    m_gt = CNT['t']

                    @emit('scalar')
                    def _(s, oc=oc, pG=pG, m_gt=m_gt):
                        s.wait_ge(SEM['t'], m_gt)
                        i = s.activation(tsil_v, pG, AF.Silu, bias=0.0, scale=1.0)
                        inc(i, 's', 1)
                    m_sil[oc] = CNT['s']
                    bfree[gi] = ('s', m_sil[oc])

                    uw_tile, uw_wq, uw_slot = ring_tile(dr["uw_l"][g][oc])

                    @emit('tensor')
                    def _(t, g=g, oc=oc, pU=pU, uw_wq=uw_wq, bf=bfree[ui],
                          m_xg_done=m_xg_done):
                        t.wait_ge(SEM[uw_wq[0]], uw_wq[1])
                        t.wait_ge(SEM['v'], m_xg_done)
                        if bf[1]:
                            t.wait_ge(SEM[bf[0]], bf[1])
                        for k in range(16):
                            t.matmul(pU[:, 0:512], uw_tile[:, k * 128:(k + 1) * 128],
                                     xg_v[:, k, 0:512], start=(k == 0), stop=(k == 15))
                            i = t.matmul(pU[:, 512:1024], uw_tile[:, k * 128:(k + 1) * 128],
                                         xg_v[:, k, 512:1024], start=(k == 0), stop=(k == 15))
                        inc(i, 't', 1)
                    ring_free[uw_slot] = CNT['t']
                    m_up = CNT['t']

                    @emit('vector')
                    def _(v, oc=oc, pU=pU, m_up=m_up, m_silo=m_sil[oc], m_scb=m_scb):
                        v.wait_ge(SEM['t'], m_up)
                        v.wait_ge(SEM['s'], m_silo)
                        v.wait_ge(SEM['v'], m_scb)
                        i = v.tensor_tensor(tsil_v, tsil_v, pU, ALU.mult)
                        mk = inc(i, 'v', 1)
                        v.wait_ge(SEM['v'], mk)
                        i = v.tensor_tensor(a_v[:, oc, :], tsil_v, scb_v, ALU.mult)
                        inc(i, 'v', 1)
                    m_a[oc] = CNT['v']
                    bfree[ui] = ('v', m_a[oc])
                m_a_done = CNT['v']

                m_dn = [0] * 16
                for hc in range(16):
                    dw_tile, dw_wq, dw_slot = ring_tile(dr["dw_l"][g][hc], width=1408)
                    pX = pA if hc % 2 == 0 else pB

                    @emit('tensor')
                    def _(t, g=g, hc=hc, pX=pX, dw_wq=dw_wq, m_a_done=m_a_done,
                          m_sil_last=m_sil[IC - 1], m_dn2=(m_dn[hc - 2] if hc >= 2 else 0)):
                        t.wait_ge(SEM[dw_wq[0]], dw_wq[1])
                        t.wait_ge(SEM['v'], m_a_done)
                        t.wait_ge(SEM['s'], m_sil_last)
                        if m_dn2:
                            t.wait_ge(SEM['v'], m_dn2)
                        for ik in range(IC):
                            t.matmul(pX[:, 0:512], dw_tile[:, ik * 128:(ik + 1) * 128],
                                     a_v[:, ik, 0:512], start=(ik == 0), stop=(ik == IC - 1))
                            i = t.matmul(pX[:, 512:1024], dw_tile[:, ik * 128:(ik + 1) * 128],
                                         a_v[:, ik, 512:1024], start=(ik == 0), stop=(ik == IC - 1))
                        inc(i, 't', 1)
                    ring_free[dw_slot] = CNT['t']
                    m_dt = CNT['t']

                    @emit('vector')
                    def _(v, hc=hc, pX=pX, m_dt=m_dt):
                        v.wait_ge(SEM['t'], m_dt)
                        i = v.tensor_tensor(x_v[:, hc, :], x_v[:, hc, :], pX, ALU.add)
                        inc(i, 'v', 1)
                    m_dn[hc] = CNT['v']
                m_xg_free_t = CNT['t']
            return list(m_dn)


        def emit_attention():
            kdv = kd_v.rearrange("p (q t) -> p q t", q=16)
            klv = gpp_v.rearrange("p (q t) -> p q t", q=16)
            bbq = bb_v.rearrange("p (q t) -> p q t", q=16)
            ebq = ebm_v.rearrange("p (q t) -> p q t", q=16)
            ballv = Ball_sb.rearrange("p (h c d) -> p h c d", h=4, c=16)

            bar = dict(CNT)

            @emit('sync')
            def _(sy, bar=bar):
                wait_marks(sy, {k: bar[k] for k in ('t', 'v', 's', 'p')})
                i = sy.dma_start(xspill_d, A.at(0, F32, [128, 16384]))
                inc(i, 'sq', 16)
            m_spill = CNT['sq']

            @emit('gpsimd')
            def _(g, m_spill=m_spill):
                g.wait_ge(SEM['sq'], m_spill)
                g.memset(S32_v, 0.0)
                g.memset(S16_v, 0.0)
                i = g.memset(Bsum_sb, 0.0)
                inc(i, 'p', 1)
            m_zero = CNT['p']

            if ATTN_PH < 2:
                return
            psl = [pA[:, 0:512], pA[:, 512:1024], pB[:, 0:512], pB[:, 512:1024]]
            ps_free = [0, 0, 0, 0]
            ps_free_sem = ['s', 's', 's', 's']
            m_s16 = [('p', m_zero)] * NH
            m_exps_prev = 0
            m_oev = [0] * NH
            m_oev_ph = [0, 0]
            m_supd = [0] * NH
            m_supd_last = 0
            m_atm_prev = 0
            m_ptfree = [0, 0]
            m_qe_last = [0]

            for hf in range(2):
                hs = slice(hf * 512, hf * 512 + 512)
                wgk2_t, wgk2_wq, wgk2_slot = ring_tile(dr["wgk2"], width=2048, rows=16)
                m_pD_free = m_atm_prev

                @emit('tensor')
                def _(t, hf=hf, hs=hs, wgk2_wq=wgk2_wq, bar=bar, m_pD_free=m_pD_free):
                    if hf == 0:
                        wait_marks(t, {k: bar[k] for k in ('v', 's', 'p')})
                    t.wait_ge(SEM[wgk2_wq[0]], wgk2_wq[1])
                    if m_pD_free:
                        t.wait_ge(SEM['v'], m_pD_free)
                    for k in range(16):
                        i = t.matmul(pD[0:16, 0:512], wgk1_sb[:, k * 16:(k + 1) * 16],
                                     h3[:, k, hs], start=(k == 0), stop=(k == 15))
                    inc(i, 't', 1)
                m_gk1 = CNT['t']

                @emit('vector')
                def _(v, m_gk1=m_gk1):
                    v.wait_ge(SEM['t'], m_gk1)
                    i = v.tensor_copy(gkt_v, pD[0:16, 0:512])
                    inc(i, 'v', 1)
                m_gkt = CNT['v']

                m_sig_prev = 0
                for op_ in range(8):
                    for half in range(2):
                        oc = op_ * 2 + half
                        pX = pC[:, half * 512:half * 512 + 512]

                        @emit('tensor')
                        def _(t, oc=oc, pX=pX, m_gkt=m_gkt, wgk2_slot=wgk2_slot,
                              m_prev=m_sig_prev):
                            t.wait_ge(SEM['v'], m_gkt)
                            if m_prev > 0:
                                t.wait_ge(SEM['s'], m_prev)
                            i = t.matmul(pX, ring[0:16, wgk2_slot * 2048 + oc * 128:
                                                  wgk2_slot * 2048 + (oc + 1) * 128],
                                         gkt_v, start=True, stop=True)
                            inc(i, 't', 1)
                    m_g2 = CNT['t']

                    @emit('scalar')
                    def _(sc, op_=op_, m_g2=m_g2):
                        sc.wait_ge(SEM['t'], m_g2)
                        sc.activation(sgt_v, pC[:, 0:512], AF.Sigmoid,
                                      bias=bgk2_sb[:, 2 * op_:2 * op_ + 1], scale=1.0)
                        i = sc.activation(sg2_v, pC[:, 512:1024], AF.Sigmoid,
                                          bias=bgk2_sb[:, 2 * op_ + 1:2 * op_ + 2], scale=1.0)
                        mk = inc(i, 's', 1)
                        sc.activation(gk_v[:, 2 * op_, :], sgt_v, AF.Ln, bias=0.0, scale=1.0)
                        i = sc.activation(gk_v[:, 2 * op_ + 1, :], sg2_v, AF.Ln,
                                          bias=0.0, scale=1.0)
                        inc(i, 's', 1)
                    m_sig_prev = CNT['s'] - 1
                ring_free[wgk2_slot] = CNT['t']
                m_gk_done = CNT['s']

                for which, wsrc, scale_ in (("q", "wq_l", QSCALE), ("k", "wk_l", 1.0)):
                    for oc in range(16):
                        tile, twq, slot = ring_tile(dr[wsrc][oc])
                        sl = oc % 4

                        @emit('tensor')
                        def _(t, oc=oc, sl=sl, tile=tile, twq=twq,
                              fm=ps_free[sl], fsem=ps_free_sem[sl], hs=hs):
                            t.wait_ge(SEM[twq[0]], twq[1])
                            if fm:
                                t.wait_ge(SEM[fsem], fm)
                            for k in range(16):
                                i = t.matmul(psl[sl], tile[:, k * 128:(k + 1) * 128],
                                             h3[:, k, hs], start=(k == 0), stop=(k == 15))
                            inc(i, 't', 1)
                        ring_free[slot] = CNT['t']
                        m_mm = CNT['t']

                        @emit('vector')
                        def _(v, which=which, oc=oc, sl=sl, m_mm=m_mm, hf=hf,
                              scale_=scale_, m_spill=m_spill):
                            v.wait_ge(SEM['t'], m_mm)
                            if which == "q":
                                v.wait_ge(SEM['sq'], m_spill)
                            dstv = (qe_v[:, oc, hf * 512:hf * 512 + 512] if which == "q"
                                    else k_v[:, oc, :])
                            if which == "q":
                                i = v.tensor_scalar(dstv, psl[sl], scale_, None, ALU.mult)
                            else:
                                i = v.tensor_copy(dstv, psl[sl])
                            inc(i, 'v', 1)
                        ps_free[sl] = CNT['v']
                        ps_free_sem[sl] = 'v'

                for oc in range(8):
                    tile, twq, slot = ring_tile(dr["wv_l"][oc])
                    sl = oc % 4

                    @emit('tensor')
                    def _(t, oc=oc, sl=sl, tile=tile, twq=twq,
                          fm=ps_free[sl], fsem=ps_free_sem[sl], hs=hs):
                        t.wait_ge(SEM[twq[0]], twq[1])
                        if fm:
                            t.wait_ge(SEM[fsem], fm)
                        for k in range(16):
                            i = t.matmul(psl[sl], tile[:, k * 128:(k + 1) * 128],
                                         h3[:, k, hs], start=(k == 0), stop=(k == 15))
                        inc(i, 't', 1)
                    ring_free[slot] = CNT['t']
                    m_mm = CNT['t']

                    @emit('vector')
                    def _(v, sl=sl, m_mm=m_mm, m_gk_done=m_gk_done):
                        v.wait_ge(SEM['t'], m_mm)
                        v.wait_ge(SEM['s'], m_gk_done)
                        i = v.tensor_copy(gpp_v[:, 0:512], psl[sl])
                        inc(i, 'v', 1)
                    ps_free[sl] = CNT['v']
                    ps_free_sem[sl] = 'v'
                    m_ev = CNT['v']

                    @emit('tensor')
                    def _(t, m_ev=m_ev, m_ptf=m_ptfree[0]):
                        t.wait_ge(SEM['v'], m_ev)
                        if m_ptf:
                            t.wait_ge(SEM['v'], m_ptf)
                        for j in range(4):
                            i = t.transpose(pT[:, j * 128:(j + 1) * 128],
                                            gpp_v[:, j * 128:(j + 1) * 128], ident_sb)
                        inc(i, 't', 1)
                    m_tr = CNT['t']

                    @emit('vector')
                    def _(v, oc=oc, m_tr=m_tr):
                        v.wait_ge(SEM['t'], m_tr)
                        i = v.tensor_copy(v_v[:, 0:4, oc * 128:(oc + 1) * 128],
                                          pT.rearrange("p (j t) -> p j t", j=8)[:, 0:4, :])
                        inc(i, 'v', 1)
                    m_ptfree[0] = CNT['v']

                for c in range(8):
                    ci = hf * 8 + c

                    @emit('vector')
                    def _(v, c=c, ci=ci, m_exps_prev=m_exps_prev, m_gk_done=m_gk_done,
                          m_zero=m_zero):
                        v.wait_ge(SEM['s'], max(m_exps_prev, m_gk_done))
                        v.wait_ge(SEM['p'], m_zero)
                        v.tensor_copy(ballv[:, :, ci, :],
                                      Bsum_sb.rearrange("p (h d) -> p h d", h=4))
                        for q in range(16):
                            i = v.tensor_tensor_scan(bbq[:, q, :],
                                                     gk_v[:, q, c * 64:(c + 1) * 64],
                                                     zz64, 0.0, ALU.add, ALU.add)
                        mk = inc(i, 'v', 1)
                        v.wait_ge(SEM['v'], mk)
                        i = v.tensor_tensor(Bsum_sb, Bsum_sb, bbq[:, :, 63], ALU.add)
                        inc(i, 'v', 1)
                    m_b = CNT['v']

                    @emit('scalar')
                    def _(sc, m_b=m_b):
                        sc.wait_ge(SEM['v'], m_b)
                        sc.activation(ebl_sb, bbq[:, :, 63], AF.Exp, bias=0.0, scale=R16)
                        i = sc.activation(ebm_v, bb_v, AF.Exp, bias=0.0, scale=R16)
                        inc(i, 's', 1)
                    m_eb = CNT['s']

                    @emit('gpsimd')
                    def _(g, ci=ci, m_eb=m_eb):
                        g.wait_ge(SEM['s'], m_eb)
                        i = g.tensor_tensor(
                            qe_v[:, :, ci * 64:ci * 64 + 64],
                            qe_v[:, :, ci * 64:ci * 64 + 64],
                            ebq, ALU.mult)
                        inc(i, 'p', 1)
                    m_qe = CNT['p']
                    m_qe_last[0] = m_qe

                    @emit('scalar')
                    def _(sc, m_qe=m_qe):
                        sc.wait_ge(SEM['p'], m_qe)
                        i = sc.activation(ebm_v, bb_v, AF.Exp, bias=0.0, scale=-R16)
                        inc(i, 's', 1)
                    m_em = CNT['s']
                    m_exps_prev = m_em

                    @emit('vector')
                    def _(v, c=c, m_em=m_em):
                        v.wait_ge(SEM['s'], m_em)
                        v.tensor_tensor(kdv, k_v[:, :, c * 64:c * 64 + 64],
                                        ebq, ALU.mult)
                        for q in range(16):
                            i = v.tensor_scalar(klv[:, q, :], kdv[:, q, :],
                                                ebl_sb[:, q:q + 1], None, ALU.mult)
                        inc(i, 'v', 1)
                    m_kl = CNT['v']

                    for h in range(NH):
                        ph = h % 2

                        b0 = 64 * (c % 2)

                        @emit('tensor')
                        def _(t, h=h, ph=ph, b0=b0, m_kl=m_kl, m_ptf=m_ptfree[ph]):
                            t.wait_ge(SEM['v'], m_kl)
                            if m_ptf:
                                t.wait_ge(SEM['v'], m_ptf)
                            for dc in range(4):
                                i = t.transpose(
                                    pT[b0:b0 + 64,
                                       ph * 512 + dc * 128:ph * 512 + (dc + 1) * 128],
                                    gpp_v[:, (4 * h + dc) * 64:(4 * h + dc + 1) * 64],
                                    ident_sb, tile_position=(0, b0))
                            inc(i, 't', 1)
                        m_trh = CNT['t']

                        @emit('vector')
                        def _(v, h=h, ph=ph, b0=b0, m_trh=m_trh):
                            v.wait_ge(SEM['t'], m_trh)
                            i = v.tensor_copy(kl_tm_v[b0:b0 + 64, 4 * h:4 * h + 4, :],
                                              pT.rearrange("p (j t) -> p j t", j=8)
                                              [b0:b0 + 64, ph * 4:ph * 4 + 4, :])
                            inc(i, 'v', 1)
                        m_klev = CNT['v']
                        m_ptfree[ph] = m_klev

                        @emit('tensor')
                        def _(t, h=h, ci=ci, b0=b0, m_kl=m_kl, m_atm_prev=m_atm_prev,
                              m_qe=m_qe):
                            if m_atm_prev:
                                t.wait_ge(SEM['v'], m_atm_prev)
                            t.wait_ge(SEM['v'], m_kl)
                            t.wait_ge(SEM['p'], m_qe)
                            for dc in range(4):
                                i = t.matmul(pD[b0:b0 + 64, 0:64], kdv[:, 4 * h + dc, :],
                                             qe_v[:, 4 * h + dc, ci * 64:ci * 64 + 64],
                                             start=(dc == 0), stop=(dc == 3),
                                             tile_position=(0, b0))
                            inc(i, 't', 1)
                        m_at = CNT['t']

                        @emit('vector')
                        def _(v, b0=b0, m_at=m_at):
                            v.wait_ge(SEM['t'], m_at)
                            i = v.tensor_tensor(at16_v[b0:b0 + 64, :],
                                                pD[b0:b0 + 64, 0:64],
                                                maskT_sb[b0:b0 + 64, :], ALU.mult)
                            inc(i, 'v', 1)
                        m_atm = CNT['v']
                        m_atm_prev = m_atm

                        @emit('tensor')
                        def _(t, h=h, c=c, ci=ci, ph=ph, m_atm=m_atm,
                              m_s16h=m_s16[h], m_oev2=m_oev_ph[ph]):
                            t.wait_ge(SEM['v'], m_atm)
                            t.wait_ge(SEM[m_s16h[0]], m_s16h[1])
                            if m_oev2:
                                t.wait_ge(SEM['v'], m_oev2)
                            b0_ = 64 * (c % 2)
                            vtm = v_v[b0_:b0_ + 64, c // 2, h * 256:(h + 1) * 256]
                            t.matmul(pC[b0_:b0_ + 64, ph * 512:ph * 512 + 256],
                                     at16_v[b0_:b0_ + 64, :], vtm,
                                     start=True, stop=False, tile_position=(b0_, b0_))
                            for dc in range(4):
                                i = t.matmul(pC[b0_:b0_ + 64, ph * 512:ph * 512 + 256],
                                             qe_v[:, 4 * h + dc, ci * 64:ci * 64 + 64],
                                             S16_v[:, 4 * h + dc, :],
                                             start=False, stop=(dc == 3),
                                             tile_position=(0, b0_))
                            inc(i, 't', 1)
                        m_o = CNT['t']

                        @emit('vector')
                        def _(v, h=h, ci=ci, ph=ph, m_o=m_o):
                            v.wait_ge(SEM['t'], m_o)
                            b0_ = 64 * (ci % 2)
                            i = v.tensor_copy(
                                oraw_v[b0_:b0_ + 64, ci // 2, h * 256:(h + 1) * 256],
                                pC[b0_:b0_ + 64, ph * 512:ph * 512 + 256])
                            inc(i, 'v', 1)
                        m_oev[h] = CNT['v']
                        m_oev_ph[ph] = CNT['v']

                        @emit('tensor')
                        def _(t, h=h, c=c, b0=b0, m_klev=m_klev, m_supd_prev=m_supd_last):
                            t.wait_ge(SEM['v'], m_klev)
                            if m_supd_prev:
                                t.wait_ge(SEM['v'], m_supd_prev)
                            vtm = v_v[b0:b0 + 64, c // 2, h * 256:(h + 1) * 256]
                            for dc in range(4):
                                i = t.matmul(pA[:, dc * 256:(dc + 1) * 256],
                                             kl_tm_v[b0:b0 + 64, 4 * h + dc, :], vtm,
                                             start=True, stop=True)
                            inc(i, 't', 1)
                        m_M = CNT['t']

                        @emit('vector')
                        def _(v, h=h, m_M=m_M):
                            v.wait_ge(SEM['t'], m_M)
                            for dc in range(4):
                                i = v.scalar_tensor_tensor(
                                    S32_v[:, 4 * h + dc, :], S32_v[:, 4 * h + dc, :],
                                    ebl_sb[:, 4 * h + dc:4 * h + dc + 1],
                                    pA[:, dc * 256:(dc + 1) * 256], ALU.mult, ALU.add)
                            inc(i, 'v', 1)
                        m_supd[h] = CNT['v']
                        m_supd_last = CNT['v']

                        if h == NH - 1:
                            @emit('vector')
                            def _(v, m_o=m_o):
                                v.wait_ge(SEM['t'], m_o)
                                i = v.tensor_copy(S16_v, S32_v)
                                inc(i, 'v', 1)
                            for hh in range(NH):
                                m_s16[hh] = ('v', CNT['v'])

            if ATTN_PH < 3:
                return
            m_scan_t = CNT['t']
            m_scan_v = CNT['v']
            ps_free = [m_scan_v] * 4
            ps_free_sem = ['v'] * 4

            @emit('scalar')
            def _(sc, m_scan_v=m_scan_v):
                sc.wait_ge(SEM['v'], m_scan_v)
                i = sc.activation(dall_sb, Bsum_sb, AF.Exp, bias=0.0, scale=R16)
                inc(i, 's', 1)
            m_dall = CNT['s']

            m_exp_dma = 0
            for hd in range(2):
                @emit('vector')
                def _(v, hd=hd, m_dall=m_dall, m_scan_t=m_scan_t, m_prev=m_exp_dma):
                    v.wait_ge(SEM['s'], m_dall)
                    v.wait_ge(SEM['t'], m_scan_t)
                    if m_prev:
                        v.wait_ge(SEM['sq'], m_prev)
                    v.tensor_copy(stg_v[:, :, 0:256], S32_v[:, hd * 8:(hd + 1) * 8, :])
                    i = v.tensor_copy(stg_v[:, :, 256], dall_sb[:, hd * 8:(hd + 1) * 8])
                    inc(i, 'v', 1)
                m_st = CNT['v']

                @emit('sync')
                def _(sy, hd=hd, m_st=m_st):
                    sy.wait_ge(SEM['v'], m_st)
                    i = sy.dma_start(
                        comm_in.rearrange("(q p) w -> p q w", p=128)
                        [:, hd * 8:(hd + 1) * 8, :],
                        stg_v)
                    inc(i, 'sq', 16)
                m_exp_dma = CNT['sq']
            m_export = CNT['sq']

            @emit('gpsimd')
            def _(g, m_export=m_export):
                g.wait_ge(SEM['sq'], m_export)
                i = g.collective_compute(
                    "AllGather", ALU.bypass, replica_groups=[list(range(8))],
                    ins=[comm_in], outs=[comm_out])
                inc(i, 'cc', 1)
            m_cc = CNT['cc']

            if ATTN_PH < 4:
                return
            for oc in range(8):
                tile, twq, slot = ring_tile(dr["wg_l"][oc])
                for hf in range(2):
                    sl = (oc * 2 + hf) % 4

                    @emit('tensor')
                    def _(t, oc=oc, hf=hf, sl=sl, tile=tile, twq=twq,
                          fm=ps_free[sl], fsem=ps_free_sem[sl]):
                        t.wait_ge(SEM[twq[0]], twq[1])
                        if fm:
                            t.wait_ge(SEM[fsem], fm)
                        for k in range(16):
                            i = t.matmul(psl[sl], tile[:, k * 128:(k + 1) * 128],
                                         h3[:, k, hf * 512:hf * 512 + 512],
                                         start=(k == 0), stop=(k == 15))
                        inc(i, 't', 1)
                    ring_free[slot] = CNT['t']
                    m_mm = CNT['t']

                    @emit('vector')
                    def _(v, sl=sl, m_mm=m_mm, m_export=m_export):
                        v.wait_ge(SEM['t'], m_mm)
                        v.wait_ge(SEM['sq'], m_export)
                        i = v.tensor_copy(gpp_v[:, 0:512], psl[sl])
                        inc(i, 'v', 1)
                    ps_free[sl] = CNT['v']
                    ps_free_sem[sl] = 'v'
                    m_ev = CNT['v']

                    @emit('tensor')
                    def _(t, m_ev=m_ev, m_ptf=m_ptfree[0]):
                        t.wait_ge(SEM['v'], m_ev)
                        if m_ptf:
                            t.wait_ge(SEM['v'], m_ptf)
                        for j in range(4):
                            i = t.transpose(pT[:, j * 128:(j + 1) * 128],
                                            gpp_v[:, j * 128:(j + 1) * 128], ident_sb)
                        inc(i, 't', 1)
                    m_tr = CNT['t']

                    @emit('vector')
                    def _(v, oc=oc, hf=hf, m_tr=m_tr):
                        v.wait_ge(SEM['t'], m_tr)
                        i = v.tensor_copy(
                            g_tm_v[:, hf * 4:hf * 4 + 4, oc * 128:(oc + 1) * 128],
                            pT.rearrange("p (j t) -> p j t", j=8)[:, 0:4, :])
                        inc(i, 'v', 1)
                    m_ptfree[0] = CNT['v']
            m_gproj = CNT['v']

            if ATTN_PH < 5:
                return
            rbh_v = [A.at(32768, BF16, [128, 8, 257]), A.at(41088, BF16, [128, 8, 257])]
            acch_v = A.at(49408, F32, [128, 8, 256])
            comm_rv = comm_out.rearrange("(r q p) w -> r p q w", r=8, q=16)
            m_rb_war = [0, 0]
            m_s0h = []

            def emit_comb_half(hcs):
                @emit('vector')
                def _(v, hcs=hcs, m_dall=m_dall, m_scan_t=m_scan_t):
                    if hcs == 0:
                        v.wait_ge(SEM['s'], m_dall)
                        v.wait_ge(SEM['t'], m_scan_t)
                    v.memset(acch_v, 0.0)
                    i = v.memset(runA_sb, 1.0)
                    inc(i, 'v', 1)

                for pi, p in enumerate(range(7, -1, -1)):
                    par = pi % 2
                    rsem = 'r0' if par == 0 else 'r1'

                    @emit('sync')
                    def _(sy, hcs=hcs, p=p, par=par, rsem=rsem, m_cc=m_cc,
                          m_rbw=m_rb_war[par], m_scan_v=m_scan_v):
                        sy.wait_ge(SEM['cc'], m_cc)
                        if m_rbw:
                            sy.wait_ge(SEM['v'], m_rbw)
                        else:
                            sy.wait_ge(SEM['v'], m_scan_v)
                        for q in range(8):
                            i = sy.dma_start(rbh_v[par][:, q, :],
                                             comm_rv[p][:, hcs * 8 + q, :])
                            inc(i, rsem, 16)
                    m_dma = CNT[rsem]

                    @emit('vector')
                    def _(v, p=p, par=par, rsem=rsem, m_dma=m_dma, m_prev=CNT['v']):
                        v.wait_ge(SEM[rsem], m_dma)
                        v.wait_ge(SEM['v'], m_prev)
                        i = v.tensor_scalar(escA_sb, runA_sb, mmask_sb[:, p:p + 1],
                                            None, ALU.mult)
                        mk = inc(i, 'v', 1)
                        v.wait_ge(SEM['v'], mk)
                        for q in range(8):
                            i = v.scalar_tensor_tensor(acch_v[:, q, :],
                                                       rbh_v[par][:, q, 0:256],
                                                       escA_sb[:, q:q + 1],
                                                       acch_v[:, q, :],
                                                       ALU.mult, ALU.add)
                        mk = inc(i, 'v', 1)
                        v.wait_ge(SEM['v'], mk)
                        i = v.tensor_scalar(escA_sb, rbh_v[par][:, :, 256],
                                            mmask_sb[:, p:p + 1],
                                            ummask_sb[:, p:p + 1], ALU.mult, ALU.add)
                        mk = inc(i, 'v', 1)
                        v.wait_ge(SEM['v'], mk)
                        i = v.tensor_tensor(runA_sb, runA_sb, escA_sb, ALU.mult)
                        inc(i, 'v', 1)
                    m_rb_war[par] = CNT['v']

                @emit('vector')
                def _(v, hcs=hcs):
                    i = v.tensor_copy(S0_v[:, hcs * 8:hcs * 8 + 8, :], acch_v)
                    inc(i, 'v', 1)
                m_s0h.append(CNT['v'])

            @emit('scalar')
            def _(sc, m_scan_v=m_scan_v, m_dall=m_dall, m_qel=m_qe_last[0]):
                sc.wait_ge(SEM['v'], m_scan_v)
                sc.wait_ge(SEM['s'], m_dall)
                sc.wait_ge(SEM['p'], m_qel)
                i = sc.activation(ebm_v[:, 0:256], Ball_sb, AF.Exp, bias=0.0, scale=R16)
                inc(i, 's', 1)
            m_ebB = CNT['s']
            eb2 = ebm_v[:, 0:256].rearrange("p (h c d) -> p h c d", h=4, c=16)

            qs2 = [qs16_v, A.at(129792, BF16, [128, 256])]
            m_qsf = [m_scan_t, m_scan_t]
            m_o3ev = [m_scan_v, m_scan_v]

            def emit_pb(h):
                for ci in range(16):
                    par = ci % 2
                    b0 = 64 * par

                    @emit('vector')
                    def _(v, ci=ci, h=h, par=par, m_ebB=m_ebB, m_qsfp=m_qsf[par]):
                        v.wait_ge(SEM['s'], m_ebB)
                        if m_qsfp:
                            v.wait_ge(SEM['t'], m_qsfp)
                        for dc in range(4):
                            i = v.tensor_scalar(
                                qs2[par][:, dc * 64:(dc + 1) * 64],
                                qe_v[:, 4 * h + dc, ci * 64:ci * 64 + 64],
                                eb2[:, h, ci, dc:dc + 1], None, ALU.mult)
                        inc(i, 'v', 1)
                    m_qs = CNT['v']

                    @emit('tensor')
                    def _(t, ci=ci, h=h, par=par, b0=b0, m_qs=m_qs,
                          m_oev2=m_o3ev[par], m_s0=m_s0h[h // 2]):
                        t.wait_ge(SEM['v'], m_qs)
                        t.wait_ge(SEM['v'], m_s0)
                        t.wait_ge(SEM['v'], m_oev2)
                        for dc in range(4):
                            i = t.matmul(pC[b0:b0 + 64, par * 512:par * 512 + 256],
                                         qs2[par][:, dc * 64:(dc + 1) * 64],
                                         S0_v[:, 4 * h + dc, :],
                                         start=(dc == 0), stop=(dc == 3),
                                         tile_position=(0, b0))
                        inc(i, 't', 1)
                    m_qsf[par] = CNT['t']
                    m_o3 = CNT['t']

                    @emit('vector')
                    def _(v, ci=ci, h=h, par=par, b0=b0, m_o3=m_o3):
                        v.wait_ge(SEM['t'], m_o3)
                        sl_ = oraw_v[b0:b0 + 64, ci // 2, h * 256:(h + 1) * 256]
                        i = v.tensor_tensor(sl_, sl_,
                                            pC[b0:b0 + 64, par * 512:par * 512 + 256],
                                            ALU.add)
                        inc(i, 'v', 1)
                    m_o3ev[par] = CNT['v']

            for hcs in range(2):
                emit_comb_half(hcs)
                emit_pb(2 * hcs)
                emit_pb(2 * hcs + 1)
            m_comb = CNT['v']
            m_qs_free_t = max(m_qsf)
            m_phaseB = CNT['v']

            if ATTN_PH < 7:
                return
            sumsq_v = A.at(138240, F32, [128, 32])
            rstd_v = A.at(138496, F32, [128, 32])

            @emit('scalar')
            def _(sc, m_phaseB=m_phaseB, m_gproj=m_gproj):
                sc.wait_ge(SEM['v'], max(m_phaseB, m_gproj))
                for j in range(8):
                    for h in range(NH):
                        idx = j * 4 + h
                        sc.activation(osq_v, oraw_v[:, j, h * 256:(h + 1) * 256],
                                      AF.Square, bias=0.0, scale=1.0,
                                      accum_out=sumsq_v[:, idx:idx + 1])
                sc.activation(rstd_v, sumsq_v, AF.Ln, bias=EPS, scale=1.0 / DV)
                i = sc.activation(rstd_v, rstd_v, AF.Exp, bias=0.0, scale=-0.5)
                inc(i, 's', 1)
            m_rstd = CNT['s']

            sil_scr = [A.at(122880, F32, [128, 1024]), A.at(130816, F32, [128, 1024])]
            m_vj = [0] * 8
            for j in range(8):
                @emit('scalar')
                def _(sc, j=j, m_rstd=m_rstd, m_prevv=(m_vj[j - 2] if j >= 2 else 0)):
                    if m_prevv:
                        sc.wait_ge(SEM['v'], m_prevv)
                    i = sc.activation(sil_scr[j % 2], g_tm_v[:, j, :], AF.Silu,
                                      bias=0.0, scale=1.0)
                    inc(i, 's', 1)
                m_silj = CNT['s']

                for h in range(NH):
                    idx = j * 4 + h
                    osl = oraw_v[:, j, h * 256:(h + 1) * 256]

                    @emit('vector')
                    def _(v, osl=osl, j=j, h=h, idx=idx, m_silj=m_silj,
                          m_rstd=m_rstd):
                        if h == 0:
                            v.wait_ge(SEM['s'], m_silj)
                        if idx == 0:
                            v.wait_ge(SEM['s'], m_rstd)
                        v.tensor_scalar(osq_v, osl, rstd_v[:, idx:idx + 1],
                                        None, ALU.mult)
                        v.tensor_tensor(osq_v, osq_v, wnorm_sb, ALU.mult)
                        i = v.tensor_tensor(osl, osq_v,
                                            sil_scr[j % 2][:, h * 256:(h + 1) * 256],
                                            ALU.mult)
                        inc(i, 'v', 1)
                m_vj[j] = CNT['v']
            m_gated = CNT['v']

            if ATTN_PH < 75:
                return
            m_ogtr = [0, 0]
            m_og_ser = [0]
            for tg in range(8):
                for vg in range(2):
                    @emit('tensor')
                    def _(t, tg=tg, vg=vg, m_gj=m_vj[tg], m_ptf=m_og_ser[0]):
                        t.wait_ge(SEM['v'], m_gj)
                        if m_ptf:
                            t.wait_ge(SEM['v'], m_ptf)
                        for j in range(4):
                            vc = vg * 4 + j
                            i = t.transpose(
                                pT[:, vg * 512 + j * 128:vg * 512 + (j + 1) * 128],
                                oraw_v[:, tg, vc * 128:(vc + 1) * 128], ident_sb)
                        inc(i, 't', 1)
                    m_tr = CNT['t']

                    @emit('vector')
                    def _(v, tg=tg, vg=vg, m_tr=m_tr):
                        v.wait_ge(SEM['t'], m_tr)
                        i = v.tensor_copy(
                            og_v[:, vg * 4:vg * 4 + 4, tg * 128:(tg + 1) * 128],
                            pT.rearrange("p (j t) -> p j t", j=8)[:, vg * 4:vg * 4 + 4, :])
                        inc(i, 'v', 1)
                    m_ogtr[vg] = CNT['v']
                    m_og_ser[0] = CNT['v']
                if tg == 3:
                    m_og_h0 = CNT['v']
            m_og = CNT['v']
            m_og_half = [m_og_h0, m_og]

            if ATTN_PH < 8:
                return
            @emit('sync')
            def _(sy, m_phaseB=m_phaseB, m_qsf=m_qs_free_t, m_qel=m_qe_last[0]):
                sy.wait_ge(SEM['v'], m_phaseB)
                sy.wait_ge(SEM['t'], m_qsf)
                sy.wait_ge(SEM['p'], m_qel)
                i = sy.dma_start(A.at(0, F32, [128, 16384]), xspill_d)
                inc(i, 'sq', 16)
            m_reload = CNT['sq']

            for oc in range(16):
                tile, twq, slot = ring_tile(dr["wo_l"][oc], width=1024)
                for hf in range(2):
                    sl = (oc * 2 + hf) % 4

                    @emit('tensor')
                    def _(t, oc=oc, hf=hf, sl=sl, tile=tile, twq=twq,
                          m_ogh=m_og_half[hf],
                          fm=ps_free[sl], fsem=ps_free_sem[sl]):
                        t.wait_ge(SEM[twq[0]], twq[1])
                        t.wait_ge(SEM['v'], m_ogh)
                        if fm:
                            t.wait_ge(SEM[fsem], fm)
                        for kv in range(8):
                            i = t.matmul(psl[sl], tile[:, kv * 128:(kv + 1) * 128],
                                         og_v[:, kv, hf * 512:hf * 512 + 512],
                                         start=(kv == 0), stop=(kv == 7))
                        inc(i, 't', 1)
                    ring_free[slot] = CNT['t']
                    m_mm = CNT['t']

                    @emit('vector')
                    def _(v, oc=oc, hf=hf, sl=sl, m_mm=m_mm, m_reload=m_reload):
                        v.wait_ge(SEM['t'], m_mm)
                        v.wait_ge(SEM['sq'], m_reload)
                        xs = x_v[:, oc, hf * 512:hf * 512 + 512]
                        i = v.tensor_tensor(xs, xs, psl[sl], ALU.add)
                        inc(i, 'v', 1)
                    ps_free[sl] = CNT['v']
                    ps_free_sem[sl] = 'v'

        hm1 = emit_ln(first=True)
        emit_ffn(hm1)

        if STAGE >= 2:
            emit_ln()
            emit_attention()
        if STAGE >= 3:
            hm3 = emit_ln()
            m_dn_final = emit_ffn(hm3)

        if DEBUG:
            bar = dict(CNT)

            @emit('gpsimd')
            def _(gg):
                wait_marks(gg, {k: bar[k] for k in ('t', 'v', 's')})
                inc(gg.dma_start(dbg_h, h_sb), 'wq', 16)
                inc(gg.dma_start(dbg_soft, softT_sb), 'wq', 16)
                inc(gg.dma_start(dbg_scal, scalT_sb), 'wq', 16)
                inc(gg.dma_start(dbg_scb, scb_v), 'wq', 16)
                inc(gg.dma_start(dbg_a, a_v.rearrange("p a b -> p (a b)")), 'wq', 16)
                i = gg.dma_start(dbg_xg, xg_v.rearrange("p a b -> p (a b)"))
                inc(i, 'wq', 16)
                gg.wait_ge(SEM['wq'], CNT['wq'])

        bar = dict(CNT)

        @emit('sync')
        def _(sy):
            ov = out_d.rearrange("(c p) t -> p c t", p=128)
            for hc in range(NCH):
                sy.wait_ge(SEM['v'], m_dn_final[hc])
                i = sy.dma_start(ov[:, hc, :], x_v[:, hc, :])
                inc(i, 'sq', 16)
            wait_marks(sy, {k: bar[k] for k in ('t', 'v', 's', 'p')})
            sy.wait_ge(SEM['sq'], CNT['sq'])

    return nc


_CACHE = {}


def kernel(**inputs):
    d = _prep_host(inputs)
    f = np.float32
    x = np.asarray(inputs["L0_attn_x"], f).reshape(8192, 2048)
    shards = [np.ascontiguousarray(x[c * T:(c + 1) * T].T) for c in range(8)]

    nc = build()

    shared = {k: v for k, v in d.items()}
    in_maps = []
    for c in range(8):
        g0 = 0 if c < 4 else 4
        mm = np.zeros((128, 8), f)
        for p in range(8):
            if g0 <= p < c:
                mm[:, p] = 1.0
        m = dict(shared)
        m["xT"] = shards[c]
        m["fconst"] = np.tile(np.array([[EPS, 0.0, 0.0, 0.0]], f), (128, 1))
        m["mmask"] = mm
        m["ummask"] = 1.0 - mm
        in_maps.append(m)

    res = run_bass_kernel_spmd(nc, in_maps, core_ids=list(range(8)))
    outs = [np.asarray(res.results[c]["out"], f).T for c in range(8)]
    return np.concatenate(outs, axis=0).reshape(2, 4096, 2048)


